# version 18
# baseline (speedup 1.0000x reference)
import contextlib
import math

import numpy as np
import ml_dtypes

import concourse.bass as bass
import concourse.mybir as mybir
from concourse.bass_utils import run_bass_kernel_spmd

F32 = mybir.dt.float32
BF16 = mybir.dt.bfloat16
ALU = mybir.AluOpType
AF = mybir.ActivationFunctionType
AX = mybir.AxisListType
_bf = ml_dtypes.bfloat16

B, T, D = 8, 4096, 2048
NT = T // 128
NH, NKV, HD, PD = 16, 4, 128, 16
GQ = NH // NKV
NE, NR = 6, 16
NCAT = 16 + NE * NR
SCALE = 1.0 / math.sqrt(HD)
LN_EPS = 1e-5
RMS_EPS = float(np.finfo(np.float32).eps)
NEGBIG = -1.0e30

ENGINES = ("sync", "gpsimd", "tensor", "vector", "scalar")
SEM_NAMES = ("spe", "sve", "sse",
             "sxa", "sxb",
             "soa", "sob",
             "scst", "sw2", "sw3", "sw45")


class _Ins:
    __slots__ = ("ctx", "real")

    def __init__(self, ctx, real):
        self.ctx, self.real = ctx, real

    def inc(self, sem_name, k=1):
        self.ctx.c[sem_name] += k
        if self.real is not None:
            self.real.then_inc(self.ctx.sems[sem_name], k)
        return self


class _Eng:
    def __init__(self, ctx, name):
        object.__setattr__(self, "ctx", ctx)
        object.__setattr__(self, "name", name)
        object.__setattr__(self, "real", None)

    def wait(self, sem_name, value):
        if self.real is not None and value > 0:
            self.real.wait_ge(self.ctx.sems[sem_name], value)

    def __getattr__(self, op):
        ctx = self.ctx
        real = self.real

        def f(*a, **kw):
            ri = getattr(real, op)(*a, **kw) if real is not None else None
            return _Ins(ctx, ri)
        return f


class Ctx:
    def __init__(self, sems, active):
        self.sems = sems
        self.active = active
        self.c = {k: 0 for k in SEM_NAMES}
        self.m = {}
        self.T = _Eng(self, "tensor")
        self.V = _Eng(self, "vector")
        self.S = _Eng(self, "scalar")
        self.Y = _Eng(self, "sync")
        self.G = _Eng(self, "gpsimd")
        self.by_name = {"tensor": self.T, "vector": self.V, "scalar": self.S,
                        "sync": self.Y, "gpsimd": self.G}

    def bind(self, eng):
        object.__setattr__(self.by_name[self.active], "real", eng)

    def unbind(self):
        object.__setattr__(self.by_name[self.active], "real", None)


def _emit_all(blk, ctxs, fn):
    for name in ENGINES:
        c = ctxs[name]

        def body(eng, c=c):
            c.bind(eng)
            fn(c)
            c.unbind()
        getattr(blk, name)(body)


def _barrier(c):
    snap = dict(c.c)
    for e in (c.T, c.V, c.S, c.Y, c.G):
        for sn in SEM_NAMES:
            if snap[sn] > 0:
                e.wait(sn, snap[sn])


def build_nc(nt=NT, dbg=False):
    nc = bass.Bass("TRN2", target_bir_lowering=False)

    def din(name, shape, dt=F32):
        return nc.dram_tensor(name, shape, dt, kind="ExternalInput").ap()

    x_d = din("x", [T, D])
    wq_i = din("wq_img", [128, 16 * D])
    wk_i = din("wk_img", [128, 16 * 512])
    wv_i = din("wv_img", [128, 16 * 512])
    wp_i = din("wp_img", [128, 16 * D])
    wb_i = din("wb_img", [128, 16 * D])
    wr_i = din("wr_img", [128, 16 * NE])
    ac_i = din("acat_img", [128, 16 * NCAT])
    bc_i = din("bcat_img", [NCAT, D])
    bb_i = din("bb_rep", [128, D])
    br_i = din("br_rep", [128, NE])
    qg_i = din("qg_rep", [128, NH])
    id_i = din("ident_bf", [128, 128], BF16)
    trikm_i = din("trikm", [128, 512], BF16)
    ones_i = din("ones_cr", [128, 129], BF16)
    eps_i = din("eps_c", [128, 2])
    rope_i = din("rope_tab", [T, 256])

    out_d = nc.dram_tensor("out", [T, D], F32, kind="ExternalOutput").ap()
    xnT_s = nc.dram_tensor("xnT_s", [16, 128, T], BF16, kind="Internal").ap()
    yT_s = nc.dram_tensor("yT_s", [16, 128, T], BF16, kind="Internal").ap()
    qT_s = nc.dram_tensor("qT_s", [16, 128, T], BF16, kind="Internal").ap()
    kT_s = nc.dram_tensor("kT_s", [128, NKV * T], BF16, kind="Internal").ap()
    v_s = nc.dram_tensor("v_s", [128, NT * 512], BF16, kind="Internal").ap()

    with contextlib.ExitStack() as es:
        sems = {n: es.enter_context(nc.semaphore(n)) for n in SEM_NAMES}
        blk = es.enter_context(nc.Block())
        ctxs = {n: Ctx(sems, n) for n in ENGINES}

        ident = es.enter_context(nc.sbuf_tensor("ident", [128, 128], BF16))
        trikm_sb = es.enter_context(nc.sbuf_tensor("trikm_sb", [128, 512], BF16))
        ones_cr = es.enter_context(nc.sbuf_tensor("ones_sb", [128, 129], BF16))
        epsc = es.enter_context(nc.sbuf_tensor("epsc", [128, 2], F32))
        qg_sb = es.enter_context(nc.sbuf_tensor("qg_sb", [128, NH], F32))
        br_sb = es.enter_context(nc.sbuf_tensor("br_sb", [128, NE], F32))
        w6 = es.enter_context(nc.sbuf_tensor("w6", [128, NT * NE], F32))
        st1 = es.enter_context(nc.sbuf_tensor("st1", [128, 16], F32))
        st2 = es.enter_context(nc.sbuf_tensor("st2", [128, 16], F32))
        st3 = es.enter_context(nc.sbuf_tensor("st3", [128, 16], F32))
        nmean = es.enter_context(nc.sbuf_tensor("nmean", [128, 1], F32))
        nmist = es.enter_context(nc.sbuf_tensor("nmist", [128, 1], F32))

        es2 = contextlib.ExitStack()
        wq_bf = es2.enter_context(nc.sbuf_tensor("wq_bf", [128, 16 * D], BF16))
        wk_bf = es2.enter_context(nc.sbuf_tensor("wk_bf", [128, 16 * 512], BF16))
        wv_bf = es2.enter_context(nc.sbuf_tensor("wv_bf", [128, 16 * 512], BF16))
        wr_bf = es2.enter_context(nc.sbuf_tensor("wr_bf", [128, 16 * NE], BF16))
        xb2 = [es2.enter_context(nc.sbuf_tensor(f"xb2_{j}", [128, D], F32)) for j in range(2)]
        xnbf = [es2.enter_context(nc.sbuf_tensor(f"xnbf{j}", [128, D], BF16)) for j in range(2)]
        xnTb2 = [es2.enter_context(nc.sbuf_tensor(f"xnTb2_{j}", [128, D], BF16)) for j in range(2)]
        ropeb = [es2.enter_context(nc.sbuf_tensor(f"ropeb{j}", [128, 256], F32)) for j in range(2)]
        rtmp = es2.enter_context(nc.sbuf_tensor("rtmp", [128, 512], F32))
        ktm = es2.enter_context(nc.sbuf_tensor("ktm", [128, 512], F32))
        kbf = es2.enter_context(nc.sbuf_tensor("kbf", [128, 512], BF16))
        kTb = [es2.enter_context(nc.sbuf_tensor(f"kTb{j}", [128, 512], BF16)) for j in range(2)]
        vb = [es2.enter_context(nc.sbuf_tensor(f"vb{j}", [128, 512], BF16)) for j in range(2)]
        qbf = es2.enter_context(nc.sbuf_tensor("qbf", [128, D], BF16))
        qsq = es2.enter_context(nc.sbuf_tensor("qsq", [128, D], BF16))
        qTb = [es2.enter_context(nc.sbuf_tensor(f"qTb{j}", [128, D], BF16)) for j in range(2)]
        sc16 = es2.enter_context(nc.sbuf_tensor("sc16", [128, NH], F32))
        rf6 = es2.enter_context(nc.sbuf_tensor("rf6", [128, NE], F32))
        g6a = es2.enter_context(nc.sbuf_tensor("g6a", [128, NE], F32))
        g6b = es2.enter_context(nc.sbuf_tensor("g6b", [128, NE], F32))
        z6 = es2.enter_context(nc.sbuf_tensor("z6", [128, 1], F32))
        m6 = es2.enter_context(nc.sbuf_tensor("m6", [128, 1], F32))
        ps_k = [es2.enter_context(nc.psum_tensor(f"ps_k{j}", [128, 512], F32)) for j in range(2)]
        ps_v = [es2.enter_context(nc.psum_tensor(f"ps_v{j}", [128, 512], F32)) for j in range(2)]
        ps_r = [es2.enter_context(nc.psum_tensor(f"ps_r{j}", [128, 512], F32)) for j in range(2)]
        ps_xt = [es2.enter_context(nc.psum_tensor(f"ps_xt{j}", [128, 512], F32)) for j in range(2)]

        def setup(c):
            c.G.dma_start(out=ident[:], in_=id_i).inc("scst", 16)
            c.G.dma_start(out=trikm_sb[:], in_=trikm_i).inc("scst", 16)
            c.G.dma_start(out=ones_cr[:], in_=ones_i).inc("scst", 16)
            c.G.dma_start(out=qg_sb[:], in_=qg_i).inc("scst", 16)
            c.G.dma_start(out=br_sb[:], in_=br_i).inc("scst", 16)
            c.G.dma_start(out=epsc[:], in_=eps_i).inc("scst", 16)
            c.G.dma_start(out=wk_bf[:], in_=wk_i).inc("sw2", 16)
            c.G.dma_start(out=wv_bf[:], in_=wv_i).inc("sw2", 16)
            c.G.dma_start(out=wr_bf[:], in_=wr_i).inc("sw2", 16)
            for fc in range(4):
                c.G.dma_start(
                    out=wq_bf[:].rearrange("p (ch d) -> p ch d", ch=16)[:, :, fc * 512:(fc + 1) * 512],
                    in_=wq_i.rearrange("p (ch d) -> p ch d", ch=16)[:, :, fc * 512:(fc + 1) * 512],
                ).inc("sw3", 16)

        _emit_all(blk, ctxs, setup)
        CONSTS = 6 * 16
        W2 = 3 * 16

        def bw(c, bank):
            prev = c.m.get(("b", bank))
            if prev is not None:
                c.T.wait(prev[0], prev[1])

        def bank_of(bid):
            return {"k0": ps_k[0], "k1": ps_k[1], "v0": ps_v[0], "v1": ps_v[1],
                    "r0": ps_r[0], "r1": ps_r[1], "x0": ps_xt[0], "x1": ps_xt[1]}[bid]

        def p2_A(c, i):
            par = "sxa" if i % 2 == 0 else "sxb"
            if i >= 2:
                c.Y.wait("sse", c.m[("p2_s_done", i - 2)])
                c.Y.wait("sve", c.m[("p2_qprep", i - 2)])
            c.Y.dma_start(out=xb2[i % 2][:], in_=x_d[i * 128:(i + 1) * 128, :]).inc(par, 16)
            c.Y.dma_start(out=ropeb[i % 2][:], in_=rope_i[i * 128:(i + 1) * 128, :]).inc(par, 16)
            ld = c.c[par]

            c.V.wait(par, ld)
            c.V.reduce_sum(out=nmean[:], in_=xb2[i % 2][:], axis=AX.X, negate=True).inc("sve")
            c.V.wait("sve", c.c["sve"])
            c.V.tensor_scalar(out=nmean[:], in0=nmean[:], scalar1=1.0 / D, scalar2=None,
                              op0=ALU.mult).inc("sve")
            c.S.wait("sve", c.c["sve"])
            c.S.wait(par, ld)
            c.S.activation(out=xnbf[i % 2][:], in_=xb2[i % 2][:], func=AF.Square,
                           bias=nmean[:], scale=1.0, accum_out=st1[:, 15:16])
            c.S.activation(out=st3[:, 14:15], in_=st1[:, 15:16], func=AF.Ln,
                           scale=1.0 / D, bias=epsc[:, 0:1])
            c.S.activation(out=st3[:, 15:16], in_=st3[:, 14:15], func=AF.Exp,
                           scale=-0.5).inc("sse")
            c.V.wait("sse", c.c["sse"])
            c.V.tensor_tensor(out=nmist[:], in0=nmean[:], in1=st3[:, 15:16], op=ALU.mult).inc("sve")
            c.S.wait("sve", c.c["sve"])
            c.S.activation(out=xnbf[i % 2][:], in_=xb2[i % 2][:], func=AF.Identity,
                           scale=st3[:, 15:16], bias=nmist[:]).inc("sse")
            c.m[("p2_s_done", i)] = c.c["sse"]
            c.m[("p2_ln", i)] = c.c["sse"]

        def p2_XTtr(c, i):
            par = "sxa" if i % 2 == 0 else "sxb"
            spar = "soa" if i % 2 == 0 else "sob"
            for g in range(4):
                c.T.wait("sse", c.m[("p2_ln", i)])
                if i == 0 and g == 0:
                    c.T.wait("scst", CONSTS)
                bid = "x%d" % (g % 2)
                bw(c, bid)
                ins = None
                for q in range(4):
                    ch = g * 4 + q
                    ins = c.T.matmul(ps_xt[g % 2][:, q * 128:(q + 1) * 128],
                                     xnbf[i % 2][:, ch * 128:(ch + 1) * 128], ident[:],
                                     start=True, stop=True, skip_group_check=True)
                ins.inc("spe")
                c.S.wait("spe", c.c["spe"])
                if g == 0 and i >= 2:
                    c.S.wait(spar, c.m[("p2_st", i - 2)])
                    c.S.wait("spe", c.m[("p2_qc_done", i - 2)])
                c.S.activation(out=xnTb2[i % 2][:, g * 512:(g + 1) * 512],
                               in_=ps_xt[g % 2][:], func=AF.Identity).inc("sse")
                c.m[("b", bid)] = ("sse", c.c["sse"])
            c.m[("p2_xnT", i)] = c.c["sse"]

            c.G.wait("sse", c.c["sse"])
            c.G.dma_start(out=xnT_s.rearrange("c p t -> p c t")[:, :, i * 128:(i + 1) * 128],
                          in_=xnTb2[i % 2][:].rearrange("p (c t) -> p c t", c=16)).inc(spar, 16)
            c.m[("p2_st", i)] = c.c[spar]

        def p2_KVR(c, i):
            c.T.wait("sw2", W2)
            c.T.wait("sse", c.m[("p2_xnT", i)])
            for bid in ("k%d" % (i % 2), "v%d" % (i % 2), "r%d" % (i % 2)):
                bw(c, bid)
            ins = None
            for ch in range(16):
                xc = xnTb2[i % 2][:, ch * 128:(ch + 1) * 128]
                c.T.matmul(ps_k[i % 2][:], xc, wk_bf[:, ch * 512:(ch + 1) * 512],
                           start=(ch == 0), stop=(ch == 15), skip_group_check=True)
                c.T.matmul(ps_v[i % 2][:], xc, wv_bf[:, ch * 512:(ch + 1) * 512],
                           start=(ch == 0), stop=(ch == 15), skip_group_check=True)
                ins = c.T.matmul(ps_r[i % 2][:, 0:NE], xc, wr_bf[:, ch * NE:(ch + 1) * NE],
                                 start=(ch == 0), stop=(ch == 15), skip_group_check=True)
            ins.inc("spe")
            c.m[("p2_kvr", i)] = c.c["spe"]

        def p2_KP(c, i):
            par = "sxa" if i % 2 == 0 else "sxb"
            spar = "soa" if i % 2 == 0 else "sob"
            kvr = c.m[("p2_kvr", i)]
            c.V.wait("spe", kvr)
            c.V.tensor_copy(out=rf6[:], in_=ps_r[i % 2][:, 0:NE]).inc("sve")
            c.m[("b", "r%d" % (i % 2))] = ("sve", c.c["sve"])
            c.S.wait("spe", kvr)
            if i >= 1:
                c.S.wait("sve", c.m[("p2_kbf", i - 1)])
            c.S.activation(out=ktm[:], in_=ps_k[i % 2][:], func=AF.Identity).inc("sse")
            c.m[("b", "k%d" % (i % 2))] = ("sse", c.c["sse"])
            if i >= 2:
                c.S.wait(spar, c.m[("p2_st_kv", i - 2)])
            c.S.activation(out=vb[i % 2][:], in_=ps_v[i % 2][:], func=AF.Identity).inc("sse")
            c.m[("b", "v%d" % (i % 2))] = ("sse", c.c["sse"])
            c.G.wait("sse", c.c["sse"])
            c.G.dma_start(out=v_s[:, i * 512:(i + 1) * 512], in_=vb[i % 2][:]).inc(spar, 16)

            c.S.activation(out=kbf[:], in_=ktm[:], func=AF.Square).inc("sse")
            c.V.wait("sse", c.c["sse"])
            c.V.tensor_reduce(out=st1[:, 0:NKV],
                              in_=kbf[:].rearrange("p (h d) -> p h d", h=NKV),
                              axis=AX.X, op=ALU.add).inc("sve")
            c.S.wait("sve", c.c["sve"])
            c.S.activation(out=st3[:, 0:NKV], in_=st1[:, 0:NKV], func=AF.Ln,
                           scale=1.0 / HD, bias=epsc[:, 1:2])
            c.S.activation(out=st3[:, 0:NKV], in_=st3[:, 0:NKV], func=AF.Exp,
                           scale=-0.5).inc("sse")
            c.V.wait("sse", c.c["sse"])
            for h in range(NKV):
                c.V.tensor_scalar(out=ktm[:, h * 128:(h + 1) * 128],
                                  in0=ktm[:, h * 128:(h + 1) * 128],
                                  scalar1=st3[:, h:h + 1], scalar2=None, op0=ALU.mult)
            kv_ = ktm[:].rearrange("p (h d) -> p h d", h=NKV)
            co = ropeb[i % 2][:, 0:32].rearrange("p (h d) -> p h d", h=NKV)
            si = ropeb[i % 2][:, 128:160].rearrange("p (h d) -> p h d", h=NKV)
            x1, x2 = kv_[:, :, 0:8], kv_[:, :, 8:16]
            r_ = rtmp[:].rearrange("p (h d) -> p h d", h=64)
            r1, r2, r3, r4 = r_[:, 0:4], r_[:, 4:8], r_[:, 8:12], r_[:, 12:16]
            c.V.tensor_tensor(out=r1, in0=x1, in1=co, op=ALU.mult)
            c.V.tensor_tensor(out=r2, in0=x1, in1=si, op=ALU.mult)
            c.V.tensor_tensor(out=r3, in0=x2, in1=si, op=ALU.mult)
            c.V.tensor_tensor(out=x1, in0=r1, in1=r3, op=ALU.add)
            c.V.tensor_tensor(out=r4, in0=x2, in1=co, op=ALU.mult)
            c.V.tensor_tensor(out=x2, in0=r2, in1=r4, op=ALU.subtract).inc("sve")
            c.V.wait("sve", c.c["sve"])
            c.V.tensor_copy(out=kbf[:], in_=ktm[:]).inc("sve")
            c.m[("p2_kbf", i)] = c.c["sve"]

            def vs(ins):
                ins.inc("sve")
                c.V.wait("sve", c.c["sve"])
            vs(c.V.tensor_tensor(out=rf6[:], in0=rf6[:], in1=br_sb[:], op=ALU.add))
            c.V.reduce_max(out=m6[:], in_=rf6[:], axis=AX.X, negate=True).inc("sve")
            c.S.wait("sve", c.c["sve"])
            c.S.activation(out=g6a[:], in_=rf6[:], func=AF.Exp, bias=m6[:],
                           accum_out=z6[:]).inc("sse")
            c.V.wait("sse", c.c["sse"])
            vs(c.V.reciprocal(out=m6[:], in_=z6[:]))
            vs(c.V.tensor_scalar(out=g6a[:], in0=g6a[:], scalar1=m6[:], scalar2=None,
                                 op0=ALU.mult))
            vs(c.V.reduce_max(out=m6[:], in_=g6a[:], axis=AX.X))
            vs(c.V.tensor_scalar(out=g6b[:], in0=g6a[:], scalar1=m6[:], scalar2=None,
                                 op0=ALU.is_ge))
            vs(c.V.scalar_tensor_tensor(out=g6b[:], in0=g6b[:], scalar=-2.0, in1=g6a[:],
                                        op0=ALU.mult, op1=ALU.add))
            vs(c.V.reduce_max(out=m6[:], in_=g6b[:], axis=AX.X))
            vs(c.V.tensor_scalar(out=g6b[:], in0=g6a[:], scalar1=m6[:], scalar2=None,
                                 op0=ALU.is_ge))
            c.V.tensor_tensor(out=w6[:, i * NE:(i + 1) * NE], in0=g6a[:], in1=g6b[:],
                              op=ALU.mult).inc("sve")
            c.m[("p2_v_done", i)] = c.c["sve"]

        def p2_QC(c, i):
            par = i % 2
            qb = ["k%d" % (1 - par), "v%d" % (1 - par), "r%d" % (1 - par), "r%d" % par]
            if i == 0:
                c.T.wait("sw3", 64)
            for pair in range(2):
                f0, f1 = 2 * pair, 2 * pair + 1
                bw(c, qb[f0])
                bw(c, qb[f1])
                ins = None
                for ch in range(16):
                    xc = xnTb2[i % 2][:, ch * 128:(ch + 1) * 128]
                    c.T.matmul(bank_of(qb[f0])[:], xc,
                               wq_bf[:, ch * D + f0 * 512: ch * D + (f0 + 1) * 512],
                               start=(ch == 0), stop=(ch == 15), skip_group_check=True)
                    ins = c.T.matmul(bank_of(qb[f1])[:], xc,
                                     wq_bf[:, ch * D + f1 * 512: ch * D + (f1 + 1) * 512],
                                     start=(ch == 0), stop=(ch == 15), skip_group_check=True)
                ins.inc("spe")
                c.S.wait("spe", c.c["spe"])
                if pair == 0 and i >= 1:
                    c.S.wait("spe", c.m[("p2_qtr", i - 1)])
                c.S.activation(out=qbf[:, f0 * 512:(f0 + 1) * 512],
                               in_=bank_of(qb[f0])[:], func=AF.Identity).inc("sse")
                c.m[("b", qb[f0])] = ("sse", c.c["sse"])
                c.S.activation(out=qbf[:, f1 * 512:(f1 + 1) * 512],
                               in_=bank_of(qb[f1])[:], func=AF.Identity).inc("sse")
                c.m[("b", qb[f1])] = ("sse", c.c["sse"])
                c.S.wait("sve", c.c["sve"])
                c.S.activation(out=qsq[:, pair * 1024:(pair + 1) * 1024],
                               in_=qbf[:, pair * 1024:(pair + 1) * 1024],
                               func=AF.Square).inc("sse")
                c.V.wait("sse", c.c["sse"])
                c.V.tensor_reduce(
                    out=st2[:, pair * 8:(pair + 1) * 8],
                    in_=qsq[:, pair * 1024:(pair + 1) * 1024].rearrange(
                        "p (h d) -> p h d", h=8),
                    axis=AX.X, op=ALU.add).inc("sve")
            c.m[("p2_qc_done", i)] = c.c["spe"]
            c.m[("p2_qstat", i)] = c.c["sve"]

        def p2_KT(c, i):
            spar = "soa" if i % 2 == 0 else "sob"
            c.T.wait("sve", c.m[("p2_kbf", i)])
            bw(c, "x0")
            ins = None
            for h in range(NKV):
                ins = c.T.matmul(ps_xt[0][:, h * 128:(h + 1) * 128],
                                 kbf[:, h * 128:(h + 1) * 128], ident[:],
                                 start=True, stop=True, skip_group_check=True)
            ins.inc("spe")
            c.m[("p2_ktr", i)] = c.c["spe"]
            c.S.wait("spe", c.c["spe"])
            if i >= 2:
                c.S.wait(spar, c.m[("p2_st_kv", i - 2)])
            c.S.activation(out=kTb[i % 2][:], in_=ps_xt[0][:], func=AF.Identity).inc("sse")
            c.m[("b", "x0")] = ("sse", c.c["sse"])
            c.G.wait("sse", c.c["sse"])
            c.G.dma_start(
                out=kT_s.rearrange("p (h t) -> p h t", h=NKV)[:, :, i * 128:(i + 1) * 128],
                in_=kTb[i % 2][:].rearrange("p (h t) -> p h t", h=NKV)).inc(spar, 16)
            c.m[("p2_st_kv", i)] = c.c[spar]

        def p2_QP(c, i):
            par = "sxa" if i % 2 == 0 else "sxb"
            c.S.wait("sve", c.m[("p2_qstat", i)])
            c.S.activation(out=st3[:], in_=st2[:], func=AF.Ln,
                           scale=1.0 / HD, bias=epsc[:, 1:2])
            c.S.activation(out=st3[:], in_=st3[:], func=AF.Exp, scale=-0.5).inc("sse")
            c.V.wait("sse", c.c["sse"])
            c.V.tensor_tensor(out=sc16[:], in0=st3[:], in1=qg_sb[:], op=ALU.mult).inc("sve")
            c.V.wait("sve", c.c["sve"])
            for h in range(NH):
                c.V.tensor_scalar(out=qbf[:, h * 128:(h + 1) * 128],
                                  in0=qbf[:, h * 128:(h + 1) * 128],
                                  scalar1=sc16[:, h:h + 1], scalar2=None, op0=ALU.mult)
            qv = qbf[:].rearrange("p (h d) -> p h d", h=NH)
            co = ropeb[i % 2][:, 0:128].rearrange("p (h d) -> p h d", h=NH)
            si = ropeb[i % 2][:, 128:256].rearrange("p (h d) -> p h d", h=NH)
            x1, x2 = qv[:, :, 0:8], qv[:, :, 8:16]
            r_ = rtmp[:].rearrange("p (h d) -> p h d", h=64)
            r1, r2, r3, r4 = r_[:, 0:16], r_[:, 16:32], r_[:, 32:48], r_[:, 48:64]
            c.V.tensor_tensor(out=r1, in0=x1, in1=co, op=ALU.mult)
            c.V.tensor_tensor(out=r2, in0=x1, in1=si, op=ALU.mult)
            c.V.tensor_tensor(out=r3, in0=x2, in1=si, op=ALU.mult)
            c.V.tensor_tensor(out=x1, in0=r1, in1=r3, op=ALU.add)
            c.V.tensor_tensor(out=r4, in0=x2, in1=co, op=ALU.mult)
            c.V.tensor_tensor(out=x2, in0=r2, in1=r4, op=ALU.subtract).inc("sve")
            c.m[("p2_qprep", i)] = c.c["sve"]

        def p2_QT(c, i):
            spar = "soa" if i % 2 == 0 else "sob"
            c.T.wait("sve", c.m[("p2_qprep", i)])
            for g in range(4):
                bid = "x%d" % ((g + 1) % 2)
                bw(c, bid)
                ins = None
                for hh in range(4):
                    h = g * 4 + hh
                    ins = c.T.matmul(ps_xt[(g + 1) % 2][:, hh * 128:(hh + 1) * 128],
                                     qbf[:, h * 128:(h + 1) * 128], ident[:],
                                     start=True, stop=True, skip_group_check=True)
                ins.inc("spe")
                c.S.wait("spe", c.c["spe"])
                if g == 0 and i >= 2:
                    c.S.wait(spar, c.m[("p2_st_q", i - 2)])
                c.S.activation(out=qTb[i % 2][:, g * 512:(g + 1) * 512],
                               in_=ps_xt[(g + 1) % 2][:], func=AF.Identity).inc("sse")
                c.m[("b", bid)] = ("sse", c.c["sse"])
            c.m[("p2_qtr", i)] = c.c["spe"]
            c.G.wait("sse", c.c["sse"])
            c.G.dma_start(out=qT_s.rearrange("c p t -> p c t")[:, :, i * 128:(i + 1) * 128],
                          in_=qTb[i % 2][:].rearrange("p (c t) -> p c t", c=16)).inc(spar, 16)
            c.m[("p2_st_q", i)] = c.c[spar]

        def p2(c):
            p2_A(c, 0)
            p2_XTtr(c, 0)
            p2_KVR(c, 0)
            for i in range(nt):
                p2_KP(c, i)
                if i + 1 < nt:
                    p2_A(c, i + 1)
                p2_QC(c, i)
                p2_KT(c, i)
                if i + 1 < nt:
                    p2_XTtr(c, i + 1)
                p2_QP(c, i)
                p2_QT(c, i)
                if i + 1 < nt:
                    p2_KVR(c, i + 1)
            _barrier(c)

        _emit_all(blk, ctxs, p2)
        es2.close()

        es34 = contextlib.ExitStack()
        wb_bf = es34.enter_context(nc.sbuf_tensor("wb_bf", [128, 16 * D], BF16))
        ac_bf = es34.enter_context(nc.sbuf_tensor("ac_bf", [128, 16 * NCAT], BF16))
        bc_bf = es34.enter_context(nc.sbuf_tensor("bc_bf", [NCAT, D], BF16))
        bb_sb = es34.enter_context(nc.sbuf_tensor("bb_sb", [128, D], F32))

        es3 = contextlib.ExitStack()
        kcache = es3.enter_context(nc.sbuf_tensor("kcache", [128, NKV * T], BF16))
        vcache = es3.enter_context(nc.sbuf_tensor("vcache", [128, NT * 512], BF16))
        qTb3 = [es3.enter_context(nc.sbuf_tensor(f"qTb3_{j}", [128, D], BF16)) for j in range(2)]
        pT = es3.enter_context(nc.sbuf_tensor("pT", [128, 12 * 512], BF16))
        yTb = [es3.enter_context(nc.sbuf_tensor(f"yTb{j}", [128, D], BF16)) for j in range(2)]
        nb1 = es3.enter_context(nc.sbuf_tensor("nb1", [128, 1], F32))
        zrow = es3.enter_context(nc.sbuf_tensor("zrow", [1, 512], F32))
        zrec = es3.enter_context(nc.sbuf_tensor("zrec", [1, D], BF16))
        rzbc = es3.enter_context(nc.sbuf_tensor("rzbc", [128, 512], F32))
        ps_d = [es3.enter_context(nc.psum_tensor(f"ps_d{j}", [128, 1024], F32)) for j in range(2)]
        ps_yu = [es3.enter_context(nc.psum_tensor(f"ps_yu{j}", [128, 512], F32)) for j in range(2)]
        ps_z = [es3.enter_context(nc.psum_tensor(f"ps_z{j}", [128, 512], F32)) for j in range(2)]

        def _bkwait(c, eng, bid):
            prev = c.m.get(("p3bk", bid))
            if prev is not None:
                eng.wait(prev[0], prev[1])

        def p3_attn_kv(c, i, kv, pre_cb=None):
            nst = i + 1
            NSL = 12
            npair = (nst + 1) // 2

            def emit_pair(p):
                s0 = 2 * p
                s1 = min(2 * p + 1, nst - 1)
                w = s1 - s0 + 1
                bk = ps_d[p % 2]
                _bkwait(c, c.T, ("d", p % 2))
                if p == 0 and kv == 0:
                    par = "sxa" if i % 2 == 0 else "sxb"
                    c.T.wait(par, c.m[("p3_ld", i)])
                ins = None
                for s in range(s0, s1 + 1):
                    last = (s == nst - 1)
                    ins = c.T.matmul(bk[:, (s - s0) * 512:(s - s0 + 1) * 512],
                                     kcache[:, kv * T + s * 128: kv * T + (s + 1) * 128],
                                     qTb3[i % 2][:, kv * 512:(kv + 1) * 512],
                                     start=True, stop=not last, skip_group_check=True)
                    if last:
                        ins = c.T.matmul(bk[:, (s - s0) * 512:(s - s0 + 1) * 512],
                                         ident[:], trikm_sb[:],
                                         start=False, stop=True, skip_group_check=True)
                ins.inc("spe")
                c.S.wait("spe", c.c["spe"])
                prev = c.m.get(("p3_pt", s0 % NSL))
                if prev is not None:
                    c.S.wait("spe", prev)
                c.S.activation(out=pT[:, (s0 % NSL) * 512:((s0 % NSL) + w) * 512],
                               in_=bk[:, 0:w * 512],
                               func=AF.Exp, bias=nb1[:], scale=SCALE).inc("sse")
                c.m[("p3bk", ("d", p % 2))] = ("sse", c.c["sse"])
                c.m[("p3_expp", p)] = c.c["sse"]

            def emit_pv(s, need_wait):
                last = (s == nst - 1)
                if need_wait:
                    c.T.wait("sse", c.m[("p3_expp", s // 2)])
                if s == 0:
                    _bkwait(c, c.T, ("yu", kv % 2))
                c.T.matmul(ps_yu[kv % 2][:],
                           vcache[:, s * 512 + kv * 128: s * 512 + (kv + 1) * 128],
                           pT[:, (s % NSL) * 512:(s % NSL + 1) * 512],
                           start=(s == 0), stop=last, skip_group_check=True).inc("spe")

            def emit_zbatch(s0, s1):
                c.T.wait("sse", c.m[("p3_expp", (s1 - 1) // 2)])
                if s0 == 0:
                    _bkwait(c, c.T, ("z", kv % 2))
                ins = None
                for s in range(s0, s1):
                    ins = c.T.matmul(ps_z[kv % 2][0:1, :], ones_cr[:, 0:1],
                                     pT[:, (s % NSL) * 512:(s % NSL + 1) * 512],
                                     start=(s == 0), stop=(s == nst - 1),
                                     skip_group_check=True)
                ins.inc("spe")
                for s in range(s0, s1):
                    c.m[("p3_pt", s % NSL)] = c.c["spe"]

            emit_pair(0)
            if npair > 1:
                emit_pair(1)
            zb = 0
            for p in range(npair):
                if p + 2 < npair:
                    emit_pair(p + 2)
                if p == 0 and pre_cb is not None:
                    pre_cb[0]()
                if p == min(4, npair - 1) and pre_cb is not None:
                    pre_cb[1]()
                s0 = 2 * p
                s1 = min(2 * p + 1, nst - 1)
                emit_pv(s0, True)
                if s1 > s0:
                    emit_pv(s1, False)
                if s1 - zb >= 6 or s1 == nst - 1:
                    emit_zbatch(zb, s1 + 1)
                    zb = s1 + 1
            c.m[("p3_pvz", kv)] = c.c["spe"]

        def p3_epi_a(c, i, kv):
            c.V.wait("spe", c.m[("p3_pvz", kv)])
            c.V.tensor_copy(out=zrow[0:1, :], in_=ps_z[kv % 2][0:1, :]).inc("sve")
            c.m[("p3bk", ("z", kv % 2))] = ("sve", c.c["sve"])
            c.V.wait("sve", c.c["sve"])
            c.V.reciprocal(out=zrow[0:1, :], in_=zrow[0:1, :]).inc("sve")
            c.V.wait("sve", c.c["sve"])
            c.V.tensor_copy(out=zrec[0:1, kv * 512:(kv + 1) * 512], in_=zrow[0:1, :]).inc("sve")
            c.m[("p3_zr", kv)] = c.c["sve"]

        def p3_epi_b(c, i, kv):
            c.T.wait("sve", c.m[("p3_zr", kv)])
            c.T.matmul(ps_z[kv % 2][:], ones_cr[0:1, 1:129], zrec[0:1, kv * 512:(kv + 1) * 512],
                       start=True, stop=True, skip_group_check=True).inc("spe")
            c.V.wait("spe", c.c["spe"])
            c.V.tensor_copy(out=rzbc[:], in_=ps_z[kv % 2][:]).inc("sve")
            c.m[("p3bk", ("z", kv % 2))] = ("sve", c.c["sve"])
            c.V.wait("sve", c.c["sve"])
            if kv == 0 and i >= 2:
                spar_prev = "soa" if i % 2 == 0 else "sob"
                c.V.wait(spar_prev, c.m[("p3_st", i - 2)])
            c.V.scalar_tensor_tensor(out=yTb[i % 2][:, kv * 512:(kv + 1) * 512],
                                     in0=ps_yu[kv % 2][:],
                                     scalar=1.0, in1=rzbc[:],
                                     op0=ALU.mult, op1=ALU.mult).inc("sve")
            c.m[("p3bk", ("yu", kv % 2))] = ("sve", c.c["sve"])

        def p3(c):
            c.V.wait("scst", CONSTS)
            c.V.reduce_max(out=nb1[:], in_=qg_sb[:], axis=AX.X).inc("sve")
            c.V.wait("sve", c.c["sve"])
            c.V.tensor_scalar(out=nb1[:], in0=nb1[:], scalar1=-math.sqrt(float(HD)),
                              scalar2=40.0, op0=ALU.mult, op1=ALU.add).inc("sve")
            c.V.wait("sve", c.c["sve"])
            for i in range(nt):
                par = "sxa" if i % 2 == 0 else "sxb"
                spar = "soa" if i % 2 == 0 else "sob"
                if i >= 2:
                    c.Y.wait("spe", c.m[("p3_pe_done", i - 2)])
                c.Y.dma_start(out=qTb3[i % 2][:].rearrange("p (c t) -> p c t", c=16),
                              in_=qT_s.rearrange("c p t -> p c t")[:, :, i * 128:(i + 1) * 128]
                              ).inc(par, 16)
                c.Y.dma_start(
                    out=kcache[:].rearrange("p (h t) -> p h t", h=NKV)[:, :, i * 128:(i + 1) * 128],
                    in_=kT_s.rearrange("p (h t) -> p h t", h=NKV)[:, :, i * 128:(i + 1) * 128]
                ).inc(par, 16)
                c.Y.dma_start(out=vcache[:, i * 512:(i + 1) * 512],
                              in_=v_s[:, i * 512:(i + 1) * 512]).inc(par, 16)
                c.m[("p3_ld", i)] = c.c[par]

                if i >= 4 and (i - 4) % 3 == 0 and (i - 4) // 3 < 8:
                    j = (i - 4) // 3
                    fc, hf = j // 2, j % 2
                    c.G.dma_start(
                        out=wb_bf[:].rearrange("p (ch d) -> p ch d", ch=16)[
                            :, hf * 8:(hf + 1) * 8, fc * 512:(fc + 1) * 512],
                        in_=wb_i.rearrange("p (ch d) -> p ch d", ch=16)[
                            :, hf * 8:(hf + 1) * 8, fc * 512:(fc + 1) * 512],
                    ).inc("sw45", 16)
                elif i == 26:
                    c.G.dma_start(out=ac_bf[:], in_=ac_i).inc("sw45", 16)
                elif i == 27:
                    c.G.dma_start(out=bc_bf[:], in_=bc_i).inc("sw45", 16)
                elif i == 28:
                    c.G.dma_start(out=bb_sb[:], in_=bb_i).inc("sw45", 16)

                for kv in range(NKV):
                    if kv == 0:
                        p3_attn_kv(c, i, kv)
                    else:
                        p3_attn_kv(c, i, kv,
                                   pre_cb=(lambda kk=kv - 1: p3_epi_a(c, i, kk),
                                           lambda kk=kv - 1: p3_epi_b(c, i, kk)))
                p3_epi_a(c, i, NKV - 1)
                p3_epi_b(c, i, NKV - 1)
                c.m[("p3_pe_done", i)] = c.c["spe"]
                c.G.wait("sve", c.c["sve"])
                c.G.dma_start(out=yT_s.rearrange("c p t -> p c t")[:, :, i * 128:(i + 1) * 128],
                              in_=yTb[i % 2][:].rearrange("p (c t) -> p c t", c=16)).inc(spar, 16)
                c.m[("p3_st", i)] = c.c[spar]
            _barrier(c)

        _emit_all(blk, ctxs, p3)
        es3.close()

        es4 = contextlib.ExitStack()
        wp_bf = es4.enter_context(nc.sbuf_tensor("wp_bf", [128, 16 * D], BF16))
        xnTb4 = [es4.enter_context(nc.sbuf_tensor(f"xnTb4_{j}", [128, D], BF16)) for j in range(2)]
        yTb4 = [es4.enter_context(nc.sbuf_tensor(f"yTb4_{j}", [128, D], BF16)) for j in range(2)]
        xb4 = [es4.enter_context(nc.sbuf_tensor(f"xb4_{j}", [128, D], F32)) for j in range(2)]
        gbuf = es4.enter_context(nc.sbuf_tensor("gbuf", [128, D], F32))
        midf = es4.enter_context(nc.sbuf_tensor("midf", [128, NCAT], F32))
        hwbf = es4.enter_context(nc.sbuf_tensor("hwbf", [128, NCAT], BF16))
        hwT = es4.enter_context(nc.sbuf_tensor("hwT", [128, 128], BF16))
        ps_f = [es4.enter_context(nc.psum_tensor(f"ps_f{j}", [128, 512], F32)) for j in range(4)]
        ps_m = es4.enter_context(nc.psum_tensor("ps_m", [128, 512], F32))
        ps_h = es4.enter_context(nc.psum_tensor("ps_h", [128, 512], F32))
        ps_w = [es4.enter_context(nc.psum_tensor(f"ps_w{j}", [128, 512], F32)) for j in range(2)]

        W45 = 11 * 16

        def p45_setup(c):
            for fc in range(4):
                c.G.dma_start(
                    out=wp_bf[:].rearrange("p (ch d) -> p ch d", ch=16)[:, :, fc * 512:(fc + 1) * 512],
                    in_=wp_i.rearrange("p (ch d) -> p ch d", ch=16)[:, :, fc * 512:(fc + 1) * 512],
                ).inc("sw45", 16)

        def p45_tile(c, i):
            par = "sxa" if i % 2 == 0 else "sxb"
            spar = "soa" if i % 2 == 0 else "sob"
            if i >= 2:
                c.Y.wait("spe", c.m[("p45_pe_done", i - 2)])
                c.Y.wait("sve", c.m[("p45_v_done", i - 2)])
                c.Y.wait(spar, c.m[("p45_st", i - 2)])
            c.Y.dma_start(out=xnTb4[i % 2][:].rearrange("p (c t) -> p c t", c=16),
                          in_=xnT_s.rearrange("c p t -> p c t")[:, :, i * 128:(i + 1) * 128]
                          ).inc(par, 16)
            c.Y.dma_start(out=yTb4[i % 2][:].rearrange("p (c t) -> p c t", c=16),
                          in_=yT_s.rearrange("c p t -> p c t")[:, :, i * 128:(i + 1) * 128]
                          ).inc(par, 16)
            c.Y.dma_start(out=xb4[i % 2][:], in_=x_d[i * 128:(i + 1) * 128, :]).inc(par, 16)
            ld = c.c[par]

            c.T.wait(par, ld)
            if i == 0:
                c.T.wait("sw45", 9 * 16)
            if i >= 1:
                c.T.wait("sve", c.m[("p45_midev", i - 1)])
                for fc in range(4):
                    c.T.wait("sve", c.m[("p45_gev", i - 1, fc)])
            ins = None
            for ch in range(16):
                ins = c.T.matmul(ps_m[:, 0:NCAT], xnTb4[i % 2][:, ch * 128:(ch + 1) * 128],
                                 ac_bf[:, ch * NCAT:(ch + 1) * NCAT],
                                 start=(ch == 0), stop=(ch == 15), skip_group_check=True)
                for fc in range(4):
                    c.T.matmul(ps_f[fc][:], xnTb4[i % 2][:, ch * 128:(ch + 1) * 128],
                               wb_bf[:, ch * D + fc * 512: ch * D + (fc + 1) * 512],
                               start=(ch == 0), stop=False, skip_group_check=True)
            ins.inc("spe")
            mid_mark = c.c["spe"]

            c.V.wait("spe", mid_mark)
            c.V.tensor_copy(out=midf[:], in_=ps_m[:, 0:NCAT]).inc("sve")
            c.m[("p45_midev", i)] = c.c["sve"]
            c.V.wait("sve", c.c["sve"])
            for e in range(NE):
                c.V.tensor_scalar(out=midf[:, 16 + e * NR:16 + (e + 1) * NR],
                                  in0=midf[:, 16 + e * NR:16 + (e + 1) * NR],
                                  scalar1=w6[:, i * NE + e:i * NE + e + 1],
                                  scalar2=None, op0=ALU.mult).inc("sve")
            c.V.wait("sve", c.c["sve"])
            c.V.tensor_copy(out=hwbf[:], in_=midf[:]).inc("sve")
            hw_mark = c.c["sve"]
            c.T.wait("sve", hw_mark)
            if i >= 1:
                c.T.wait("sve", c.m[("p45_hwev", i - 1)])
            c.T.matmul(ps_h[0:NCAT, 0:128], hwbf[:, 0:NCAT], ident[:],
                       start=True, stop=True, skip_group_check=True).inc("spe")
            c.V.wait("spe", c.c["spe"])
            c.V.tensor_copy(out=hwT[0:NCAT, :], in_=ps_h[0:NCAT, 0:128]).inc("sve")
            c.m[("p45_hwev", i)] = c.c["sve"]

            c.T.wait("sve", c.c["sve"])
            if i == 0:
                c.T.wait("sw45", 10 * 16)
            ins = None
            for fc in range(4):
                ins = c.T.matmul(ps_f[fc][:], hwT[0:NCAT, :],
                                 bc_bf[0:NCAT, fc * 512:(fc + 1) * 512],
                                 start=False, stop=True, skip_group_check=True)
            ins.inc("spe")
            ffn_mark = c.c["spe"]

            c.V.wait("spe", ffn_mark)
            if i == 0:
                c.V.wait("sw45", W45)
            silu_marks = {}

            def bias_silu(fc):
                sl = slice(fc * 512, (fc + 1) * 512)
                c.V.tensor_tensor(out=gbuf[:, sl], in0=ps_f[fc][:],
                                  in1=bb_sb[:, sl], op=ALU.add).inc("sve")
                c.m[("p45_gev", i, fc)] = c.c["sve"]
                c.S.wait("sve", c.c["sve"])
                c.S.activation(out=gbuf[:, sl], in_=gbuf[:, sl],
                               func=AF.Silu).inc("sse")
                silu_marks[fc] = c.c["sse"]

            bias_silu(0)
            bias_silu(1)

            for pair in range(2):
                if i == 0:
                    c.T.wait("sw45", W45 + (pair + 1) * 32)
                for fc in (pair * 2, pair * 2 + 1):
                    if ("p45_sgev", i, fc - 2) in c.m:
                        c.T.wait("sve", c.m[("p45_sgev", i, fc - 2)])
                    elif ("p45_sg_prev", fc % 2) in c.m:
                        c.T.wait("sve", c.m[("p45_sg_prev", fc % 2)])
                ins = None
                for ch in range(16):
                    for fc in (pair * 2, pair * 2 + 1):
                        ins = c.T.matmul(ps_w[fc % 2][:], yTb4[i % 2][:, ch * 128:(ch + 1) * 128],
                                         wp_bf[:, ch * D + fc * 512: ch * D + (fc + 1) * 512],
                                         start=(ch == 0), stop=(ch == 15), skip_group_check=True)
                ins.inc("spe")
                wp_mark = c.c["spe"]
                if pair == 0:
                    bias_silu(2)
                    bias_silu(3)
                for fc in (pair * 2, pair * 2 + 1):
                    c.V.wait("spe", wp_mark)
                    c.V.wait("sse", silu_marks[fc])
                    sl = slice(fc * 512, (fc + 1) * 512)
                    c.V.scalar_tensor_tensor(out=gbuf[:, sl], in0=ps_w[fc % 2][:],
                                             scalar=1.0, in1=gbuf[:, sl],
                                             op0=ALU.mult, op1=ALU.mult)
                    c.V.tensor_tensor(out=xb4[i % 2][:, sl], in0=gbuf[:, sl],
                                      in1=xb4[i % 2][:, sl], op=ALU.add).inc("sve")
                    c.m[("p45_sgev", i, fc)] = c.c["sve"]
            c.m[("p45_sg_prev", 0)] = c.m[("p45_sgev", i, 2)]
            c.m[("p45_sg_prev", 1)] = c.m[("p45_sgev", i, 3)]
            c.m[("p45_v_done", i)] = c.c["sve"]
            c.m[("p45_pe_done", i)] = c.c["spe"]

            c.G.wait("sve", c.c["sve"])
            c.G.dma_start(out=out_d[i * 128:(i + 1) * 128, :], in_=xb4[i % 2][:]).inc(spar, 16)
            c.m[("p45_st", i)] = c.c[spar]

        def p45(c):
            p45_setup(c)
            for i in range(nt):
                p45_tile(c, i)
            c.G.wait("soa", c.c["soa"])
            c.G.wait("sob", c.c["sob"])

        _emit_all(blk, ctxs, p45)
        es4.close()
        es34.close()

    return nc


_NC_CACHE = {}


def _get_nc():
    if "nc" not in _NC_CACHE:
        _NC_CACHE["nc"] = build_nc()
    return _NC_CACHE["nc"]


def _img(mat):
    n = mat.shape[1]
    return np.ascontiguousarray(
        mat.reshape(16, 128, n).transpose(1, 0, 2).reshape(128, 16 * n).astype(np.float32))


def prepare_shared(L0_attn_wq, L0_attn_wk, L0_attn_wv, L0_attn_wp, L0_attn_qg,
                   L0_ffn_W_base, L0_ffn_b_base, L0_ffn_A_s, L0_ffn_B_s, L0_ffn_A_r,
                   L0_ffn_B_r, L0_ffn_W_router, L0_ffn_b_router, **_ignore):
    wq_img = _img(np.asarray(L0_attn_wq, np.float32).T)
    wk_img = _img(np.asarray(L0_attn_wk, np.float32).T)
    wv_img = _img(np.asarray(L0_attn_wv, np.float32).T)
    wp_img = _img(np.asarray(L0_attn_wp, np.float32).T)
    wb_img = _img(np.asarray(L0_ffn_W_base, np.float32))
    wr_img = _img(np.asarray(L0_ffn_W_router, np.float32))
    A_r = np.asarray(L0_ffn_A_r, np.float32)
    acat = np.concatenate([np.asarray(L0_ffn_A_s, np.float32),
                           A_r.transpose(1, 0, 2).reshape(D, NE * NR)], axis=1)
    acat_img = _img(acat)
    bcat_img = np.ascontiguousarray(np.concatenate(
        [np.asarray(L0_ffn_B_s, np.float32),
         np.asarray(L0_ffn_B_r, np.float32).reshape(NE * NR, D)], axis=0))
    bb_rep = np.ascontiguousarray(np.tile(np.asarray(L0_ffn_b_base, np.float32)[None, :], (128, 1)))
    br_rep = np.ascontiguousarray(np.tile(np.asarray(L0_ffn_b_router, np.float32)[None, :], (128, 1)))
    qg_rep = np.ascontiguousarray(np.tile(np.asarray(L0_attn_qg, np.float32)[None, :], (128, 1)))

    ident_bf = np.eye(128, dtype=_bf)
    trikm1 = np.where(np.arange(128)[:, None] <= np.arange(128)[None, :],
                      np.float32(0.0), np.float32(NEGBIG)).astype(np.float32)
    trikm = np.ascontiguousarray(np.tile(trikm1, (1, 4)).astype(_bf))
    ones_cr = np.ones((128, 129), dtype=_bf)
    eps_c = np.ascontiguousarray(np.tile(np.array([[LN_EPS, RMS_EPS]], np.float32), (128, 1)))
    t_ = np.arange(T, dtype=np.float32)
    inv = (1.0 / (10000.0 ** (np.arange(0, PD, 2, dtype=np.float32) / PD))).astype(np.float32)
    f_ = t_[:, None] * inv[None, :]
    rope_tab = np.ascontiguousarray(np.concatenate(
        [np.tile(np.cos(f_).astype(np.float32), (1, 16)),
         np.tile(np.sin(f_).astype(np.float32), (1, 16))], axis=1))

    return dict(wq_img=wq_img, wk_img=wk_img, wv_img=wv_img, wp_img=wp_img,
                wb_img=wb_img, wr_img=wr_img, acat_img=acat_img, bcat_img=bcat_img,
                bb_rep=bb_rep, br_rep=br_rep, qg_rep=qg_rep, ident_bf=ident_bf,
                trikm=trikm, ones_cr=ones_cr, eps_c=eps_c, rope_tab=rope_tab)


def kernel(**inputs):
    x = np.asarray(inputs["L0_attn_x"], dtype=np.float32)
    shared = prepare_shared(**inputs)
    in_maps = [dict(shared, x=np.ascontiguousarray(x[b])) for b in range(B)]
    nc = _get_nc()
    res = run_bass_kernel_spmd(nc, in_maps, core_ids=list(range(B)))
    return np.stack([np.asarray(res.results[b]["out"], dtype=np.float32) for b in range(B)],
                    axis=0)


# version 19
# speedup vs baseline: 1.0779x; 1.0779x over previous
import contextlib
import math

import numpy as np
import ml_dtypes

import concourse.bass as bass
import concourse.mybir as mybir
from concourse.bass_utils import run_bass_kernel_spmd

F32 = mybir.dt.float32
BF16 = mybir.dt.bfloat16
ALU = mybir.AluOpType
AF = mybir.ActivationFunctionType
AX = mybir.AxisListType
_bf = ml_dtypes.bfloat16

B, T, D = 8, 4096, 2048
NT = T // 128
NH, NKV, HD, PD = 16, 4, 128, 16
GQ = NH // NKV
NE, NR = 6, 16
NCAT = 16 + NE * NR
SCALE = 1.0 / math.sqrt(HD)
LN_EPS = 1e-5
RMS_EPS = float(np.finfo(np.float32).eps)
NEGBIG = -1.0e30

ENGINES = ("sync", "gpsimd", "tensor", "vector", "scalar")
SEM_NAMES = ("spe", "sve", "sse",
             "sxa", "sxb",
             "soa", "sob",
             "scst", "sw2", "sw3", "sw45")


class _Ins:
    __slots__ = ("ctx", "real")

    def __init__(self, ctx, real):
        self.ctx, self.real = ctx, real

    def inc(self, sem_name, k=1):
        self.ctx.c[sem_name] += k
        if self.real is not None:
            self.real.then_inc(self.ctx.sems[sem_name], k)
        return self


class _Eng:
    def __init__(self, ctx, name):
        object.__setattr__(self, "ctx", ctx)
        object.__setattr__(self, "name", name)
        object.__setattr__(self, "real", None)

    def wait(self, sem_name, value):
        if self.real is not None and value > 0:
            self.real.wait_ge(self.ctx.sems[sem_name], value)

    def __getattr__(self, op):
        ctx = self.ctx
        real = self.real

        def f(*a, **kw):
            ri = getattr(real, op)(*a, **kw) if real is not None else None
            return _Ins(ctx, ri)
        return f


class Ctx:
    def __init__(self, sems, active):
        self.sems = sems
        self.active = active
        self.c = {k: 0 for k in SEM_NAMES}
        self.m = {}
        self.T = _Eng(self, "tensor")
        self.V = _Eng(self, "vector")
        self.S = _Eng(self, "scalar")
        self.Y = _Eng(self, "sync")
        self.G = _Eng(self, "gpsimd")
        self.by_name = {"tensor": self.T, "vector": self.V, "scalar": self.S,
                        "sync": self.Y, "gpsimd": self.G}

    def bind(self, eng):
        object.__setattr__(self.by_name[self.active], "real", eng)

    def unbind(self):
        object.__setattr__(self.by_name[self.active], "real", None)


def _emit_all(blk, ctxs, fn):
    for name in ENGINES:
        c = ctxs[name]

        def body(eng, c=c):
            c.bind(eng)
            fn(c)
            c.unbind()
        getattr(blk, name)(body)


def _barrier(c):
    snap = dict(c.c)
    for e in (c.T, c.V, c.S, c.Y, c.G):
        for sn in SEM_NAMES:
            if snap[sn] > 0:
                e.wait(sn, snap[sn])


def build_nc(nt=NT, dbg=False):
    nc = bass.Bass("TRN2", target_bir_lowering=False)

    def din(name, shape, dt=F32):
        return nc.dram_tensor(name, shape, dt, kind="ExternalInput").ap()

    x_d = din("x", [T, D])
    wq_i = din("wq_img", [128, 16 * D])
    wk_i = din("wk_img", [128, 16 * 512])
    wv_i = din("wv_img", [128, 16 * 512])
    wp_i = din("wp_img", [128, 16 * D])
    wb_i = din("wb_img", [128, 16 * D])
    wr_i = din("wr_img", [128, 16 * NE])
    ac_i = din("acat_img", [128, 16 * NCAT])
    bc_i = din("bcat_img", [NCAT, D])
    bb_i = din("bb_rep", [128, D])
    br_i = din("br_rep", [128, NE])
    qg_i = din("qg_rep", [128, NH])
    id_i = din("ident_bf", [128, 128], BF16)
    trikm_i = din("trikm", [128, 512], BF16)
    ones_i = din("ones_cr", [128, 129], BF16)
    eps_i = din("eps_c", [128, 2])
    rope_i = din("rope_tab", [T, 256])

    out_d = nc.dram_tensor("out", [T, D], F32, kind="ExternalOutput").ap()
    xnT_s = nc.dram_tensor("xnT_s", [16, 128, T], BF16, kind="Internal").ap()
    yT_s = nc.dram_tensor("yT_s", [16, 128, T], BF16, kind="Internal").ap()
    qT_s = nc.dram_tensor("qT_s", [16, 128, T], BF16, kind="Internal").ap()
    kT_s = nc.dram_tensor("kT_s", [128, NKV * T], BF16, kind="Internal").ap()
    v_s = nc.dram_tensor("v_s", [128, NT * 512], BF16, kind="Internal").ap()

    with contextlib.ExitStack() as es:
        sems = {n: es.enter_context(nc.semaphore(n)) for n in SEM_NAMES}
        blk = es.enter_context(nc.Block())
        ctxs = {n: Ctx(sems, n) for n in ENGINES}

        ident = es.enter_context(nc.sbuf_tensor("ident", [128, 128], BF16))
        trikm_sb = es.enter_context(nc.sbuf_tensor("trikm_sb", [128, 512], BF16))
        ones_cr = es.enter_context(nc.sbuf_tensor("ones_sb", [128, 129], BF16))
        epsc = es.enter_context(nc.sbuf_tensor("epsc", [128, 2], F32))
        qg_sb = es.enter_context(nc.sbuf_tensor("qg_sb", [128, NH], F32))
        br_sb = es.enter_context(nc.sbuf_tensor("br_sb", [128, NE], F32))
        w6 = es.enter_context(nc.sbuf_tensor("w6", [128, NT * NE], F32))
        st1 = es.enter_context(nc.sbuf_tensor("st1", [128, 16], F32))
        st2 = es.enter_context(nc.sbuf_tensor("st2", [128, 16], F32))
        st3 = es.enter_context(nc.sbuf_tensor("st3", [128, 16], F32))
        nmean = es.enter_context(nc.sbuf_tensor("nmean", [128, 1], F32))
        nmist = es.enter_context(nc.sbuf_tensor("nmist", [128, 1], F32))

        es2 = contextlib.ExitStack()
        wq_bf = es2.enter_context(nc.sbuf_tensor("wq_bf", [128, 16 * D], BF16))
        wk_bf = es2.enter_context(nc.sbuf_tensor("wk_bf", [128, 16 * 512], BF16))
        wv_bf = es2.enter_context(nc.sbuf_tensor("wv_bf", [128, 16 * 512], BF16))
        wr_bf = es2.enter_context(nc.sbuf_tensor("wr_bf", [128, 16 * NE], BF16))
        xb2 = [es2.enter_context(nc.sbuf_tensor(f"xb2_{j}", [128, D], F32)) for j in range(2)]
        xnbf = [es2.enter_context(nc.sbuf_tensor(f"xnbf{j}", [128, D], BF16)) for j in range(2)]
        xnTb2 = [es2.enter_context(nc.sbuf_tensor(f"xnTb2_{j}", [128, D], BF16)) for j in range(2)]
        ropeb = [es2.enter_context(nc.sbuf_tensor(f"ropeb{j}", [128, 256], F32)) for j in range(2)]
        rtmp = es2.enter_context(nc.sbuf_tensor("rtmp", [128, 512], F32))
        ktm = es2.enter_context(nc.sbuf_tensor("ktm", [128, 512], F32))
        kbf = es2.enter_context(nc.sbuf_tensor("kbf", [128, 512], BF16))
        kTb = [es2.enter_context(nc.sbuf_tensor(f"kTb{j}", [128, 512], BF16)) for j in range(2)]
        vb = [es2.enter_context(nc.sbuf_tensor(f"vb{j}", [128, 512], BF16)) for j in range(2)]
        qbf = es2.enter_context(nc.sbuf_tensor("qbf", [128, D], BF16))
        qsq = es2.enter_context(nc.sbuf_tensor("qsq", [128, D], BF16))
        qTb = [es2.enter_context(nc.sbuf_tensor(f"qTb{j}", [128, D], BF16)) for j in range(2)]
        sc16 = es2.enter_context(nc.sbuf_tensor("sc16", [128, NH], F32))
        rf6 = es2.enter_context(nc.sbuf_tensor("rf6", [128, NE], F32))
        g6a = es2.enter_context(nc.sbuf_tensor("g6a", [128, NE], F32))
        g6b = es2.enter_context(nc.sbuf_tensor("g6b", [128, NE], F32))
        z6 = es2.enter_context(nc.sbuf_tensor("z6", [128, 1], F32))
        m6 = es2.enter_context(nc.sbuf_tensor("m6", [128, 1], F32))
        ps_k = [es2.enter_context(nc.psum_tensor(f"ps_k{j}", [128, 512], F32)) for j in range(2)]
        ps_v = [es2.enter_context(nc.psum_tensor(f"ps_v{j}", [128, 512], F32)) for j in range(2)]
        ps_r = [es2.enter_context(nc.psum_tensor(f"ps_r{j}", [128, 512], F32)) for j in range(2)]
        ps_xt = [es2.enter_context(nc.psum_tensor(f"ps_xt{j}", [128, 512], F32)) for j in range(2)]

        def setup(c):
            c.G.dma_start(out=ident[:], in_=id_i).inc("scst", 16)
            c.G.dma_start(out=trikm_sb[:], in_=trikm_i).inc("scst", 16)
            c.G.dma_start(out=ones_cr[:], in_=ones_i).inc("scst", 16)
            c.G.dma_start(out=qg_sb[:], in_=qg_i).inc("scst", 16)
            c.G.dma_start(out=br_sb[:], in_=br_i).inc("scst", 16)
            c.G.dma_start(out=epsc[:], in_=eps_i).inc("scst", 16)
            c.G.dma_start(out=wk_bf[:], in_=wk_i).inc("sw2", 16)
            c.G.dma_start(out=wv_bf[:], in_=wv_i).inc("sw2", 16)
            c.G.dma_start(out=wr_bf[:], in_=wr_i).inc("sw2", 16)
            for fc in range(4):
                c.G.dma_start(
                    out=wq_bf[:].rearrange("p (ch d) -> p ch d", ch=16)[:, :, fc * 512:(fc + 1) * 512],
                    in_=wq_i.rearrange("p (ch d) -> p ch d", ch=16)[:, :, fc * 512:(fc + 1) * 512],
                ).inc("sw3", 16)

        _emit_all(blk, ctxs, setup)
        CONSTS = 6 * 16
        W2 = 3 * 16

        def bw(c, bank):
            prev = c.m.get(("b", bank))
            if prev is not None:
                c.T.wait(prev[0], prev[1])

        def bank_of(bid):
            return {"k0": ps_k[0], "k1": ps_k[1], "v0": ps_v[0], "v1": ps_v[1],
                    "r0": ps_r[0], "r1": ps_r[1], "x0": ps_xt[0], "x1": ps_xt[1]}[bid]

        def p2_A(c, i):
            par = "sxa" if i % 2 == 0 else "sxb"
            if i >= 2:
                c.Y.wait("sse", c.m[("p2_s_done", i - 2)])
                c.Y.wait("sve", c.m[("p2_qprep", i - 2)])
            c.Y.dma_start(out=xb2[i % 2][:], in_=x_d[i * 128:(i + 1) * 128, :]).inc(par, 16)
            c.Y.dma_start(out=ropeb[i % 2][:], in_=rope_i[i * 128:(i + 1) * 128, :]).inc(par, 16)
            ld = c.c[par]

            c.V.wait(par, ld)
            c.V.reduce_sum(out=nmean[:], in_=xb2[i % 2][:], axis=AX.X, negate=True).inc("sve")
            c.V.wait("sve", c.c["sve"])
            c.V.tensor_scalar(out=nmean[:], in0=nmean[:], scalar1=1.0 / D, scalar2=None,
                              op0=ALU.mult).inc("sve")
            c.S.wait("sve", c.c["sve"])
            c.S.wait(par, ld)
            c.S.activation(out=xnbf[i % 2][:], in_=xb2[i % 2][:], func=AF.Square,
                           bias=nmean[:], scale=1.0, accum_out=st1[:, 15:16])
            c.S.activation(out=st3[:, 14:15], in_=st1[:, 15:16], func=AF.Ln,
                           scale=1.0 / D, bias=epsc[:, 0:1])
            c.S.activation(out=st3[:, 15:16], in_=st3[:, 14:15], func=AF.Exp,
                           scale=-0.5).inc("sse")
            c.V.wait("sse", c.c["sse"])
            c.V.tensor_tensor(out=nmist[:], in0=nmean[:], in1=st3[:, 15:16], op=ALU.mult).inc("sve")
            c.S.wait("sve", c.c["sve"])
            c.S.activation(out=xnbf[i % 2][:], in_=xb2[i % 2][:], func=AF.Identity,
                           scale=st3[:, 15:16], bias=nmist[:]).inc("sse")
            c.m[("p2_s_done", i)] = c.c["sse"]
            c.m[("p2_ln", i)] = c.c["sse"]

        def p2_XTtr(c, i):
            par = "sxa" if i % 2 == 0 else "sxb"
            spar = "soa" if i % 2 == 0 else "sob"
            for g in range(4):
                c.T.wait("sse", c.m[("p2_ln", i)])
                if i == 0 and g == 0:
                    c.T.wait("scst", CONSTS)
                bid = "x%d" % (g % 2)
                bw(c, bid)
                ins = None
                for q in range(4):
                    ch = g * 4 + q
                    ins = c.T.matmul(ps_xt[g % 2][:, q * 128:(q + 1) * 128],
                                     xnbf[i % 2][:, ch * 128:(ch + 1) * 128], ident[:],
                                     start=True, stop=True, skip_group_check=True)
                ins.inc("spe")
                c.S.wait("spe", c.c["spe"])
                if g == 0 and i >= 2:
                    c.S.wait(spar, c.m[("p2_st", i - 2)])
                    c.S.wait("spe", c.m[("p2_qc_done", i - 2)])
                c.S.activation(out=xnTb2[i % 2][:, g * 512:(g + 1) * 512],
                               in_=ps_xt[g % 2][:], func=AF.Identity).inc("sse")
                c.m[("b", bid)] = ("sse", c.c["sse"])
            c.m[("p2_xnT", i)] = c.c["sse"]

            c.G.wait("sse", c.c["sse"])
            c.G.dma_start(out=xnT_s.rearrange("c p t -> p c t")[:, :, i * 128:(i + 1) * 128],
                          in_=xnTb2[i % 2][:].rearrange("p (c t) -> p c t", c=16)).inc(spar, 16)
            c.m[("p2_st", i)] = c.c[spar]

        def p2_KVR(c, i):
            c.T.wait("sw2", W2)
            c.T.wait("sse", c.m[("p2_xnT", i)])
            for bid in ("k%d" % (i % 2), "v%d" % (i % 2), "r%d" % (i % 2)):
                bw(c, bid)
            ins = None
            for ch in range(16):
                xc = xnTb2[i % 2][:, ch * 128:(ch + 1) * 128]
                c.T.matmul(ps_k[i % 2][:], xc, wk_bf[:, ch * 512:(ch + 1) * 512],
                           start=(ch == 0), stop=(ch == 15), skip_group_check=True)
                c.T.matmul(ps_v[i % 2][:], xc, wv_bf[:, ch * 512:(ch + 1) * 512],
                           start=(ch == 0), stop=(ch == 15), skip_group_check=True)
                ins = c.T.matmul(ps_r[i % 2][:, 0:NE], xc, wr_bf[:, ch * NE:(ch + 1) * NE],
                                 start=(ch == 0), stop=(ch == 15), skip_group_check=True)
            ins.inc("spe")
            c.m[("p2_kvr", i)] = c.c["spe"]

        def p2_KP(c, i):
            par = "sxa" if i % 2 == 0 else "sxb"
            spar = "soa" if i % 2 == 0 else "sob"
            kvr = c.m[("p2_kvr", i)]
            c.V.wait("spe", kvr)
            c.V.tensor_copy(out=rf6[:], in_=ps_r[i % 2][:, 0:NE]).inc("sve")
            c.m[("b", "r%d" % (i % 2))] = ("sve", c.c["sve"])
            c.S.wait("spe", kvr)
            if i >= 1:
                c.S.wait("sve", c.m[("p2_kbf", i - 1)])
            c.S.activation(out=ktm[:], in_=ps_k[i % 2][:], func=AF.Identity).inc("sse")
            c.m[("b", "k%d" % (i % 2))] = ("sse", c.c["sse"])
            if i >= 2:
                c.S.wait(spar, c.m[("p2_st_kv", i - 2)])
            c.S.activation(out=vb[i % 2][:], in_=ps_v[i % 2][:], func=AF.Identity).inc("sse")
            c.m[("b", "v%d" % (i % 2))] = ("sse", c.c["sse"])
            c.G.wait("sse", c.c["sse"])
            c.G.dma_start(out=v_s[:, i * 512:(i + 1) * 512], in_=vb[i % 2][:]).inc(spar, 16)

            c.S.activation(out=kbf[:], in_=ktm[:], func=AF.Square).inc("sse")
            c.V.wait("sse", c.c["sse"])
            c.V.tensor_reduce(out=st1[:, 0:NKV],
                              in_=kbf[:].rearrange("p (h d) -> p h d", h=NKV),
                              axis=AX.X, op=ALU.add).inc("sve")
            c.S.wait("sve", c.c["sve"])
            c.S.activation(out=st3[:, 0:NKV], in_=st1[:, 0:NKV], func=AF.Ln,
                           scale=1.0 / HD, bias=epsc[:, 1:2])
            c.S.activation(out=st3[:, 0:NKV], in_=st3[:, 0:NKV], func=AF.Exp,
                           scale=-0.5).inc("sse")
            c.V.wait("sse", c.c["sse"])
            for h in range(NKV):
                c.V.tensor_scalar(out=ktm[:, h * 128:(h + 1) * 128],
                                  in0=ktm[:, h * 128:(h + 1) * 128],
                                  scalar1=st3[:, h:h + 1], scalar2=None, op0=ALU.mult)
            kv_ = ktm[:].rearrange("p (h d) -> p h d", h=NKV)
            co = ropeb[i % 2][:, 0:32].rearrange("p (h d) -> p h d", h=NKV)
            si = ropeb[i % 2][:, 128:160].rearrange("p (h d) -> p h d", h=NKV)
            x1, x2 = kv_[:, :, 0:8], kv_[:, :, 8:16]
            r_ = rtmp[:].rearrange("p (h d) -> p h d", h=64)
            r1, r2, r3, r4 = r_[:, 0:4], r_[:, 4:8], r_[:, 8:12], r_[:, 12:16]
            c.V.tensor_tensor(out=r1, in0=x1, in1=co, op=ALU.mult)
            c.V.tensor_tensor(out=r2, in0=x1, in1=si, op=ALU.mult)
            c.V.tensor_tensor(out=r3, in0=x2, in1=si, op=ALU.mult)
            c.V.tensor_tensor(out=x1, in0=r1, in1=r3, op=ALU.add)
            c.V.tensor_tensor(out=r4, in0=x2, in1=co, op=ALU.mult)
            c.V.tensor_tensor(out=x2, in0=r2, in1=r4, op=ALU.subtract).inc("sve")
            c.V.wait("sve", c.c["sve"])
            c.V.tensor_copy(out=kbf[:], in_=ktm[:]).inc("sve")
            c.m[("p2_kbf", i)] = c.c["sve"]

            def vs(ins):
                ins.inc("sve")
                c.V.wait("sve", c.c["sve"])
            vs(c.V.tensor_tensor(out=rf6[:], in0=rf6[:], in1=br_sb[:], op=ALU.add))
            c.V.reduce_max(out=m6[:], in_=rf6[:], axis=AX.X, negate=True).inc("sve")
            c.S.wait("sve", c.c["sve"])
            c.S.activation(out=g6a[:], in_=rf6[:], func=AF.Exp, bias=m6[:],
                           accum_out=z6[:]).inc("sse")
            c.V.wait("sse", c.c["sse"])
            vs(c.V.reciprocal(out=m6[:], in_=z6[:]))
            vs(c.V.tensor_scalar(out=g6a[:], in0=g6a[:], scalar1=m6[:], scalar2=None,
                                 op0=ALU.mult))
            vs(c.V.reduce_max(out=m6[:], in_=g6a[:], axis=AX.X))
            vs(c.V.tensor_scalar(out=g6b[:], in0=g6a[:], scalar1=m6[:], scalar2=None,
                                 op0=ALU.is_ge))
            vs(c.V.scalar_tensor_tensor(out=g6b[:], in0=g6b[:], scalar=-2.0, in1=g6a[:],
                                        op0=ALU.mult, op1=ALU.add))
            vs(c.V.reduce_max(out=m6[:], in_=g6b[:], axis=AX.X))
            vs(c.V.tensor_scalar(out=g6b[:], in0=g6a[:], scalar1=m6[:], scalar2=None,
                                 op0=ALU.is_ge))
            c.V.tensor_tensor(out=w6[:, i * NE:(i + 1) * NE], in0=g6a[:], in1=g6b[:],
                              op=ALU.mult).inc("sve")
            c.m[("p2_v_done", i)] = c.c["sve"]

        def p2_QC(c, i):
            par = i % 2
            qb = ["k%d" % (1 - par), "v%d" % (1 - par), "r%d" % (1 - par), "r%d" % par]
            if i == 0:
                c.T.wait("sw3", 64)
            for pair in range(2):
                f0, f1 = 2 * pair, 2 * pair + 1
                bw(c, qb[f0])
                bw(c, qb[f1])
                ins = None
                for ch in range(16):
                    xc = xnTb2[i % 2][:, ch * 128:(ch + 1) * 128]
                    c.T.matmul(bank_of(qb[f0])[:], xc,
                               wq_bf[:, ch * D + f0 * 512: ch * D + (f0 + 1) * 512],
                               start=(ch == 0), stop=(ch == 15), skip_group_check=True)
                    ins = c.T.matmul(bank_of(qb[f1])[:], xc,
                                     wq_bf[:, ch * D + f1 * 512: ch * D + (f1 + 1) * 512],
                                     start=(ch == 0), stop=(ch == 15), skip_group_check=True)
                ins.inc("spe")
                c.S.wait("spe", c.c["spe"])
                if pair == 0 and i >= 1:
                    c.S.wait("spe", c.m[("p2_qtr", i - 1)])
                c.S.activation(out=qbf[:, f0 * 512:(f0 + 1) * 512],
                               in_=bank_of(qb[f0])[:], func=AF.Identity).inc("sse")
                c.m[("b", qb[f0])] = ("sse", c.c["sse"])
                c.S.activation(out=qbf[:, f1 * 512:(f1 + 1) * 512],
                               in_=bank_of(qb[f1])[:], func=AF.Identity).inc("sse")
                c.m[("b", qb[f1])] = ("sse", c.c["sse"])
                c.S.wait("sve", c.c["sve"])
                c.S.activation(out=qsq[:, pair * 1024:(pair + 1) * 1024],
                               in_=qbf[:, pair * 1024:(pair + 1) * 1024],
                               func=AF.Square).inc("sse")
                c.V.wait("sse", c.c["sse"])
                c.V.tensor_reduce(
                    out=st2[:, pair * 8:(pair + 1) * 8],
                    in_=qsq[:, pair * 1024:(pair + 1) * 1024].rearrange(
                        "p (h d) -> p h d", h=8),
                    axis=AX.X, op=ALU.add).inc("sve")
            c.m[("p2_qc_done", i)] = c.c["spe"]
            c.m[("p2_qstat", i)] = c.c["sve"]

        def p2_KT(c, i):
            spar = "soa" if i % 2 == 0 else "sob"
            c.T.wait("sve", c.m[("p2_kbf", i)])
            bw(c, "x0")
            ins = None
            for h in range(NKV):
                ins = c.T.matmul(ps_xt[0][:, h * 128:(h + 1) * 128],
                                 kbf[:, h * 128:(h + 1) * 128], ident[:],
                                 start=True, stop=True, skip_group_check=True)
            ins.inc("spe")
            c.m[("p2_ktr", i)] = c.c["spe"]
            c.S.wait("spe", c.c["spe"])
            if i >= 2:
                c.S.wait(spar, c.m[("p2_st_kv", i - 2)])
            c.S.activation(out=kTb[i % 2][:], in_=ps_xt[0][:], func=AF.Identity).inc("sse")
            c.m[("b", "x0")] = ("sse", c.c["sse"])
            c.G.wait("sse", c.c["sse"])
            c.G.dma_start(
                out=kT_s.rearrange("p (h t) -> p h t", h=NKV)[:, :, i * 128:(i + 1) * 128],
                in_=kTb[i % 2][:].rearrange("p (h t) -> p h t", h=NKV)).inc(spar, 16)
            c.m[("p2_st_kv", i)] = c.c[spar]

        def p2_QP(c, i):
            par = "sxa" if i % 2 == 0 else "sxb"
            c.S.wait("sve", c.m[("p2_qstat", i)])
            c.S.activation(out=st3[:], in_=st2[:], func=AF.Ln,
                           scale=1.0 / HD, bias=epsc[:, 1:2])
            c.S.activation(out=st3[:], in_=st3[:], func=AF.Exp, scale=-0.5).inc("sse")
            c.V.wait("sse", c.c["sse"])
            c.V.tensor_tensor(out=sc16[:], in0=st3[:], in1=qg_sb[:], op=ALU.mult).inc("sve")
            c.V.wait("sve", c.c["sve"])
            for h in range(NH):
                c.V.tensor_scalar(out=qbf[:, h * 128:(h + 1) * 128],
                                  in0=qbf[:, h * 128:(h + 1) * 128],
                                  scalar1=sc16[:, h:h + 1], scalar2=None, op0=ALU.mult)
            qv = qbf[:].rearrange("p (h d) -> p h d", h=NH)
            co = ropeb[i % 2][:, 0:128].rearrange("p (h d) -> p h d", h=NH)
            si = ropeb[i % 2][:, 128:256].rearrange("p (h d) -> p h d", h=NH)
            x1, x2 = qv[:, :, 0:8], qv[:, :, 8:16]
            r_ = rtmp[:].rearrange("p (h d) -> p h d", h=64)
            r1, r2, r3, r4 = r_[:, 0:16], r_[:, 16:32], r_[:, 32:48], r_[:, 48:64]
            c.V.tensor_tensor(out=r1, in0=x1, in1=co, op=ALU.mult)
            c.V.tensor_tensor(out=r2, in0=x1, in1=si, op=ALU.mult)
            c.V.tensor_tensor(out=r3, in0=x2, in1=si, op=ALU.mult)
            c.V.tensor_tensor(out=x1, in0=r1, in1=r3, op=ALU.add)
            c.V.tensor_tensor(out=r4, in0=x2, in1=co, op=ALU.mult)
            c.V.tensor_tensor(out=x2, in0=r2, in1=r4, op=ALU.subtract).inc("sve")
            c.m[("p2_qprep", i)] = c.c["sve"]

        def p2_QT(c, i):
            spar = "soa" if i % 2 == 0 else "sob"
            c.T.wait("sve", c.m[("p2_qprep", i)])
            for g in range(4):
                bid = "x%d" % ((g + 1) % 2)
                bw(c, bid)
                ins = None
                for hh in range(4):
                    h = g * 4 + hh
                    ins = c.T.matmul(ps_xt[(g + 1) % 2][:, hh * 128:(hh + 1) * 128],
                                     qbf[:, h * 128:(h + 1) * 128], ident[:],
                                     start=True, stop=True, skip_group_check=True)
                ins.inc("spe")
                c.S.wait("spe", c.c["spe"])
                if g == 0 and i >= 2:
                    c.S.wait(spar, c.m[("p2_st_q", i - 2)])
                c.S.activation(out=qTb[i % 2][:, g * 512:(g + 1) * 512],
                               in_=ps_xt[(g + 1) % 2][:], func=AF.Identity).inc("sse")
                c.m[("b", bid)] = ("sse", c.c["sse"])
            c.m[("p2_qtr", i)] = c.c["spe"]
            c.G.wait("sse", c.c["sse"])
            c.G.dma_start(out=qT_s.rearrange("c p t -> p c t")[:, :, i * 128:(i + 1) * 128],
                          in_=qTb[i % 2][:].rearrange("p (c t) -> p c t", c=16)).inc(spar, 16)
            c.m[("p2_st_q", i)] = c.c[spar]

        def p2(c):
            p2_A(c, 0)
            p2_XTtr(c, 0)
            p2_KVR(c, 0)
            for i in range(nt):
                p2_KP(c, i)
                if i + 1 < nt:
                    p2_A(c, i + 1)
                p2_QC(c, i)
                p2_KT(c, i)
                if i + 1 < nt:
                    p2_XTtr(c, i + 1)
                    p2_KVR(c, i + 1)
                p2_QP(c, i)
                p2_QT(c, i)
            _barrier(c)

        _emit_all(blk, ctxs, p2)
        es2.close()

        es34 = contextlib.ExitStack()
        wb_bf = es34.enter_context(nc.sbuf_tensor("wb_bf", [128, 16 * D], BF16))
        ac_bf = es34.enter_context(nc.sbuf_tensor("ac_bf", [128, 16 * NCAT], BF16))
        bc_bf = es34.enter_context(nc.sbuf_tensor("bc_bf", [NCAT, D], BF16))
        bb_sb = es34.enter_context(nc.sbuf_tensor("bb_sb", [128, D], F32))

        es3 = contextlib.ExitStack()
        kcache = es3.enter_context(nc.sbuf_tensor("kcache", [128, NKV * T], BF16))
        vcache = es3.enter_context(nc.sbuf_tensor("vcache", [128, NT * 512], BF16))
        qTb3 = [es3.enter_context(nc.sbuf_tensor(f"qTb3_{j}", [128, D], BF16)) for j in range(2)]
        pT = es3.enter_context(nc.sbuf_tensor("pT", [128, 12 * 512], BF16))
        yTb = [es3.enter_context(nc.sbuf_tensor(f"yTb{j}", [128, D], BF16)) for j in range(2)]
        nb1 = es3.enter_context(nc.sbuf_tensor("nb1", [128, 1], F32))
        zrow = es3.enter_context(nc.sbuf_tensor("zrow", [1, 512], F32))
        zrec = es3.enter_context(nc.sbuf_tensor("zrec", [1, D], BF16))
        rzbc = es3.enter_context(nc.sbuf_tensor("rzbc", [128, 512], F32))
        ps_d = [es3.enter_context(nc.psum_tensor(f"ps_d{j}", [128, 1024], F32)) for j in range(2)]
        ps_yu = [es3.enter_context(nc.psum_tensor(f"ps_yu{j}", [128, 512], F32)) for j in range(2)]
        ps_z = [es3.enter_context(nc.psum_tensor(f"ps_z{j}", [128, 512], F32)) for j in range(2)]

        def _bkwait(c, eng, bid):
            prev = c.m.get(("p3bk", bid))
            if prev is not None:
                eng.wait(prev[0], prev[1])

        def p3_attn_kv(c, i, kv, pre_cb=None):
            nst = i + 1
            NSL = 12
            npair = (nst + 1) // 2

            def emit_pair(p):
                s0 = 2 * p
                s1 = min(2 * p + 1, nst - 1)
                w = s1 - s0 + 1
                bk = ps_d[p % 2]
                _bkwait(c, c.T, ("d", p % 2))
                if p == 0 and kv == 0:
                    par = "sxa" if i % 2 == 0 else "sxb"
                    c.T.wait(par, c.m[("p3_ld", i)])
                ins = None
                for s in range(s0, s1 + 1):
                    last = (s == nst - 1)
                    ins = c.T.matmul(bk[:, (s - s0) * 512:(s - s0 + 1) * 512],
                                     kcache[:, kv * T + s * 128: kv * T + (s + 1) * 128],
                                     qTb3[i % 2][:, kv * 512:(kv + 1) * 512],
                                     start=True, stop=not last, skip_group_check=True)
                    if last:
                        ins = c.T.matmul(bk[:, (s - s0) * 512:(s - s0 + 1) * 512],
                                         ident[:], trikm_sb[:],
                                         start=False, stop=True, skip_group_check=True)
                ins.inc("spe")
                c.S.wait("spe", c.c["spe"])
                prev = c.m.get(("p3_pt", s0 % NSL))
                if prev is not None:
                    c.S.wait("spe", prev)
                c.S.activation(out=pT[:, (s0 % NSL) * 512:((s0 % NSL) + w) * 512],
                               in_=bk[:, 0:w * 512],
                               func=AF.Exp, bias=nb1[:], scale=SCALE).inc("sse")
                c.m[("p3bk", ("d", p % 2))] = ("sse", c.c["sse"])
                c.m[("p3_expp", p)] = c.c["sse"]

            def emit_pv(s, need_wait):
                last = (s == nst - 1)
                if need_wait:
                    c.T.wait("sse", c.m[("p3_expp", s // 2)])
                if s == 0:
                    _bkwait(c, c.T, ("yu", kv % 2))
                c.T.matmul(ps_yu[kv % 2][:],
                           vcache[:, s * 512 + kv * 128: s * 512 + (kv + 1) * 128],
                           pT[:, (s % NSL) * 512:(s % NSL + 1) * 512],
                           start=(s == 0), stop=last, skip_group_check=True).inc("spe")

            def emit_zbatch(s0, s1):
                c.T.wait("sse", c.m[("p3_expp", (s1 - 1) // 2)])
                if s0 == 0:
                    _bkwait(c, c.T, ("z", kv % 2))
                ins = None
                for s in range(s0, s1):
                    ins = c.T.matmul(ps_z[kv % 2][0:1, :], ones_cr[:, 0:1],
                                     pT[:, (s % NSL) * 512:(s % NSL + 1) * 512],
                                     start=(s == 0), stop=(s == nst - 1),
                                     skip_group_check=True)
                ins.inc("spe")
                for s in range(s0, s1):
                    c.m[("p3_pt", s % NSL)] = c.c["spe"]

            emit_pair(0)
            if npair > 1:
                emit_pair(1)
            zb = 0
            for p in range(npair):
                if p + 2 < npair:
                    emit_pair(p + 2)
                if p == 0 and pre_cb is not None:
                    pre_cb[0]()
                if p == min(4, npair - 1) and pre_cb is not None:
                    pre_cb[1]()
                s0 = 2 * p
                s1 = min(2 * p + 1, nst - 1)
                emit_pv(s0, True)
                if s1 > s0:
                    emit_pv(s1, False)
                if s1 - zb >= 6 or s1 == nst - 1:
                    emit_zbatch(zb, s1 + 1)
                    zb = s1 + 1
            c.m[("p3_pvz", kv)] = c.c["spe"]

        def p3_epi_a(c, i, kv):
            c.V.wait("spe", c.m[("p3_pvz", kv)])
            c.V.tensor_copy(out=zrow[0:1, :], in_=ps_z[kv % 2][0:1, :]).inc("sve")
            c.m[("p3bk", ("z", kv % 2))] = ("sve", c.c["sve"])
            c.V.wait("sve", c.c["sve"])
            c.V.reciprocal(out=zrow[0:1, :], in_=zrow[0:1, :]).inc("sve")
            c.V.wait("sve", c.c["sve"])
            c.V.tensor_copy(out=zrec[0:1, kv * 512:(kv + 1) * 512], in_=zrow[0:1, :]).inc("sve")
            c.m[("p3_zr", kv)] = c.c["sve"]

        def p3_epi_b(c, i, kv):
            c.T.wait("sve", c.m[("p3_zr", kv)])
            c.T.matmul(ps_z[kv % 2][:], ones_cr[0:1, 1:129], zrec[0:1, kv * 512:(kv + 1) * 512],
                       start=True, stop=True, skip_group_check=True).inc("spe")
            c.V.wait("spe", c.c["spe"])
            c.V.tensor_copy(out=rzbc[:], in_=ps_z[kv % 2][:]).inc("sve")
            c.m[("p3bk", ("z", kv % 2))] = ("sve", c.c["sve"])
            c.V.wait("sve", c.c["sve"])
            if kv == 0 and i >= 2:
                spar_prev = "soa" if i % 2 == 0 else "sob"
                c.V.wait(spar_prev, c.m[("p3_st", i - 2)])
            c.V.scalar_tensor_tensor(out=yTb[i % 2][:, kv * 512:(kv + 1) * 512],
                                     in0=ps_yu[kv % 2][:],
                                     scalar=1.0, in1=rzbc[:],
                                     op0=ALU.mult, op1=ALU.mult).inc("sve")
            c.m[("p3bk", ("yu", kv % 2))] = ("sve", c.c["sve"])

        def p3(c):
            c.V.wait("scst", CONSTS)
            c.V.reduce_max(out=nb1[:], in_=qg_sb[:], axis=AX.X).inc("sve")
            c.V.wait("sve", c.c["sve"])
            c.V.tensor_scalar(out=nb1[:], in0=nb1[:], scalar1=-math.sqrt(float(HD)),
                              scalar2=40.0, op0=ALU.mult, op1=ALU.add).inc("sve")
            c.V.wait("sve", c.c["sve"])
            for i in range(nt):
                par = "sxa" if i % 2 == 0 else "sxb"
                spar = "soa" if i % 2 == 0 else "sob"
                if i >= 2:
                    c.Y.wait("spe", c.m[("p3_pe_done", i - 2)])
                c.Y.dma_start(out=qTb3[i % 2][:].rearrange("p (c t) -> p c t", c=16),
                              in_=qT_s.rearrange("c p t -> p c t")[:, :, i * 128:(i + 1) * 128]
                              ).inc(par, 16)
                c.Y.dma_start(
                    out=kcache[:].rearrange("p (h t) -> p h t", h=NKV)[:, :, i * 128:(i + 1) * 128],
                    in_=kT_s.rearrange("p (h t) -> p h t", h=NKV)[:, :, i * 128:(i + 1) * 128]
                ).inc(par, 16)
                c.Y.dma_start(out=vcache[:, i * 512:(i + 1) * 512],
                              in_=v_s[:, i * 512:(i + 1) * 512]).inc(par, 16)
                c.m[("p3_ld", i)] = c.c[par]

                if i >= 4 and (i - 4) % 3 == 0 and (i - 4) // 3 < 8:
                    j = (i - 4) // 3
                    fc, hf = j // 2, j % 2
                    c.G.dma_start(
                        out=wb_bf[:].rearrange("p (ch d) -> p ch d", ch=16)[
                            :, hf * 8:(hf + 1) * 8, fc * 512:(fc + 1) * 512],
                        in_=wb_i.rearrange("p (ch d) -> p ch d", ch=16)[
                            :, hf * 8:(hf + 1) * 8, fc * 512:(fc + 1) * 512],
                    ).inc("sw45", 16)
                elif i == 26:
                    c.G.dma_start(out=ac_bf[:], in_=ac_i).inc("sw45", 16)
                elif i == 27:
                    c.G.dma_start(out=bc_bf[:], in_=bc_i).inc("sw45", 16)
                elif i == 28:
                    c.G.dma_start(out=bb_sb[:], in_=bb_i).inc("sw45", 16)

                for kv in range(NKV):
                    if kv == 0:
                        p3_attn_kv(c, i, kv)
                    else:
                        p3_attn_kv(c, i, kv,
                                   pre_cb=(lambda kk=kv - 1: p3_epi_a(c, i, kk),
                                           lambda kk=kv - 1: p3_epi_b(c, i, kk)))
                p3_epi_a(c, i, NKV - 1)
                p3_epi_b(c, i, NKV - 1)
                c.m[("p3_pe_done", i)] = c.c["spe"]
                c.G.wait("sve", c.c["sve"])
                c.G.dma_start(out=yT_s.rearrange("c p t -> p c t")[:, :, i * 128:(i + 1) * 128],
                              in_=yTb[i % 2][:].rearrange("p (c t) -> p c t", c=16)).inc(spar, 16)
                c.m[("p3_st", i)] = c.c[spar]
            _barrier(c)

        _emit_all(blk, ctxs, p3)
        es3.close()

        es4 = contextlib.ExitStack()
        wp_bf = es4.enter_context(nc.sbuf_tensor("wp_bf", [128, 16 * D], BF16))
        xnTb4 = [es4.enter_context(nc.sbuf_tensor(f"xnTb4_{j}", [128, D], BF16)) for j in range(2)]
        yTb4 = [es4.enter_context(nc.sbuf_tensor(f"yTb4_{j}", [128, D], BF16)) for j in range(2)]
        xb4 = [es4.enter_context(nc.sbuf_tensor(f"xb4_{j}", [128, D], F32)) for j in range(2)]
        gbuf = es4.enter_context(nc.sbuf_tensor("gbuf", [128, D], F32))
        midf = es4.enter_context(nc.sbuf_tensor("midf", [128, NCAT], F32))
        hwbf = es4.enter_context(nc.sbuf_tensor("hwbf", [128, NCAT], BF16))
        hwT = es4.enter_context(nc.sbuf_tensor("hwT", [128, 128], BF16))
        ps_f = [es4.enter_context(nc.psum_tensor(f"ps_f{j}", [128, 512], F32)) for j in range(4)]
        ps_m = es4.enter_context(nc.psum_tensor("ps_m", [128, 512], F32))
        ps_h = es4.enter_context(nc.psum_tensor("ps_h", [128, 512], F32))
        ps_w = [es4.enter_context(nc.psum_tensor(f"ps_w{j}", [128, 512], F32)) for j in range(2)]

        W45 = 11 * 16

        def p45_setup(c):
            for fc in range(4):
                c.G.dma_start(
                    out=wp_bf[:].rearrange("p (ch d) -> p ch d", ch=16)[:, :, fc * 512:(fc + 1) * 512],
                    in_=wp_i.rearrange("p (ch d) -> p ch d", ch=16)[:, :, fc * 512:(fc + 1) * 512],
                ).inc("sw45", 16)

        def p45_tile(c, i):
            par = "sxa" if i % 2 == 0 else "sxb"
            spar = "soa" if i % 2 == 0 else "sob"
            if i >= 2:
                c.Y.wait("spe", c.m[("p45_pe_done", i - 2)])
                c.Y.wait("sve", c.m[("p45_v_done", i - 2)])
                c.Y.wait(spar, c.m[("p45_st", i - 2)])
            c.Y.dma_start(out=xnTb4[i % 2][:].rearrange("p (c t) -> p c t", c=16),
                          in_=xnT_s.rearrange("c p t -> p c t")[:, :, i * 128:(i + 1) * 128]
                          ).inc(par, 16)
            c.Y.dma_start(out=yTb4[i % 2][:].rearrange("p (c t) -> p c t", c=16),
                          in_=yT_s.rearrange("c p t -> p c t")[:, :, i * 128:(i + 1) * 128]
                          ).inc(par, 16)
            c.Y.dma_start(out=xb4[i % 2][:], in_=x_d[i * 128:(i + 1) * 128, :]).inc(par, 16)
            ld = c.c[par]

            c.T.wait(par, ld)
            if i == 0:
                c.T.wait("sw45", 9 * 16)
            if i >= 1:
                c.T.wait("sve", c.m[("p45_midev", i - 1)])
                for fc in range(4):
                    c.T.wait("sve", c.m[("p45_gev", i - 1, fc)])
            ins = None
            for ch in range(16):
                ins = c.T.matmul(ps_m[:, 0:NCAT], xnTb4[i % 2][:, ch * 128:(ch + 1) * 128],
                                 ac_bf[:, ch * NCAT:(ch + 1) * NCAT],
                                 start=(ch == 0), stop=(ch == 15), skip_group_check=True)
                for fc in range(4):
                    c.T.matmul(ps_f[fc][:], xnTb4[i % 2][:, ch * 128:(ch + 1) * 128],
                               wb_bf[:, ch * D + fc * 512: ch * D + (fc + 1) * 512],
                               start=(ch == 0), stop=False, skip_group_check=True)
            ins.inc("spe")
            mid_mark = c.c["spe"]

            c.V.wait("spe", mid_mark)
            c.V.tensor_copy(out=midf[:], in_=ps_m[:, 0:NCAT]).inc("sve")
            c.m[("p45_midev", i)] = c.c["sve"]
            c.V.wait("sve", c.c["sve"])
            for e in range(NE):
                c.V.tensor_scalar(out=midf[:, 16 + e * NR:16 + (e + 1) * NR],
                                  in0=midf[:, 16 + e * NR:16 + (e + 1) * NR],
                                  scalar1=w6[:, i * NE + e:i * NE + e + 1],
                                  scalar2=None, op0=ALU.mult).inc("sve")
            c.V.wait("sve", c.c["sve"])
            c.V.tensor_copy(out=hwbf[:], in_=midf[:]).inc("sve")
            hw_mark = c.c["sve"]
            c.T.wait("sve", hw_mark)
            if i >= 1:
                c.T.wait("sve", c.m[("p45_hwev", i - 1)])
            c.T.matmul(ps_h[0:NCAT, 0:128], hwbf[:, 0:NCAT], ident[:],
                       start=True, stop=True, skip_group_check=True).inc("spe")
            c.V.wait("spe", c.c["spe"])
            c.V.tensor_copy(out=hwT[0:NCAT, :], in_=ps_h[0:NCAT, 0:128]).inc("sve")
            c.m[("p45_hwev", i)] = c.c["sve"]

            c.T.wait("sve", c.c["sve"])
            if i == 0:
                c.T.wait("sw45", 10 * 16)
            ins = None
            for fc in range(4):
                ins = c.T.matmul(ps_f[fc][:], hwT[0:NCAT, :],
                                 bc_bf[0:NCAT, fc * 512:(fc + 1) * 512],
                                 start=False, stop=True, skip_group_check=True)
            ins.inc("spe")
            ffn_mark = c.c["spe"]

            c.V.wait("spe", ffn_mark)
            if i == 0:
                c.V.wait("sw45", W45)
            silu_marks = {}

            def bias_silu(fc):
                sl = slice(fc * 512, (fc + 1) * 512)
                c.V.tensor_tensor(out=gbuf[:, sl], in0=ps_f[fc][:],
                                  in1=bb_sb[:, sl], op=ALU.add).inc("sve")
                c.m[("p45_gev", i, fc)] = c.c["sve"]
                c.S.wait("sve", c.c["sve"])
                c.S.activation(out=gbuf[:, sl], in_=gbuf[:, sl],
                               func=AF.Silu).inc("sse")
                silu_marks[fc] = c.c["sse"]

            bias_silu(0)
            bias_silu(1)

            for pair in range(2):
                if i == 0:
                    c.T.wait("sw45", W45 + (pair + 1) * 32)
                for fc in (pair * 2, pair * 2 + 1):
                    if ("p45_sgev", i, fc - 2) in c.m:
                        c.T.wait("sve", c.m[("p45_sgev", i, fc - 2)])
                    elif ("p45_sg_prev", fc % 2) in c.m:
                        c.T.wait("sve", c.m[("p45_sg_prev", fc % 2)])
                ins = None
                for ch in range(16):
                    for fc in (pair * 2, pair * 2 + 1):
                        ins = c.T.matmul(ps_w[fc % 2][:], yTb4[i % 2][:, ch * 128:(ch + 1) * 128],
                                         wp_bf[:, ch * D + fc * 512: ch * D + (fc + 1) * 512],
                                         start=(ch == 0), stop=(ch == 15), skip_group_check=True)
                ins.inc("spe")
                wp_mark = c.c["spe"]
                if pair == 0:
                    bias_silu(2)
                    bias_silu(3)
                for fc in (pair * 2, pair * 2 + 1):
                    c.V.wait("spe", wp_mark)
                    c.V.wait("sse", silu_marks[fc])
                    sl = slice(fc * 512, (fc + 1) * 512)
                    c.V.scalar_tensor_tensor(out=gbuf[:, sl], in0=ps_w[fc % 2][:],
                                             scalar=1.0, in1=gbuf[:, sl],
                                             op0=ALU.mult, op1=ALU.mult)
                    c.V.tensor_tensor(out=xb4[i % 2][:, sl], in0=gbuf[:, sl],
                                      in1=xb4[i % 2][:, sl], op=ALU.add).inc("sve")
                    c.m[("p45_sgev", i, fc)] = c.c["sve"]
            c.m[("p45_sg_prev", 0)] = c.m[("p45_sgev", i, 2)]
            c.m[("p45_sg_prev", 1)] = c.m[("p45_sgev", i, 3)]
            c.m[("p45_v_done", i)] = c.c["sve"]
            c.m[("p45_pe_done", i)] = c.c["spe"]

            c.G.wait("sve", c.c["sve"])
            c.G.dma_start(out=out_d[i * 128:(i + 1) * 128, :], in_=xb4[i % 2][:]).inc(spar, 16)
            c.m[("p45_st", i)] = c.c[spar]

        def p45(c):
            p45_setup(c)
            for i in range(nt):
                p45_tile(c, i)
            c.G.wait("soa", c.c["soa"])
            c.G.wait("sob", c.c["sob"])

        _emit_all(blk, ctxs, p45)
        es4.close()
        es34.close()

    return nc


_NC_CACHE = {}


def _get_nc():
    if "nc" not in _NC_CACHE:
        _NC_CACHE["nc"] = build_nc()
    return _NC_CACHE["nc"]


def _img(mat):
    n = mat.shape[1]
    return np.ascontiguousarray(
        mat.reshape(16, 128, n).transpose(1, 0, 2).reshape(128, 16 * n).astype(np.float32))


def prepare_shared(L0_attn_wq, L0_attn_wk, L0_attn_wv, L0_attn_wp, L0_attn_qg,
                   L0_ffn_W_base, L0_ffn_b_base, L0_ffn_A_s, L0_ffn_B_s, L0_ffn_A_r,
                   L0_ffn_B_r, L0_ffn_W_router, L0_ffn_b_router, **_ignore):
    wq_img = _img(np.asarray(L0_attn_wq, np.float32).T)
    wk_img = _img(np.asarray(L0_attn_wk, np.float32).T)
    wv_img = _img(np.asarray(L0_attn_wv, np.float32).T)
    wp_img = _img(np.asarray(L0_attn_wp, np.float32).T)
    wb_img = _img(np.asarray(L0_ffn_W_base, np.float32))
    wr_img = _img(np.asarray(L0_ffn_W_router, np.float32))
    A_r = np.asarray(L0_ffn_A_r, np.float32)
    acat = np.concatenate([np.asarray(L0_ffn_A_s, np.float32),
                           A_r.transpose(1, 0, 2).reshape(D, NE * NR)], axis=1)
    acat_img = _img(acat)
    bcat_img = np.ascontiguousarray(np.concatenate(
        [np.asarray(L0_ffn_B_s, np.float32),
         np.asarray(L0_ffn_B_r, np.float32).reshape(NE * NR, D)], axis=0))
    bb_rep = np.ascontiguousarray(np.tile(np.asarray(L0_ffn_b_base, np.float32)[None, :], (128, 1)))
    br_rep = np.ascontiguousarray(np.tile(np.asarray(L0_ffn_b_router, np.float32)[None, :], (128, 1)))
    qg_rep = np.ascontiguousarray(np.tile(np.asarray(L0_attn_qg, np.float32)[None, :], (128, 1)))

    ident_bf = np.eye(128, dtype=_bf)
    trikm1 = np.where(np.arange(128)[:, None] <= np.arange(128)[None, :],
                      np.float32(0.0), np.float32(NEGBIG)).astype(np.float32)
    trikm = np.ascontiguousarray(np.tile(trikm1, (1, 4)).astype(_bf))
    ones_cr = np.ones((128, 129), dtype=_bf)
    eps_c = np.ascontiguousarray(np.tile(np.array([[LN_EPS, RMS_EPS]], np.float32), (128, 1)))
    t_ = np.arange(T, dtype=np.float32)
    inv = (1.0 / (10000.0 ** (np.arange(0, PD, 2, dtype=np.float32) / PD))).astype(np.float32)
    f_ = t_[:, None] * inv[None, :]
    rope_tab = np.ascontiguousarray(np.concatenate(
        [np.tile(np.cos(f_).astype(np.float32), (1, 16)),
         np.tile(np.sin(f_).astype(np.float32), (1, 16))], axis=1))

    return dict(wq_img=wq_img, wk_img=wk_img, wv_img=wv_img, wp_img=wp_img,
                wb_img=wb_img, wr_img=wr_img, acat_img=acat_img, bcat_img=bcat_img,
                bb_rep=bb_rep, br_rep=br_rep, qg_rep=qg_rep, ident_bf=ident_bf,
                trikm=trikm, ones_cr=ones_cr, eps_c=eps_c, rope_tab=rope_tab)


def kernel(**inputs):
    x = np.asarray(inputs["L0_attn_x"], dtype=np.float32)
    shared = prepare_shared(**inputs)
    in_maps = [dict(shared, x=np.ascontiguousarray(x[b])) for b in range(B)]
    nc = _get_nc()
    res = run_bass_kernel_spmd(nc, in_maps, core_ids=list(range(B)))
    return np.stack([np.asarray(res.results[b]["out"], dtype=np.float32) for b in range(B)],
                    axis=0)


# version 20
# speedup vs baseline: 1.0888x; 1.0102x over previous
import contextlib
import math

import numpy as np
import ml_dtypes

import concourse.bass as bass
import concourse.mybir as mybir
from concourse.bass_utils import run_bass_kernel_spmd

F32 = mybir.dt.float32
BF16 = mybir.dt.bfloat16
ALU = mybir.AluOpType
AF = mybir.ActivationFunctionType
AX = mybir.AxisListType
_bf = ml_dtypes.bfloat16

B, T, D = 8, 4096, 2048
NT = T // 128
NH, NKV, HD, PD = 16, 4, 128, 16
GQ = NH // NKV
NE, NR = 6, 16
NCAT = 16 + NE * NR
SCALE = 1.0 / math.sqrt(HD)
LN_EPS = 1e-5
RMS_EPS = float(np.finfo(np.float32).eps)
NEGBIG = -1.0e30

ENGINES = ("sync", "gpsimd", "tensor", "vector", "scalar")
SEM_NAMES = ("spe", "sve", "sse",
             "sxa", "sxb",
             "soa", "sob",
             "scst", "sw2", "sw3", "sw45")


class _Ins:
    __slots__ = ("ctx", "real")

    def __init__(self, ctx, real):
        self.ctx, self.real = ctx, real

    def inc(self, sem_name, k=1):
        self.ctx.c[sem_name] += k
        if self.real is not None:
            self.real.then_inc(self.ctx.sems[sem_name], k)
        return self


class _Eng:
    def __init__(self, ctx, name):
        object.__setattr__(self, "ctx", ctx)
        object.__setattr__(self, "name", name)
        object.__setattr__(self, "real", None)

    def wait(self, sem_name, value):
        if self.real is not None and value > 0:
            self.real.wait_ge(self.ctx.sems[sem_name], value)

    def __getattr__(self, op):
        ctx = self.ctx
        real = self.real

        def f(*a, **kw):
            ri = getattr(real, op)(*a, **kw) if real is not None else None
            return _Ins(ctx, ri)
        return f


class Ctx:
    def __init__(self, sems, active):
        self.sems = sems
        self.active = active
        self.c = {k: 0 for k in SEM_NAMES}
        self.m = {}
        self.T = _Eng(self, "tensor")
        self.V = _Eng(self, "vector")
        self.S = _Eng(self, "scalar")
        self.Y = _Eng(self, "sync")
        self.G = _Eng(self, "gpsimd")
        self.by_name = {"tensor": self.T, "vector": self.V, "scalar": self.S,
                        "sync": self.Y, "gpsimd": self.G}

    def bind(self, eng):
        object.__setattr__(self.by_name[self.active], "real", eng)

    def unbind(self):
        object.__setattr__(self.by_name[self.active], "real", None)


def _emit_all(blk, ctxs, fn):
    for name in ENGINES:
        c = ctxs[name]

        def body(eng, c=c):
            c.bind(eng)
            fn(c)
            c.unbind()
        getattr(blk, name)(body)


def _barrier(c):
    snap = dict(c.c)
    for e in (c.T, c.V, c.S, c.Y, c.G):
        for sn in SEM_NAMES:
            if snap[sn] > 0:
                e.wait(sn, snap[sn])


def build_nc(nt=NT, dbg=False):
    nc = bass.Bass("TRN2", target_bir_lowering=False)

    def din(name, shape, dt=F32):
        return nc.dram_tensor(name, shape, dt, kind="ExternalInput").ap()

    x_d = din("x", [T, D])
    wq_i = din("wq_img", [128, 16 * D])
    wk_i = din("wk_img", [128, 16 * 512])
    wv_i = din("wv_img", [128, 16 * 512])
    wp_i = din("wp_img", [128, 16 * D])
    wb_i = din("wb_img", [128, 16 * D])
    wr_i = din("wr_img", [128, 16 * NE])
    ac_i = din("acat_img", [128, 16 * NCAT])
    bc_i = din("bcat_img", [NCAT, D])
    bb_i = din("bb_rep", [128, D])
    br_i = din("br_rep", [128, NE])
    qg_i = din("qg_rep", [128, NH])
    id_i = din("ident_bf", [128, 128], BF16)
    trikm_i = din("trikm", [128, 512], BF16)
    ones_i = din("ones_cr", [128, 129], BF16)
    eps_i = din("eps_c", [128, 2])
    rope_i = din("rope_tab", [T, 256])

    out_d = nc.dram_tensor("out", [T, D], F32, kind="ExternalOutput").ap()
    xnT_s = nc.dram_tensor("xnT_s", [16, 128, T], BF16, kind="Internal").ap()
    yT_s = nc.dram_tensor("yT_s", [16, 128, T], BF16, kind="Internal").ap()
    qT_s = nc.dram_tensor("qT_s", [16, 128, T], BF16, kind="Internal").ap()
    kT_s = nc.dram_tensor("kT_s", [128, NKV * T], BF16, kind="Internal").ap()
    v_s = nc.dram_tensor("v_s", [128, NT * 512], BF16, kind="Internal").ap()

    with contextlib.ExitStack() as es:
        sems = {n: es.enter_context(nc.semaphore(n)) for n in SEM_NAMES}
        blk = es.enter_context(nc.Block())
        ctxs = {n: Ctx(sems, n) for n in ENGINES}

        ident = es.enter_context(nc.sbuf_tensor("ident", [128, 128], BF16))
        trikm_sb = es.enter_context(nc.sbuf_tensor("trikm_sb", [128, 512], BF16))
        ones_cr = es.enter_context(nc.sbuf_tensor("ones_sb", [128, 129], BF16))
        epsc = es.enter_context(nc.sbuf_tensor("epsc", [128, 2], F32))
        qg_sb = es.enter_context(nc.sbuf_tensor("qg_sb", [128, NH], F32))
        br_sb = es.enter_context(nc.sbuf_tensor("br_sb", [128, NE], F32))
        w6 = es.enter_context(nc.sbuf_tensor("w6", [128, NT * NE], F32))
        st1 = es.enter_context(nc.sbuf_tensor("st1", [128, 16], F32))
        st2 = es.enter_context(nc.sbuf_tensor("st2", [128, 16], F32))
        st3 = es.enter_context(nc.sbuf_tensor("st3", [128, 16], F32))
        nmean = es.enter_context(nc.sbuf_tensor("nmean", [128, 1], F32))
        nmist = es.enter_context(nc.sbuf_tensor("nmist", [128, 1], F32))

        es2 = contextlib.ExitStack()
        wq_bf = es2.enter_context(nc.sbuf_tensor("wq_bf", [128, 16 * D], BF16))
        wk_bf = es2.enter_context(nc.sbuf_tensor("wk_bf", [128, 16 * 512], BF16))
        wv_bf = es2.enter_context(nc.sbuf_tensor("wv_bf", [128, 16 * 512], BF16))
        wr_bf = es2.enter_context(nc.sbuf_tensor("wr_bf", [128, 16 * NE], BF16))
        xb2 = [es2.enter_context(nc.sbuf_tensor(f"xb2_{j}", [128, D], F32)) for j in range(2)]
        xnbf = [es2.enter_context(nc.sbuf_tensor(f"xnbf{j}", [128, D], BF16)) for j in range(2)]
        xnTb2 = [es2.enter_context(nc.sbuf_tensor(f"xnTb2_{j}", [128, D], BF16)) for j in range(2)]
        ropeb = [es2.enter_context(nc.sbuf_tensor(f"ropeb{j}", [128, 256], F32)) for j in range(2)]
        rtmp = es2.enter_context(nc.sbuf_tensor("rtmp", [128, 512], F32))
        ktm = es2.enter_context(nc.sbuf_tensor("ktm", [128, 512], F32))
        kbf = es2.enter_context(nc.sbuf_tensor("kbf", [128, 512], BF16))
        kTb = [es2.enter_context(nc.sbuf_tensor(f"kTb{j}", [128, 512], BF16)) for j in range(2)]
        vb = [es2.enter_context(nc.sbuf_tensor(f"vb{j}", [128, 512], BF16)) for j in range(2)]
        qbf = es2.enter_context(nc.sbuf_tensor("qbf", [128, D], BF16))
        qsq = es2.enter_context(nc.sbuf_tensor("qsq", [128, D], BF16))
        qTb = [es2.enter_context(nc.sbuf_tensor(f"qTb{j}", [128, D], BF16)) for j in range(2)]
        sc16 = es2.enter_context(nc.sbuf_tensor("sc16", [128, NH], F32))
        rf6 = es2.enter_context(nc.sbuf_tensor("rf6", [128, NE], F32))
        g6a = es2.enter_context(nc.sbuf_tensor("g6a", [128, NE], F32))
        g6b = es2.enter_context(nc.sbuf_tensor("g6b", [128, NE], F32))
        z6 = es2.enter_context(nc.sbuf_tensor("z6", [128, 1], F32))
        m6 = es2.enter_context(nc.sbuf_tensor("m6", [128, 1], F32))
        ps_k = [es2.enter_context(nc.psum_tensor(f"ps_k{j}", [128, 512], F32)) for j in range(2)]
        ps_v = [es2.enter_context(nc.psum_tensor(f"ps_v{j}", [128, 512], F32)) for j in range(2)]
        ps_r = [es2.enter_context(nc.psum_tensor(f"ps_r{j}", [128, 512], F32)) for j in range(2)]
        ps_xt = [es2.enter_context(nc.psum_tensor(f"ps_xt{j}", [128, 512], F32)) for j in range(2)]

        def setup(c):
            c.G.dma_start(out=ident[:], in_=id_i).inc("scst", 16)
            c.G.dma_start(out=trikm_sb[:], in_=trikm_i).inc("scst", 16)
            c.G.dma_start(out=ones_cr[:], in_=ones_i).inc("scst", 16)
            c.G.dma_start(out=qg_sb[:], in_=qg_i).inc("scst", 16)
            c.G.dma_start(out=br_sb[:], in_=br_i).inc("scst", 16)
            c.G.dma_start(out=epsc[:], in_=eps_i).inc("scst", 16)
            c.G.dma_start(out=wk_bf[:], in_=wk_i).inc("sw2", 16)
            c.G.dma_start(out=wv_bf[:], in_=wv_i).inc("sw2", 16)
            c.G.dma_start(out=wr_bf[:], in_=wr_i).inc("sw2", 16)
            for fc in range(4):
                c.G.dma_start(
                    out=wq_bf[:].rearrange("p (ch d) -> p ch d", ch=16)[:, :, fc * 512:(fc + 1) * 512],
                    in_=wq_i.rearrange("p (ch d) -> p ch d", ch=16)[:, :, fc * 512:(fc + 1) * 512],
                ).inc("sw3", 16)

        _emit_all(blk, ctxs, setup)
        CONSTS = 6 * 16
        W2 = 3 * 16

        def bw(c, bank):
            prev = c.m.get(("b", bank))
            if prev is not None:
                c.T.wait(prev[0], prev[1])

        def bank_of(bid):
            return {"k0": ps_k[0], "k1": ps_k[1], "v0": ps_v[0], "v1": ps_v[1],
                    "r0": ps_r[0], "r1": ps_r[1], "x0": ps_xt[0], "x1": ps_xt[1]}[bid]

        def p2_A(c, i):
            par = "sxa" if i % 2 == 0 else "sxb"
            if i >= 2:
                c.Y.wait("sse", c.m[("p2_s_done", i - 2)])
                c.Y.wait("sve", c.m[("p2_qprep", i - 2)])
            c.Y.dma_start(out=xb2[i % 2][:], in_=x_d[i * 128:(i + 1) * 128, :]).inc(par, 16)
            c.Y.dma_start(out=ropeb[i % 2][:], in_=rope_i[i * 128:(i + 1) * 128, :]).inc(par, 16)
            ld = c.c[par]

            c.V.wait(par, ld)
            c.V.reduce_sum(out=nmean[:], in_=xb2[i % 2][:], axis=AX.X, negate=True).inc("sve")
            c.V.wait("sve", c.c["sve"])
            c.V.tensor_scalar(out=nmean[:], in0=nmean[:], scalar1=1.0 / D, scalar2=None,
                              op0=ALU.mult).inc("sve")
            c.S.wait("sve", c.c["sve"])
            c.S.wait(par, ld)
            c.S.activation(out=xnbf[i % 2][:], in_=xb2[i % 2][:], func=AF.Square,
                           bias=nmean[:], scale=1.0, accum_out=st1[:, 15:16])
            c.S.activation(out=st3[:, 14:15], in_=st1[:, 15:16], func=AF.Ln,
                           scale=1.0 / D, bias=epsc[:, 0:1])
            c.S.activation(out=st3[:, 15:16], in_=st3[:, 14:15], func=AF.Exp,
                           scale=-0.5).inc("sse")
            c.V.wait("sse", c.c["sse"])
            c.V.tensor_tensor(out=nmist[:], in0=nmean[:], in1=st3[:, 15:16], op=ALU.mult).inc("sve")
            c.S.wait("sve", c.c["sve"])
            c.S.activation(out=xnbf[i % 2][:], in_=xb2[i % 2][:], func=AF.Identity,
                           scale=st3[:, 15:16], bias=nmist[:]).inc("sse")
            c.m[("p2_s_done", i)] = c.c["sse"]
            c.m[("p2_ln", i)] = c.c["sse"]

        def p2_XTtr(c, i):
            par = "sxa" if i % 2 == 0 else "sxb"
            spar = "soa" if i % 2 == 0 else "sob"
            for g in range(4):
                c.T.wait("sse", c.m[("p2_ln", i)])
                if i == 0 and g == 0:
                    c.T.wait("scst", CONSTS)
                bid = "x%d" % (g % 2)
                bw(c, bid)
                ins = None
                for q in range(4):
                    ch = g * 4 + q
                    ins = c.T.matmul(ps_xt[g % 2][:, q * 128:(q + 1) * 128],
                                     xnbf[i % 2][:, ch * 128:(ch + 1) * 128], ident[:],
                                     start=True, stop=True, skip_group_check=True)
                ins.inc("spe")
                c.S.wait("spe", c.c["spe"])
                if g == 0 and i >= 2:
                    c.S.wait(spar, c.m[("p2_st", i - 2)])
                    c.S.wait("spe", c.m[("p2_qc_done", i - 2)])
                c.S.activation(out=xnTb2[i % 2][:, g * 512:(g + 1) * 512],
                               in_=ps_xt[g % 2][:], func=AF.Identity).inc("sse")
                c.m[("b", bid)] = ("sse", c.c["sse"])
            c.m[("p2_xnT", i)] = c.c["sse"]

            c.G.wait("sse", c.c["sse"])
            c.G.dma_start(out=xnT_s.rearrange("c p t -> p c t")[:, :, i * 128:(i + 1) * 128],
                          in_=xnTb2[i % 2][:].rearrange("p (c t) -> p c t", c=16)).inc(spar, 16)
            c.m[("p2_st", i)] = c.c[spar]

        def p2_KVR(c, i):
            c.T.wait("sw2", W2)
            c.T.wait("sse", c.m[("p2_xnT", i)])
            for bid in ("k%d" % (i % 2), "v%d" % (i % 2), "r%d" % (i % 2)):
                bw(c, bid)
            ins = None
            for ch in range(16):
                xc = xnTb2[i % 2][:, ch * 128:(ch + 1) * 128]
                c.T.matmul(ps_k[i % 2][:], xc, wk_bf[:, ch * 512:(ch + 1) * 512],
                           start=(ch == 0), stop=(ch == 15), skip_group_check=True)
                c.T.matmul(ps_v[i % 2][:], xc, wv_bf[:, ch * 512:(ch + 1) * 512],
                           start=(ch == 0), stop=(ch == 15), skip_group_check=True)
                ins = c.T.matmul(ps_r[i % 2][:, 0:NE], xc, wr_bf[:, ch * NE:(ch + 1) * NE],
                                 start=(ch == 0), stop=(ch == 15), skip_group_check=True)
            ins.inc("spe")
            c.m[("p2_kvr", i)] = c.c["spe"]

        def p2_KP(c, i):
            par = "sxa" if i % 2 == 0 else "sxb"
            spar = "soa" if i % 2 == 0 else "sob"
            kvr = c.m[("p2_kvr", i)]
            c.V.wait("spe", kvr)
            c.V.tensor_copy(out=rf6[:], in_=ps_r[i % 2][:, 0:NE]).inc("sve")
            c.m[("b", "r%d" % (i % 2))] = ("sve", c.c["sve"])
            c.S.wait("spe", kvr)
            if i >= 1:
                c.S.wait("sve", c.m[("p2_kbf", i - 1)])
            c.S.activation(out=ktm[:], in_=ps_k[i % 2][:], func=AF.Identity).inc("sse")
            c.m[("b", "k%d" % (i % 2))] = ("sse", c.c["sse"])
            if i >= 2:
                c.S.wait(spar, c.m[("p2_st_kv", i - 2)])
            c.S.activation(out=vb[i % 2][:], in_=ps_v[i % 2][:], func=AF.Identity).inc("sse")
            c.m[("b", "v%d" % (i % 2))] = ("sse", c.c["sse"])
            c.G.wait("sse", c.c["sse"])
            c.G.dma_start(out=v_s[:, i * 512:(i + 1) * 512], in_=vb[i % 2][:]).inc(spar, 16)

            c.S.activation(out=kbf[:], in_=ktm[:], func=AF.Square).inc("sse")
            c.V.wait("sse", c.c["sse"])
            c.V.tensor_reduce(out=st1[:, 0:NKV],
                              in_=kbf[:].rearrange("p (h d) -> p h d", h=NKV),
                              axis=AX.X, op=ALU.add).inc("sve")
            c.S.wait("sve", c.c["sve"])
            c.S.activation(out=st3[:, 0:NKV], in_=st1[:, 0:NKV], func=AF.Ln,
                           scale=1.0 / HD, bias=epsc[:, 1:2])
            c.S.activation(out=st3[:, 0:NKV], in_=st3[:, 0:NKV], func=AF.Exp,
                           scale=-0.5).inc("sse")
            c.V.wait("sse", c.c["sse"])
            for h in range(NKV):
                c.V.tensor_scalar(out=ktm[:, h * 128:(h + 1) * 128],
                                  in0=ktm[:, h * 128:(h + 1) * 128],
                                  scalar1=st3[:, h:h + 1], scalar2=None, op0=ALU.mult)
            kv_ = ktm[:].rearrange("p (h d) -> p h d", h=NKV)
            co = ropeb[i % 2][:, 0:32].rearrange("p (h d) -> p h d", h=NKV)
            si = ropeb[i % 2][:, 128:160].rearrange("p (h d) -> p h d", h=NKV)
            x1, x2 = kv_[:, :, 0:8], kv_[:, :, 8:16]
            r_ = rtmp[:].rearrange("p (h d) -> p h d", h=64)
            r1, r2, r3, r4 = r_[:, 0:4], r_[:, 4:8], r_[:, 8:12], r_[:, 12:16]
            c.V.tensor_tensor(out=r1, in0=x1, in1=co, op=ALU.mult)
            c.V.tensor_tensor(out=r2, in0=x1, in1=si, op=ALU.mult)
            c.V.tensor_tensor(out=r3, in0=x2, in1=si, op=ALU.mult)
            c.V.tensor_tensor(out=x1, in0=r1, in1=r3, op=ALU.add)
            c.V.tensor_tensor(out=r4, in0=x2, in1=co, op=ALU.mult)
            c.V.tensor_tensor(out=x2, in0=r2, in1=r4, op=ALU.subtract).inc("sve")
            c.V.wait("sve", c.c["sve"])
            c.V.tensor_copy(out=kbf[:], in_=ktm[:]).inc("sve")
            c.m[("p2_kbf", i)] = c.c["sve"]

            def vs(ins):
                ins.inc("sve")
                c.V.wait("sve", c.c["sve"])
            vs(c.V.tensor_tensor(out=rf6[:], in0=rf6[:], in1=br_sb[:], op=ALU.add))
            c.V.reduce_max(out=m6[:], in_=rf6[:], axis=AX.X, negate=True).inc("sve")
            c.S.wait("sve", c.c["sve"])
            c.S.activation(out=g6a[:], in_=rf6[:], func=AF.Exp, bias=m6[:],
                           accum_out=z6[:]).inc("sse")
            c.V.wait("sse", c.c["sse"])
            vs(c.V.reciprocal(out=m6[:], in_=z6[:]))
            vs(c.V.tensor_scalar(out=g6a[:], in0=g6a[:], scalar1=m6[:], scalar2=None,
                                 op0=ALU.mult))
            vs(c.V.reduce_max(out=m6[:], in_=g6a[:], axis=AX.X))
            vs(c.V.tensor_scalar(out=g6b[:], in0=g6a[:], scalar1=m6[:], scalar2=None,
                                 op0=ALU.is_ge))
            vs(c.V.scalar_tensor_tensor(out=g6b[:], in0=g6b[:], scalar=-2.0, in1=g6a[:],
                                        op0=ALU.mult, op1=ALU.add))
            vs(c.V.reduce_max(out=m6[:], in_=g6b[:], axis=AX.X))
            vs(c.V.tensor_scalar(out=g6b[:], in0=g6a[:], scalar1=m6[:], scalar2=None,
                                 op0=ALU.is_ge))
            c.V.tensor_tensor(out=w6[:, i * NE:(i + 1) * NE], in0=g6a[:], in1=g6b[:],
                              op=ALU.mult).inc("sve")
            c.m[("p2_v_done", i)] = c.c["sve"]

        def p2_QC(c, i):
            par = i % 2
            qb = ["k%d" % (1 - par), "v%d" % (1 - par), "r%d" % (1 - par), "r%d" % par]
            if i == 0:
                c.T.wait("sw3", 64)
            for pair in range(2):
                f0, f1 = 2 * pair, 2 * pair + 1
                bw(c, qb[f0])
                bw(c, qb[f1])
                ins = None
                for ch in range(16):
                    xc = xnTb2[i % 2][:, ch * 128:(ch + 1) * 128]
                    c.T.matmul(bank_of(qb[f0])[:], xc,
                               wq_bf[:, ch * D + f0 * 512: ch * D + (f0 + 1) * 512],
                               start=(ch == 0), stop=(ch == 15), skip_group_check=True)
                    ins = c.T.matmul(bank_of(qb[f1])[:], xc,
                                     wq_bf[:, ch * D + f1 * 512: ch * D + (f1 + 1) * 512],
                                     start=(ch == 0), stop=(ch == 15), skip_group_check=True)
                ins.inc("spe")
                c.S.wait("spe", c.c["spe"])
                if pair == 0 and i >= 1:
                    c.S.wait("spe", c.m[("p2_qtr", i - 1)])
                c.S.activation(out=qbf[:, f0 * 512:(f0 + 1) * 512],
                               in_=bank_of(qb[f0])[:], func=AF.Identity).inc("sse")
                c.m[("b", qb[f0])] = ("sse", c.c["sse"])
                c.S.activation(out=qbf[:, f1 * 512:(f1 + 1) * 512],
                               in_=bank_of(qb[f1])[:], func=AF.Identity).inc("sse")
                c.m[("b", qb[f1])] = ("sse", c.c["sse"])
                c.S.wait("sve", c.c["sve"])
                c.S.activation(out=qsq[:, pair * 1024:(pair + 1) * 1024],
                               in_=qbf[:, pair * 1024:(pair + 1) * 1024],
                               func=AF.Square).inc("sse")
                c.V.wait("sse", c.c["sse"])
                c.V.tensor_reduce(
                    out=st2[:, pair * 8:(pair + 1) * 8],
                    in_=qsq[:, pair * 1024:(pair + 1) * 1024].rearrange(
                        "p (h d) -> p h d", h=8),
                    axis=AX.X, op=ALU.add).inc("sve")
            c.m[("p2_qc_done", i)] = c.c["spe"]
            c.m[("p2_qstat", i)] = c.c["sve"]

        def p2_KT(c, i):
            spar = "soa" if i % 2 == 0 else "sob"
            c.T.wait("sve", c.m[("p2_kbf", i)])
            bw(c, "x0")
            ins = None
            for h in range(NKV):
                ins = c.T.matmul(ps_xt[0][:, h * 128:(h + 1) * 128],
                                 kbf[:, h * 128:(h + 1) * 128], ident[:],
                                 start=True, stop=True, skip_group_check=True)
            ins.inc("spe")
            c.m[("p2_ktr", i)] = c.c["spe"]
            c.S.wait("spe", c.c["spe"])
            if i >= 2:
                c.S.wait(spar, c.m[("p2_st_kv", i - 2)])
            c.S.activation(out=kTb[i % 2][:], in_=ps_xt[0][:], func=AF.Identity).inc("sse")
            c.m[("b", "x0")] = ("sse", c.c["sse"])
            c.G.wait("sse", c.c["sse"])
            c.G.dma_start(
                out=kT_s.rearrange("p (h t) -> p h t", h=NKV)[:, :, i * 128:(i + 1) * 128],
                in_=kTb[i % 2][:].rearrange("p (h t) -> p h t", h=NKV)).inc(spar, 16)
            c.m[("p2_st_kv", i)] = c.c[spar]

        def p2_QP(c, i):
            par = "sxa" if i % 2 == 0 else "sxb"
            c.S.wait("sve", c.m[("p2_qstat", i)])
            c.S.activation(out=st3[:], in_=st2[:], func=AF.Ln,
                           scale=1.0 / HD, bias=epsc[:, 1:2])
            c.S.activation(out=st3[:], in_=st3[:], func=AF.Exp, scale=-0.5).inc("sse")
            c.V.wait("sse", c.c["sse"])
            c.V.tensor_tensor(out=sc16[:], in0=st3[:], in1=qg_sb[:], op=ALU.mult).inc("sve")
            c.V.wait("sve", c.c["sve"])
            for h in range(NH):
                c.V.tensor_scalar(out=qbf[:, h * 128:(h + 1) * 128],
                                  in0=qbf[:, h * 128:(h + 1) * 128],
                                  scalar1=sc16[:, h:h + 1], scalar2=None, op0=ALU.mult)
            qv = qbf[:].rearrange("p (h d) -> p h d", h=NH)
            co = ropeb[i % 2][:, 0:128].rearrange("p (h d) -> p h d", h=NH)
            si = ropeb[i % 2][:, 128:256].rearrange("p (h d) -> p h d", h=NH)
            x1, x2 = qv[:, :, 0:8], qv[:, :, 8:16]
            r_ = rtmp[:].rearrange("p (h d) -> p h d", h=64)
            r1, r2, r3, r4 = r_[:, 0:16], r_[:, 16:32], r_[:, 32:48], r_[:, 48:64]
            c.V.tensor_tensor(out=r1, in0=x1, in1=co, op=ALU.mult)
            c.V.tensor_tensor(out=r2, in0=x1, in1=si, op=ALU.mult)
            c.V.tensor_tensor(out=r3, in0=x2, in1=si, op=ALU.mult)
            c.V.tensor_tensor(out=x1, in0=r1, in1=r3, op=ALU.add)
            c.V.tensor_tensor(out=r4, in0=x2, in1=co, op=ALU.mult)
            c.V.tensor_tensor(out=x2, in0=r2, in1=r4, op=ALU.subtract).inc("sve")
            c.m[("p2_qprep", i)] = c.c["sve"]

        def p2_QT(c, i):
            spar = "soa" if i % 2 == 0 else "sob"
            c.T.wait("sve", c.m[("p2_qprep", i)])
            for g in range(4):
                bid = "x%d" % ((g + 1) % 2)
                bw(c, bid)
                ins = None
                for hh in range(4):
                    h = g * 4 + hh
                    ins = c.T.matmul(ps_xt[(g + 1) % 2][:, hh * 128:(hh + 1) * 128],
                                     qbf[:, h * 128:(h + 1) * 128], ident[:],
                                     start=True, stop=True, skip_group_check=True)
                ins.inc("spe")
                c.S.wait("spe", c.c["spe"])
                if g == 0 and i >= 2:
                    c.S.wait(spar, c.m[("p2_st_q", i - 2)])
                c.S.activation(out=qTb[i % 2][:, g * 512:(g + 1) * 512],
                               in_=ps_xt[(g + 1) % 2][:], func=AF.Identity).inc("sse")
                c.m[("b", bid)] = ("sse", c.c["sse"])
            c.m[("p2_qtr", i)] = c.c["spe"]
            c.G.wait("sse", c.c["sse"])
            c.G.dma_start(out=qT_s.rearrange("c p t -> p c t")[:, :, i * 128:(i + 1) * 128],
                          in_=qTb[i % 2][:].rearrange("p (c t) -> p c t", c=16)).inc(spar, 16)
            c.m[("p2_st_q", i)] = c.c[spar]

        def p2(c):
            p2_A(c, 0)
            p2_XTtr(c, 0)
            p2_KVR(c, 0)
            for i in range(nt):
                p2_KP(c, i)
                if i + 1 < nt:
                    p2_A(c, i + 1)
                p2_QC(c, i)
                p2_KT(c, i)
                if i + 1 < nt:
                    p2_XTtr(c, i + 1)
                    p2_KVR(c, i + 1)
                p2_QP(c, i)
                p2_QT(c, i)
            _barrier(c)

        _emit_all(blk, ctxs, p2)
        es2.close()

        es34 = contextlib.ExitStack()
        wb_bf = es34.enter_context(nc.sbuf_tensor("wb_bf", [128, 16 * D], BF16))
        ac_bf = es34.enter_context(nc.sbuf_tensor("ac_bf", [128, 16 * NCAT], BF16))
        bc_bf = es34.enter_context(nc.sbuf_tensor("bc_bf", [NCAT, D], BF16))
        bb_sb = es34.enter_context(nc.sbuf_tensor("bb_sb", [128, D], F32))

        es3 = contextlib.ExitStack()
        kcache = es3.enter_context(nc.sbuf_tensor("kcache", [128, NKV * T], BF16))
        vcache = es3.enter_context(nc.sbuf_tensor("vcache", [128, NT * 512], BF16))
        qTb3 = [es3.enter_context(nc.sbuf_tensor(f"qTb3_{j}", [128, D], BF16)) for j in range(2)]
        pT = es3.enter_context(nc.sbuf_tensor("pT", [128, 12 * 512], BF16))
        yTb = [es3.enter_context(nc.sbuf_tensor(f"yTb{j}", [128, D], BF16)) for j in range(2)]
        nb1 = es3.enter_context(nc.sbuf_tensor("nb1", [128, 1], F32))
        zrow = es3.enter_context(nc.sbuf_tensor("zrow", [1, 512], F32))
        zrec = es3.enter_context(nc.sbuf_tensor("zrec", [1, D], BF16))
        rzbc = es3.enter_context(nc.sbuf_tensor("rzbc", [128, 512], F32))
        ps_d = [es3.enter_context(nc.psum_tensor(f"ps_d{j}", [128, 1024], F32)) for j in range(2)]
        ps_yu = [es3.enter_context(nc.psum_tensor(f"ps_yu{j}", [128, 512], F32)) for j in range(2)]
        ps_z = [es3.enter_context(nc.psum_tensor(f"ps_z{j}", [128, 512], F32)) for j in range(2)]

        def _bkwait(c, eng, bid):
            prev = c.m.get(("p3bk", bid))
            if prev is not None:
                eng.wait(prev[0], prev[1])

        def p3_attn_kv(c, i, kv, pre_cb=None):
            nst = i + 1
            NSL = 12
            npair = (nst + 1) // 2

            def emit_pair(p):
                s0 = 2 * p
                s1 = min(2 * p + 1, nst - 1)
                w = s1 - s0 + 1
                bk = ps_d[p % 2]
                _bkwait(c, c.T, ("d", p % 2))
                if p == 0 and kv == 0:
                    par = "sxa" if i % 2 == 0 else "sxb"
                    c.T.wait(par, c.m[("p3_ld", i)])
                ins = None
                for s in range(s0, s1 + 1):
                    last = (s == nst - 1)
                    ins = c.T.matmul(bk[:, (s - s0) * 512:(s - s0 + 1) * 512],
                                     kcache[:, kv * T + s * 128: kv * T + (s + 1) * 128],
                                     qTb3[i % 2][:, kv * 512:(kv + 1) * 512],
                                     start=True, stop=not last, skip_group_check=True)
                    if last:
                        ins = c.T.matmul(bk[:, (s - s0) * 512:(s - s0 + 1) * 512],
                                         ident[:], trikm_sb[:],
                                         start=False, stop=True, skip_group_check=True)
                ins.inc("spe")
                c.S.wait("spe", c.c["spe"])
                prev = c.m.get(("p3_pt", s0 % NSL))
                if prev is not None:
                    c.S.wait("spe", prev)
                c.S.activation(out=pT[:, (s0 % NSL) * 512:((s0 % NSL) + w) * 512],
                               in_=bk[:, 0:w * 512],
                               func=AF.Exp, bias=nb1[:], scale=SCALE).inc("sse")
                c.m[("p3bk", ("d", p % 2))] = ("sse", c.c["sse"])
                c.m[("p3_expp", p)] = c.c["sse"]

            def emit_pv(s, need_wait):
                last = (s == nst - 1)
                if need_wait:
                    c.T.wait("sse", c.m[("p3_expp", s // 2)])
                if s == 0:
                    _bkwait(c, c.T, ("yu", kv % 2))
                c.T.matmul(ps_yu[kv % 2][:],
                           vcache[:, s * 512 + kv * 128: s * 512 + (kv + 1) * 128],
                           pT[:, (s % NSL) * 512:(s % NSL + 1) * 512],
                           start=(s == 0), stop=last, skip_group_check=True).inc("spe")

            def emit_zbatch(s0, s1):
                c.T.wait("sse", c.m[("p3_expp", (s1 - 1) // 2)])
                if s0 == 0:
                    _bkwait(c, c.T, ("z", kv % 2))
                ins = None
                for s in range(s0, s1):
                    ins = c.T.matmul(ps_z[kv % 2][0:1, :], ones_cr[:, 0:1],
                                     pT[:, (s % NSL) * 512:(s % NSL + 1) * 512],
                                     start=(s == 0), stop=(s == nst - 1),
                                     skip_group_check=True)
                ins.inc("spe")
                for s in range(s0, s1):
                    c.m[("p3_pt", s % NSL)] = c.c["spe"]

            emit_pair(0)
            if npair > 1:
                emit_pair(1)
            zb = 0
            for p in range(npair):
                if p + 2 < npair:
                    emit_pair(p + 2)
                if p == 0 and pre_cb is not None:
                    pre_cb[0]()
                if p == min(4, npair - 1) and pre_cb is not None:
                    pre_cb[1]()
                s0 = 2 * p
                s1 = min(2 * p + 1, nst - 1)
                emit_pv(s0, True)
                if s1 > s0:
                    emit_pv(s1, False)
                if s1 - zb >= 6 or s1 == nst - 1:
                    emit_zbatch(zb, s1 + 1)
                    zb = s1 + 1
            c.m[("p3_pvz", kv)] = c.c["spe"]

        def p3_epi_a(c, i, kv):
            c.V.wait("spe", c.m[("p3_pvz", kv)])
            c.V.tensor_copy(out=zrow[0:1, :], in_=ps_z[kv % 2][0:1, :]).inc("sve")
            c.m[("p3bk", ("z", kv % 2))] = ("sve", c.c["sve"])
            c.V.wait("sve", c.c["sve"])
            c.V.reciprocal(out=zrow[0:1, :], in_=zrow[0:1, :]).inc("sve")
            c.V.wait("sve", c.c["sve"])
            c.V.tensor_copy(out=zrec[0:1, kv * 512:(kv + 1) * 512], in_=zrow[0:1, :]).inc("sve")
            c.m[("p3_zr", kv)] = c.c["sve"]

        def p3_epi_b(c, i, kv):
            c.T.wait("sve", c.m[("p3_zr", kv)])
            c.T.matmul(ps_z[kv % 2][:], ones_cr[0:1, 1:129], zrec[0:1, kv * 512:(kv + 1) * 512],
                       start=True, stop=True, skip_group_check=True).inc("spe")
            c.V.wait("spe", c.c["spe"])
            c.V.tensor_copy(out=rzbc[:], in_=ps_z[kv % 2][:]).inc("sve")
            c.m[("p3bk", ("z", kv % 2))] = ("sve", c.c["sve"])
            c.V.wait("sve", c.c["sve"])
            if kv == 0 and i >= 2:
                spar_prev = "soa" if i % 2 == 0 else "sob"
                c.V.wait(spar_prev, c.m[("p3_st", i - 2)])
            c.V.scalar_tensor_tensor(out=yTb[i % 2][:, kv * 512:(kv + 1) * 512],
                                     in0=ps_yu[kv % 2][:],
                                     scalar=1.0, in1=rzbc[:],
                                     op0=ALU.mult, op1=ALU.mult).inc("sve")
            c.m[("p3bk", ("yu", kv % 2))] = ("sve", c.c["sve"])

        def p3(c):
            c.V.wait("scst", CONSTS)
            c.V.reduce_max(out=nb1[:], in_=qg_sb[:], axis=AX.X).inc("sve")
            c.V.wait("sve", c.c["sve"])
            c.V.tensor_scalar(out=nb1[:], in0=nb1[:], scalar1=-math.sqrt(float(HD)),
                              scalar2=40.0, op0=ALU.mult, op1=ALU.add).inc("sve")
            c.V.wait("sve", c.c["sve"])
            for i in range(nt):
                par = "sxa" if i % 2 == 0 else "sxb"
                spar = "soa" if i % 2 == 0 else "sob"
                if i >= 2:
                    c.Y.wait("spe", c.m[("p3_pe_done", i - 2)])
                c.Y.dma_start(out=qTb3[i % 2][:].rearrange("p (c t) -> p c t", c=16),
                              in_=qT_s.rearrange("c p t -> p c t")[:, :, i * 128:(i + 1) * 128]
                              ).inc(par, 16)
                c.Y.dma_start(
                    out=kcache[:].rearrange("p (h t) -> p h t", h=NKV)[:, :, i * 128:(i + 1) * 128],
                    in_=kT_s.rearrange("p (h t) -> p h t", h=NKV)[:, :, i * 128:(i + 1) * 128]
                ).inc(par, 16)
                c.Y.dma_start(out=vcache[:, i * 512:(i + 1) * 512],
                              in_=v_s[:, i * 512:(i + 1) * 512]).inc(par, 16)
                c.m[("p3_ld", i)] = c.c[par]

                if i >= 4 and (i - 4) % 3 == 0 and (i - 4) // 3 < 8:
                    j = (i - 4) // 3
                    fc, hf = j // 2, j % 2
                    c.G.dma_start(
                        out=wb_bf[:].rearrange("p (ch d) -> p ch d", ch=16)[
                            :, hf * 8:(hf + 1) * 8, fc * 512:(fc + 1) * 512],
                        in_=wb_i.rearrange("p (ch d) -> p ch d", ch=16)[
                            :, hf * 8:(hf + 1) * 8, fc * 512:(fc + 1) * 512],
                    ).inc("sw45", 16)
                elif i == 26:
                    c.G.dma_start(out=ac_bf[:], in_=ac_i).inc("sw45", 16)
                elif i == 27:
                    c.G.dma_start(out=bc_bf[:], in_=bc_i).inc("sw45", 16)
                elif i == 28:
                    c.G.dma_start(out=bb_sb[:], in_=bb_i).inc("sw45", 16)

                for kv in range(NKV):
                    if kv == 0:
                        p3_attn_kv(c, i, kv)
                    else:
                        p3_attn_kv(c, i, kv,
                                   pre_cb=(lambda kk=kv - 1: p3_epi_a(c, i, kk),
                                           lambda kk=kv - 1: p3_epi_b(c, i, kk)))
                p3_epi_a(c, i, NKV - 1)
                p3_epi_b(c, i, NKV - 1)
                c.m[("p3_pe_done", i)] = c.c["spe"]
                c.G.wait("sve", c.c["sve"])
                c.G.dma_start(out=yT_s.rearrange("c p t -> p c t")[:, :, i * 128:(i + 1) * 128],
                              in_=yTb[i % 2][:].rearrange("p (c t) -> p c t", c=16)).inc(spar, 16)
                c.m[("p3_st", i)] = c.c[spar]
            _barrier(c)

        _emit_all(blk, ctxs, p3)
        es3.close()

        es4 = contextlib.ExitStack()
        wp_bf = es4.enter_context(nc.sbuf_tensor("wp_bf", [128, 16 * D], BF16))
        xnTb4 = [es4.enter_context(nc.sbuf_tensor(f"xnTb4_{j}", [128, D], BF16)) for j in range(2)]
        yTb4 = [es4.enter_context(nc.sbuf_tensor(f"yTb4_{j}", [128, D], BF16)) for j in range(2)]
        xb4 = [es4.enter_context(nc.sbuf_tensor(f"xb4_{j}", [128, D], F32)) for j in range(2)]
        gbuf = es4.enter_context(nc.sbuf_tensor("gbuf", [128, D], F32))
        midf = es4.enter_context(nc.sbuf_tensor("midf", [128, NCAT], F32))
        hwbf = es4.enter_context(nc.sbuf_tensor("hwbf", [128, NCAT], BF16))
        hwT = es4.enter_context(nc.sbuf_tensor("hwT", [128, 128], BF16))
        ps_f = [es4.enter_context(nc.psum_tensor(f"ps_f{j}", [128, 512], F32)) for j in range(4)]
        ps_m = es4.enter_context(nc.psum_tensor("ps_m", [128, 512], F32))
        ps_h = es4.enter_context(nc.psum_tensor("ps_h", [128, 512], F32))
        ps_w = [es4.enter_context(nc.psum_tensor(f"ps_w{j}", [128, 512], F32)) for j in range(2)]

        W45 = 11 * 16

        def p45_setup(c):
            for fc in range(4):
                c.G.dma_start(
                    out=wp_bf[:].rearrange("p (ch d) -> p ch d", ch=16)[:, :, fc * 512:(fc + 1) * 512],
                    in_=wp_i.rearrange("p (ch d) -> p ch d", ch=16)[:, :, fc * 512:(fc + 1) * 512],
                ).inc("sw45", 16)

        def p45_tile(c, i):
            par = "sxa" if i % 2 == 0 else "sxb"
            spar = "soa" if i % 2 == 0 else "sob"
            if i >= 2:
                c.Y.wait("spe", c.m[("p45_pe_done", i - 2)])
                c.Y.wait("sve", c.m[("p45_v_done", i - 2)])
                c.Y.wait(spar, c.m[("p45_st", i - 2)])
            c.Y.dma_start(out=xnTb4[i % 2][:].rearrange("p (c t) -> p c t", c=16),
                          in_=xnT_s.rearrange("c p t -> p c t")[:, :, i * 128:(i + 1) * 128]
                          ).inc(par, 16)
            c.Y.dma_start(out=yTb4[i % 2][:].rearrange("p (c t) -> p c t", c=16),
                          in_=yT_s.rearrange("c p t -> p c t")[:, :, i * 128:(i + 1) * 128]
                          ).inc(par, 16)
            c.Y.dma_start(out=xb4[i % 2][:], in_=x_d[i * 128:(i + 1) * 128, :]).inc(par, 16)
            ld = c.c[par]

            c.T.wait(par, ld)
            if i == 0:
                c.T.wait("sw45", 9 * 16)
            if i >= 1:
                c.T.wait("sve", c.m[("p45_sgev", i - 1, 2)])
                for fc in range(4):
                    c.T.wait("sve", c.m[("p45_gev", i - 1, fc)])
            ins = None
            for ch in range(16):
                ins = c.T.matmul(ps_m[:, 0:NCAT], xnTb4[i % 2][:, ch * 128:(ch + 1) * 128],
                                 ac_bf[:, ch * NCAT:(ch + 1) * NCAT],
                                 start=(ch == 0), stop=(ch == 15), skip_group_check=True)
                for fc in range(4):
                    c.T.matmul(ps_f[fc][:], xnTb4[i % 2][:, ch * 128:(ch + 1) * 128],
                               wb_bf[:, ch * D + fc * 512: ch * D + (fc + 1) * 512],
                               start=(ch == 0), stop=False, skip_group_check=True)
            ins.inc("spe")
            mid_mark = c.c["spe"]

            c.V.wait("spe", mid_mark)
            c.V.tensor_copy(out=midf[:], in_=ps_m[:, 0:NCAT]).inc("sve")
            c.m[("p45_midev", i)] = c.c["sve"]
            c.V.wait("sve", c.c["sve"])
            for e in range(NE):
                c.V.tensor_scalar(out=midf[:, 16 + e * NR:16 + (e + 1) * NR],
                                  in0=midf[:, 16 + e * NR:16 + (e + 1) * NR],
                                  scalar1=w6[:, i * NE + e:i * NE + e + 1],
                                  scalar2=None, op0=ALU.mult).inc("sve")
            c.V.wait("sve", c.c["sve"])
            c.V.tensor_copy(out=hwbf[:], in_=midf[:]).inc("sve")
            hw_mark = c.c["sve"]
            c.T.wait("sve", hw_mark)
            if i >= 1:
                c.T.wait("sve", c.m[("p45_sgev", i - 1, 3)])
            c.T.matmul(ps_h[0:NCAT, 0:128], hwbf[:, 0:NCAT], ident[:],
                       start=True, stop=True, skip_group_check=True).inc("spe")
            c.V.wait("spe", c.c["spe"])
            c.V.tensor_copy(out=hwT[0:NCAT, :], in_=ps_h[0:NCAT, 0:128]).inc("sve")
            c.m[("p45_hwev", i)] = c.c["sve"]

            if i == 0:
                c.T.wait("sw45", W45 + 32)
            if i >= 1:
                c.T.wait("sve", c.m[("p45_sgev", i - 1, 0)])
                c.T.wait("sve", c.m[("p45_sgev", i - 1, 1)])
            ins = None
            for ch in range(16):
                for fc in (0, 1):
                    ins = c.T.matmul(ps_w[fc][:], yTb4[i % 2][:, ch * 128:(ch + 1) * 128],
                                     wp_bf[:, ch * D + fc * 512: ch * D + (fc + 1) * 512],
                                     start=(ch == 0), stop=(ch == 15), skip_group_check=True)
            ins.inc("spe")
            wp0_mark = c.c["spe"]

            c.T.wait("sve", c.m[("p45_hwev", i)])
            if i == 0:
                c.T.wait("sw45", 10 * 16)
            ins = None
            for fc in range(4):
                ins = c.T.matmul(ps_f[fc][:], hwT[0:NCAT, :],
                                 bc_bf[0:NCAT, fc * 512:(fc + 1) * 512],
                                 start=False, stop=True, skip_group_check=True)
            ins.inc("spe")
            ffn_mark = c.c["spe"]

            if i == 0:
                c.T.wait("sw45", W45 + 64)
            c.T.wait("sve", c.m[("p45_hwev", i)])
            ins = None
            for ch in range(16):
                for fc in (2, 3):
                    bank = ps_m if fc == 2 else ps_h
                    ins = c.T.matmul(bank[:], yTb4[i % 2][:, ch * 128:(ch + 1) * 128],
                                     wp_bf[:, ch * D + fc * 512: ch * D + (fc + 1) * 512],
                                     start=(ch == 0), stop=(ch == 15), skip_group_check=True)
            ins.inc("spe")
            wp1_mark = c.c["spe"]

            c.V.wait("spe", ffn_mark)
            if i == 0:
                c.V.wait("sw45", W45)
            silu_marks = {}

            def bias_silu(fc):
                sl = slice(fc * 512, (fc + 1) * 512)
                c.V.tensor_tensor(out=gbuf[:, sl], in0=ps_f[fc][:],
                                  in1=bb_sb[:, sl], op=ALU.add).inc("sve")
                c.m[("p45_gev", i, fc)] = c.c["sve"]
                c.S.wait("sve", c.c["sve"])
                c.S.activation(out=gbuf[:, sl], in_=gbuf[:, sl],
                               func=AF.Silu).inc("sse")
                silu_marks[fc] = c.c["sse"]

            for fc in range(4):
                bias_silu(fc)
            for fc in range(4):
                c.V.wait("spe", wp0_mark if fc < 2 else wp1_mark)
                c.V.wait("sse", silu_marks[fc])
                src = (ps_w[0], ps_w[1], ps_m, ps_h)[fc]
                sl = slice(fc * 512, (fc + 1) * 512)
                c.V.scalar_tensor_tensor(out=gbuf[:, sl], in0=src[:, 0:512],
                                         scalar=1.0, in1=gbuf[:, sl],
                                         op0=ALU.mult, op1=ALU.mult)
                c.V.tensor_tensor(out=xb4[i % 2][:, sl], in0=gbuf[:, sl],
                                  in1=xb4[i % 2][:, sl], op=ALU.add).inc("sve")
                c.m[("p45_sgev", i, fc)] = c.c["sve"]
            c.m[("p45_v_done", i)] = c.c["sve"]
            c.m[("p45_pe_done", i)] = c.c["spe"]

            c.G.wait("sve", c.c["sve"])
            c.G.dma_start(out=out_d[i * 128:(i + 1) * 128, :], in_=xb4[i % 2][:]).inc(spar, 16)
            c.m[("p45_st", i)] = c.c[spar]

        def p45(c):
            p45_setup(c)
            for i in range(nt):
                p45_tile(c, i)
            c.G.wait("soa", c.c["soa"])
            c.G.wait("sob", c.c["sob"])

        _emit_all(blk, ctxs, p45)
        es4.close()
        es34.close()

    return nc


_NC_CACHE = {}


def _get_nc():
    if "nc" not in _NC_CACHE:
        _NC_CACHE["nc"] = build_nc()
    return _NC_CACHE["nc"]


def _img(mat):
    n = mat.shape[1]
    return np.ascontiguousarray(
        mat.reshape(16, 128, n).transpose(1, 0, 2).reshape(128, 16 * n).astype(np.float32))


def prepare_shared(L0_attn_wq, L0_attn_wk, L0_attn_wv, L0_attn_wp, L0_attn_qg,
                   L0_ffn_W_base, L0_ffn_b_base, L0_ffn_A_s, L0_ffn_B_s, L0_ffn_A_r,
                   L0_ffn_B_r, L0_ffn_W_router, L0_ffn_b_router, **_ignore):
    wq_img = _img(np.asarray(L0_attn_wq, np.float32).T)
    wk_img = _img(np.asarray(L0_attn_wk, np.float32).T)
    wv_img = _img(np.asarray(L0_attn_wv, np.float32).T)
    wp_img = _img(np.asarray(L0_attn_wp, np.float32).T)
    wb_img = _img(np.asarray(L0_ffn_W_base, np.float32))
    wr_img = _img(np.asarray(L0_ffn_W_router, np.float32))
    A_r = np.asarray(L0_ffn_A_r, np.float32)
    acat = np.concatenate([np.asarray(L0_ffn_A_s, np.float32),
                           A_r.transpose(1, 0, 2).reshape(D, NE * NR)], axis=1)
    acat_img = _img(acat)
    bcat_img = np.ascontiguousarray(np.concatenate(
        [np.asarray(L0_ffn_B_s, np.float32),
         np.asarray(L0_ffn_B_r, np.float32).reshape(NE * NR, D)], axis=0))
    bb_rep = np.ascontiguousarray(np.tile(np.asarray(L0_ffn_b_base, np.float32)[None, :], (128, 1)))
    br_rep = np.ascontiguousarray(np.tile(np.asarray(L0_ffn_b_router, np.float32)[None, :], (128, 1)))
    qg_rep = np.ascontiguousarray(np.tile(np.asarray(L0_attn_qg, np.float32)[None, :], (128, 1)))

    ident_bf = np.eye(128, dtype=_bf)
    trikm1 = np.where(np.arange(128)[:, None] <= np.arange(128)[None, :],
                      np.float32(0.0), np.float32(NEGBIG)).astype(np.float32)
    trikm = np.ascontiguousarray(np.tile(trikm1, (1, 4)).astype(_bf))
    ones_cr = np.ones((128, 129), dtype=_bf)
    eps_c = np.ascontiguousarray(np.tile(np.array([[LN_EPS, RMS_EPS]], np.float32), (128, 1)))
    t_ = np.arange(T, dtype=np.float32)
    inv = (1.0 / (10000.0 ** (np.arange(0, PD, 2, dtype=np.float32) / PD))).astype(np.float32)
    f_ = t_[:, None] * inv[None, :]
    rope_tab = np.ascontiguousarray(np.concatenate(
        [np.tile(np.cos(f_).astype(np.float32), (1, 16)),
         np.tile(np.sin(f_).astype(np.float32), (1, 16))], axis=1))

    return dict(wq_img=wq_img, wk_img=wk_img, wv_img=wv_img, wp_img=wp_img,
                wb_img=wb_img, wr_img=wr_img, acat_img=acat_img, bcat_img=bcat_img,
                bb_rep=bb_rep, br_rep=br_rep, qg_rep=qg_rep, ident_bf=ident_bf,
                trikm=trikm, ones_cr=ones_cr, eps_c=eps_c, rope_tab=rope_tab)


def kernel(**inputs):
    x = np.asarray(inputs["L0_attn_x"], dtype=np.float32)
    shared = prepare_shared(**inputs)
    in_maps = [dict(shared, x=np.ascontiguousarray(x[b])) for b in range(B)]
    nc = _get_nc()
    res = run_bass_kernel_spmd(nc, in_maps, core_ids=list(range(B)))
    return np.stack([np.asarray(res.results[b]["out"], dtype=np.float32) for b in range(B)],
                    axis=0)


# version 21
# speedup vs baseline: 1.0889x; 1.0000x over previous
import contextlib
import math

import numpy as np
import ml_dtypes

import concourse.bass as bass
import concourse.mybir as mybir
from concourse.bass_utils import run_bass_kernel_spmd

F32 = mybir.dt.float32
BF16 = mybir.dt.bfloat16
ALU = mybir.AluOpType
AF = mybir.ActivationFunctionType
AX = mybir.AxisListType
_bf = ml_dtypes.bfloat16

B, T, D = 8, 4096, 2048
NT = T // 128
NH, NKV, HD, PD = 16, 4, 128, 16
GQ = NH // NKV
NE, NR = 6, 16
NCAT = 16 + NE * NR
SCALE = 1.0 / math.sqrt(HD)
LN_EPS = 1e-5
RMS_EPS = float(np.finfo(np.float32).eps)
NEGBIG = -1.0e30

ENGINES = ("sync", "gpsimd", "tensor", "vector", "scalar")
SEM_NAMES = ("spe", "sve", "sse",
             "sxa", "sxb",
             "soa", "sob",
             "scst", "sw2", "sw3", "sw45")


class _Ins:
    __slots__ = ("ctx", "real")

    def __init__(self, ctx, real):
        self.ctx, self.real = ctx, real

    def inc(self, sem_name, k=1):
        self.ctx.c[sem_name] += k
        if self.real is not None:
            self.real.then_inc(self.ctx.sems[sem_name], k)
        return self


class _Eng:
    def __init__(self, ctx, name):
        object.__setattr__(self, "ctx", ctx)
        object.__setattr__(self, "name", name)
        object.__setattr__(self, "real", None)

    def wait(self, sem_name, value):
        if self.real is not None and value > 0:
            self.real.wait_ge(self.ctx.sems[sem_name], value)

    def __getattr__(self, op):
        ctx = self.ctx
        real = self.real

        def f(*a, **kw):
            ri = getattr(real, op)(*a, **kw) if real is not None else None
            return _Ins(ctx, ri)
        return f


class Ctx:
    def __init__(self, sems, active):
        self.sems = sems
        self.active = active
        self.c = {k: 0 for k in SEM_NAMES}
        self.m = {}
        self.T = _Eng(self, "tensor")
        self.V = _Eng(self, "vector")
        self.S = _Eng(self, "scalar")
        self.Y = _Eng(self, "sync")
        self.G = _Eng(self, "gpsimd")
        self.by_name = {"tensor": self.T, "vector": self.V, "scalar": self.S,
                        "sync": self.Y, "gpsimd": self.G}

    def bind(self, eng):
        object.__setattr__(self.by_name[self.active], "real", eng)

    def unbind(self):
        object.__setattr__(self.by_name[self.active], "real", None)


def _emit_all(blk, ctxs, fn):
    for name in ENGINES:
        c = ctxs[name]

        def body(eng, c=c):
            c.bind(eng)
            fn(c)
            c.unbind()
        getattr(blk, name)(body)


def _barrier(c):
    snap = dict(c.c)
    for e in (c.T, c.V, c.S, c.Y, c.G):
        for sn in SEM_NAMES:
            if snap[sn] > 0:
                e.wait(sn, snap[sn])


def build_nc(nt=NT, dbg=False):
    nc = bass.Bass("TRN2", target_bir_lowering=False)

    def din(name, shape, dt=F32):
        return nc.dram_tensor(name, shape, dt, kind="ExternalInput").ap()

    x_d = din("x", [T, D])
    wq_i = din("wq_img", [128, 16 * D])
    wk_i = din("wk_img", [128, 16 * 512])
    wv_i = din("wv_img", [128, 16 * 512])
    wp_i = din("wp_img", [128, 16 * D])
    wb_i = din("wb_img", [128, 16 * D])
    wr_i = din("wr_img", [128, 16 * NE])
    ac_i = din("acat_img", [128, 16 * NCAT])
    bc_i = din("bcat_img", [NCAT, D])
    bb_i = din("bb_rep", [128, D])
    br_i = din("br_rep", [128, NE])
    qg_i = din("qg_rep", [128, NH])
    id_i = din("ident_bf", [128, 128], BF16)
    trikm_i = din("trikm", [128, 512], BF16)
    ones_i = din("ones_cr", [128, 129], BF16)
    eps_i = din("eps_c", [128, 2])
    rope_i = din("rope_tab", [T, 256])

    out_d = nc.dram_tensor("out", [T, D], F32, kind="ExternalOutput").ap()
    xnT_s = nc.dram_tensor("xnT_s", [16, 128, T], BF16, kind="Internal").ap()
    yT_s = nc.dram_tensor("yT_s", [16, 128, T], BF16, kind="Internal").ap()
    qT_s = nc.dram_tensor("qT_s", [16, 128, T], BF16, kind="Internal").ap()
    kT_s = nc.dram_tensor("kT_s", [128, NKV * T], BF16, kind="Internal").ap()
    v_s = nc.dram_tensor("v_s", [128, NT * 512], BF16, kind="Internal").ap()

    with contextlib.ExitStack() as es:
        sems = {n: es.enter_context(nc.semaphore(n)) for n in SEM_NAMES}
        blk = es.enter_context(nc.Block())
        ctxs = {n: Ctx(sems, n) for n in ENGINES}

        ident = es.enter_context(nc.sbuf_tensor("ident", [128, 128], BF16))
        trikm_sb = es.enter_context(nc.sbuf_tensor("trikm_sb", [128, 512], BF16))
        ones_cr = es.enter_context(nc.sbuf_tensor("ones_sb", [128, 129], BF16))
        epsc = es.enter_context(nc.sbuf_tensor("epsc", [128, 2], F32))
        qg_sb = es.enter_context(nc.sbuf_tensor("qg_sb", [128, NH], F32))
        br_sb = es.enter_context(nc.sbuf_tensor("br_sb", [128, NE], F32))
        w6 = es.enter_context(nc.sbuf_tensor("w6", [128, NT * NE], F32))
        st1 = es.enter_context(nc.sbuf_tensor("st1", [128, 16], F32))
        st2 = es.enter_context(nc.sbuf_tensor("st2", [128, 16], F32))
        st3 = es.enter_context(nc.sbuf_tensor("st3", [128, 16], F32))
        nmean = es.enter_context(nc.sbuf_tensor("nmean", [128, 1], F32))
        nmist = es.enter_context(nc.sbuf_tensor("nmist", [128, 1], F32))

        es2 = contextlib.ExitStack()
        wq_bf = es2.enter_context(nc.sbuf_tensor("wq_bf", [128, 16 * D], BF16))
        wk_bf = es2.enter_context(nc.sbuf_tensor("wk_bf", [128, 16 * 512], BF16))
        wv_bf = es2.enter_context(nc.sbuf_tensor("wv_bf", [128, 16 * 512], BF16))
        wr_bf = es2.enter_context(nc.sbuf_tensor("wr_bf", [128, 16 * NE], BF16))
        xb2 = [es2.enter_context(nc.sbuf_tensor(f"xb2_{j}", [128, D], F32)) for j in range(2)]
        xnbf = [es2.enter_context(nc.sbuf_tensor(f"xnbf{j}", [128, D], BF16)) for j in range(2)]
        xnTb2 = [es2.enter_context(nc.sbuf_tensor(f"xnTb2_{j}", [128, D], BF16)) for j in range(2)]
        ropeb = [es2.enter_context(nc.sbuf_tensor(f"ropeb{j}", [128, 256], F32)) for j in range(2)]
        rtmp = es2.enter_context(nc.sbuf_tensor("rtmp", [128, 512], F32))
        ktm = es2.enter_context(nc.sbuf_tensor("ktm", [128, 512], F32))
        kbf = es2.enter_context(nc.sbuf_tensor("kbf", [128, 512], BF16))
        kTb = [es2.enter_context(nc.sbuf_tensor(f"kTb{j}", [128, 512], BF16)) for j in range(2)]
        vb = [es2.enter_context(nc.sbuf_tensor(f"vb{j}", [128, 512], BF16)) for j in range(2)]
        qbf = es2.enter_context(nc.sbuf_tensor("qbf", [128, D], BF16))
        qsq = es2.enter_context(nc.sbuf_tensor("qsq", [128, D], BF16))
        qTb = [es2.enter_context(nc.sbuf_tensor(f"qTb{j}", [128, D], BF16)) for j in range(2)]
        sc16 = es2.enter_context(nc.sbuf_tensor("sc16", [128, NH], F32))
        rf6 = es2.enter_context(nc.sbuf_tensor("rf6", [128, NE], F32))
        g6a = es2.enter_context(nc.sbuf_tensor("g6a", [128, NE], F32))
        g6b = es2.enter_context(nc.sbuf_tensor("g6b", [128, NE], F32))
        z6 = es2.enter_context(nc.sbuf_tensor("z6", [128, 1], F32))
        m6 = es2.enter_context(nc.sbuf_tensor("m6", [128, 1], F32))
        ps_k = [es2.enter_context(nc.psum_tensor(f"ps_k{j}", [128, 512], F32)) for j in range(2)]
        ps_v = [es2.enter_context(nc.psum_tensor(f"ps_v{j}", [128, 512], F32)) for j in range(2)]
        ps_r = [es2.enter_context(nc.psum_tensor(f"ps_r{j}", [128, 512], F32)) for j in range(2)]
        ps_xt = [es2.enter_context(nc.psum_tensor(f"ps_xt{j}", [128, 512], F32)) for j in range(2)]

        def setup(c):
            c.G.dma_start(out=ident[:], in_=id_i).inc("scst", 16)
            c.G.dma_start(out=trikm_sb[:], in_=trikm_i).inc("scst", 16)
            c.G.dma_start(out=ones_cr[:], in_=ones_i).inc("scst", 16)
            c.G.dma_start(out=qg_sb[:], in_=qg_i).inc("scst", 16)
            c.G.dma_start(out=br_sb[:], in_=br_i).inc("scst", 16)
            c.G.dma_start(out=epsc[:], in_=eps_i).inc("scst", 16)
            c.G.dma_start(out=wk_bf[:], in_=wk_i).inc("sw2", 16)
            c.G.dma_start(out=wv_bf[:], in_=wv_i).inc("sw2", 16)
            c.G.dma_start(out=wr_bf[:], in_=wr_i).inc("sw2", 16)
            for fc in range(4):
                c.G.dma_start(
                    out=wq_bf[:].rearrange("p (ch d) -> p ch d", ch=16)[:, :, fc * 512:(fc + 1) * 512],
                    in_=wq_i.rearrange("p (ch d) -> p ch d", ch=16)[:, :, fc * 512:(fc + 1) * 512],
                ).inc("sw3", 16)

        _emit_all(blk, ctxs, setup)
        CONSTS = 6 * 16
        W2 = 3 * 16

        def bw(c, bank):
            prev = c.m.get(("b", bank))
            if prev is not None:
                c.T.wait(prev[0], prev[1])

        def bank_of(bid):
            return {"k0": ps_k[0], "k1": ps_k[1], "v0": ps_v[0], "v1": ps_v[1],
                    "r0": ps_r[0], "r1": ps_r[1], "x0": ps_xt[0], "x1": ps_xt[1]}[bid]

        def p2_A(c, i):
            par = "sxa" if i % 2 == 0 else "sxb"
            if i >= 2:
                c.Y.wait("sse", c.m[("p2_s_done", i - 2)])
                c.Y.wait("sve", c.m[("p2_qprep", i - 2)])
            c.Y.dma_start(out=xb2[i % 2][:], in_=x_d[i * 128:(i + 1) * 128, :]).inc(par, 16)
            c.Y.dma_start(out=ropeb[i % 2][:], in_=rope_i[i * 128:(i + 1) * 128, :]).inc(par, 16)
            ld = c.c[par]

            c.V.wait(par, ld)
            c.V.reduce_sum(out=nmean[:], in_=xb2[i % 2][:], axis=AX.X, negate=True).inc("sve")
            c.V.wait("sve", c.c["sve"])
            c.V.tensor_scalar(out=nmean[:], in0=nmean[:], scalar1=1.0 / D, scalar2=None,
                              op0=ALU.mult).inc("sve")
            c.S.wait("sve", c.c["sve"])
            c.S.wait(par, ld)
            c.S.activation(out=xnbf[i % 2][:], in_=xb2[i % 2][:], func=AF.Square,
                           bias=nmean[:], scale=1.0, accum_out=st1[:, 15:16])
            c.S.activation(out=st3[:, 14:15], in_=st1[:, 15:16], func=AF.Ln,
                           scale=1.0 / D, bias=epsc[:, 0:1])
            c.S.activation(out=st3[:, 15:16], in_=st3[:, 14:15], func=AF.Exp,
                           scale=-0.5).inc("sse")
            c.V.wait("sse", c.c["sse"])
            c.V.tensor_tensor(out=nmist[:], in0=nmean[:], in1=st3[:, 15:16], op=ALU.mult).inc("sve")
            c.S.wait("sve", c.c["sve"])
            c.S.activation(out=xnbf[i % 2][:], in_=xb2[i % 2][:], func=AF.Identity,
                           scale=st3[:, 15:16], bias=nmist[:]).inc("sse")
            c.m[("p2_s_done", i)] = c.c["sse"]
            c.m[("p2_ln", i)] = c.c["sse"]

        def p2_XTtr(c, i):
            par = "sxa" if i % 2 == 0 else "sxb"
            spar = "soa" if i % 2 == 0 else "sob"
            for g in range(4):
                c.T.wait("sse", c.m[("p2_ln", i)])
                if i == 0 and g == 0:
                    c.T.wait("scst", CONSTS)
                bid = "x%d" % (g % 2)
                bw(c, bid)
                ins = None
                for q in range(4):
                    ch = g * 4 + q
                    ins = c.T.matmul(ps_xt[g % 2][:, q * 128:(q + 1) * 128],
                                     xnbf[i % 2][:, ch * 128:(ch + 1) * 128], ident[:],
                                     start=True, stop=True, skip_group_check=True)
                ins.inc("spe")
                eng = c.S if g % 2 == 0 else c.V
                sem = "sse" if g % 2 == 0 else "sve"
                eng.wait("spe", c.c["spe"])
                if g < 2 and i >= 2:
                    eng.wait(spar, c.m[("p2_st", i - 2)])
                    eng.wait("spe", c.m[("p2_qc_done", i - 2)])
                if g % 2 == 0:
                    c.S.activation(out=xnTb2[i % 2][:, g * 512:(g + 1) * 512],
                                   in_=ps_xt[g % 2][:], func=AF.Identity).inc("sse")
                else:
                    c.V.tensor_copy(out=xnTb2[i % 2][:, g * 512:(g + 1) * 512],
                                    in_=ps_xt[g % 2][:]).inc("sve")
                c.m[("b", bid)] = (sem, c.c[sem])
            c.m[("p2_xnT_s", i)] = c.c["sse"]
            c.m[("p2_xnT_v", i)] = c.c["sve"]

            c.G.wait("sse", c.c["sse"])
            c.G.wait("sve", c.c["sve"])
            c.G.dma_start(out=xnT_s.rearrange("c p t -> p c t")[:, :, i * 128:(i + 1) * 128],
                          in_=xnTb2[i % 2][:].rearrange("p (c t) -> p c t", c=16)).inc(spar, 16)
            c.m[("p2_st", i)] = c.c[spar]

        def p2_KVR(c, i):
            c.T.wait("sw2", W2)
            c.T.wait("sse", c.m[("p2_xnT_s", i)])
            c.T.wait("sve", c.m[("p2_xnT_v", i)])
            for bid in ("k%d" % (i % 2), "v%d" % (i % 2), "r%d" % (i % 2)):
                bw(c, bid)
            ins = None
            for ch in range(16):
                xc = xnTb2[i % 2][:, ch * 128:(ch + 1) * 128]
                c.T.matmul(ps_k[i % 2][:], xc, wk_bf[:, ch * 512:(ch + 1) * 512],
                           start=(ch == 0), stop=(ch == 15), skip_group_check=True)
                c.T.matmul(ps_v[i % 2][:], xc, wv_bf[:, ch * 512:(ch + 1) * 512],
                           start=(ch == 0), stop=(ch == 15), skip_group_check=True)
                ins = c.T.matmul(ps_r[i % 2][:, 0:NE], xc, wr_bf[:, ch * NE:(ch + 1) * NE],
                                 start=(ch == 0), stop=(ch == 15), skip_group_check=True)
            ins.inc("spe")
            c.m[("p2_kvr", i)] = c.c["spe"]

        def p2_KP(c, i):
            par = "sxa" if i % 2 == 0 else "sxb"
            spar = "soa" if i % 2 == 0 else "sob"
            kvr = c.m[("p2_kvr", i)]
            c.V.wait("spe", kvr)
            c.V.tensor_copy(out=rf6[:], in_=ps_r[i % 2][:, 0:NE]).inc("sve")
            c.m[("b", "r%d" % (i % 2))] = ("sve", c.c["sve"])
            c.S.wait("spe", kvr)
            if i >= 1:
                c.S.wait("sve", c.m[("p2_kbf", i - 1)])
            c.S.activation(out=ktm[:], in_=ps_k[i % 2][:], func=AF.Identity).inc("sse")
            c.m[("b", "k%d" % (i % 2))] = ("sse", c.c["sse"])
            if i >= 2:
                c.S.wait(spar, c.m[("p2_st_kv", i - 2)])
            c.S.activation(out=vb[i % 2][:], in_=ps_v[i % 2][:], func=AF.Identity).inc("sse")
            c.m[("b", "v%d" % (i % 2))] = ("sse", c.c["sse"])
            c.G.wait("sse", c.c["sse"])
            c.G.dma_start(out=v_s[:, i * 512:(i + 1) * 512], in_=vb[i % 2][:]).inc(spar, 16)

            c.S.activation(out=kbf[:], in_=ktm[:], func=AF.Square).inc("sse")
            c.V.wait("sse", c.c["sse"])
            c.V.tensor_reduce(out=st1[:, 0:NKV],
                              in_=kbf[:].rearrange("p (h d) -> p h d", h=NKV),
                              axis=AX.X, op=ALU.add).inc("sve")
            c.S.wait("sve", c.c["sve"])
            c.S.activation(out=st3[:, 0:NKV], in_=st1[:, 0:NKV], func=AF.Ln,
                           scale=1.0 / HD, bias=epsc[:, 1:2])
            c.S.activation(out=st3[:, 0:NKV], in_=st3[:, 0:NKV], func=AF.Exp,
                           scale=-0.5).inc("sse")
            c.V.wait("sse", c.c["sse"])
            for h in range(NKV):
                c.V.tensor_scalar(out=ktm[:, h * 128:(h + 1) * 128],
                                  in0=ktm[:, h * 128:(h + 1) * 128],
                                  scalar1=st3[:, h:h + 1], scalar2=None, op0=ALU.mult)
            kv_ = ktm[:].rearrange("p (h d) -> p h d", h=NKV)
            co = ropeb[i % 2][:, 0:32].rearrange("p (h d) -> p h d", h=NKV)
            si = ropeb[i % 2][:, 128:160].rearrange("p (h d) -> p h d", h=NKV)
            x1, x2 = kv_[:, :, 0:8], kv_[:, :, 8:16]
            r_ = rtmp[:].rearrange("p (h d) -> p h d", h=64)
            r1, r2, r3, r4 = r_[:, 0:4], r_[:, 4:8], r_[:, 8:12], r_[:, 12:16]
            c.V.tensor_tensor(out=r1, in0=x1, in1=co, op=ALU.mult)
            c.V.tensor_tensor(out=r2, in0=x1, in1=si, op=ALU.mult)
            c.V.tensor_tensor(out=r3, in0=x2, in1=si, op=ALU.mult)
            c.V.tensor_tensor(out=x1, in0=r1, in1=r3, op=ALU.add)
            c.V.tensor_tensor(out=r4, in0=x2, in1=co, op=ALU.mult)
            c.V.tensor_tensor(out=x2, in0=r2, in1=r4, op=ALU.subtract).inc("sve")
            c.V.wait("sve", c.c["sve"])
            c.V.tensor_copy(out=kbf[:], in_=ktm[:]).inc("sve")
            c.m[("p2_kbf", i)] = c.c["sve"]

            def vs(ins):
                ins.inc("sve")
                c.V.wait("sve", c.c["sve"])
            vs(c.V.tensor_tensor(out=rf6[:], in0=rf6[:], in1=br_sb[:], op=ALU.add))
            c.V.reduce_max(out=m6[:], in_=rf6[:], axis=AX.X, negate=True).inc("sve")
            c.S.wait("sve", c.c["sve"])
            c.S.activation(out=g6a[:], in_=rf6[:], func=AF.Exp, bias=m6[:],
                           accum_out=z6[:]).inc("sse")
            c.V.wait("sse", c.c["sse"])
            vs(c.V.reciprocal(out=m6[:], in_=z6[:]))
            vs(c.V.tensor_scalar(out=g6a[:], in0=g6a[:], scalar1=m6[:], scalar2=None,
                                 op0=ALU.mult))
            vs(c.V.reduce_max(out=m6[:], in_=g6a[:], axis=AX.X))
            vs(c.V.tensor_scalar(out=g6b[:], in0=g6a[:], scalar1=m6[:], scalar2=None,
                                 op0=ALU.is_ge))
            vs(c.V.scalar_tensor_tensor(out=g6b[:], in0=g6b[:], scalar=-2.0, in1=g6a[:],
                                        op0=ALU.mult, op1=ALU.add))
            vs(c.V.reduce_max(out=m6[:], in_=g6b[:], axis=AX.X))
            vs(c.V.tensor_scalar(out=g6b[:], in0=g6a[:], scalar1=m6[:], scalar2=None,
                                 op0=ALU.is_ge))
            c.V.tensor_tensor(out=w6[:, i * NE:(i + 1) * NE], in0=g6a[:], in1=g6b[:],
                              op=ALU.mult).inc("sve")
            c.m[("p2_v_done", i)] = c.c["sve"]

        def p2_QC(c, i):
            par = i % 2
            qb = ["k%d" % (1 - par), "v%d" % (1 - par), "r%d" % (1 - par), "r%d" % par]
            if i == 0:
                c.T.wait("sw3", 64)
            for pair in range(2):
                f0, f1 = 2 * pair, 2 * pair + 1
                bw(c, qb[f0])
                bw(c, qb[f1])
                ins = None
                for ch in range(16):
                    xc = xnTb2[i % 2][:, ch * 128:(ch + 1) * 128]
                    c.T.matmul(bank_of(qb[f0])[:], xc,
                               wq_bf[:, ch * D + f0 * 512: ch * D + (f0 + 1) * 512],
                               start=(ch == 0), stop=(ch == 15), skip_group_check=True)
                    ins = c.T.matmul(bank_of(qb[f1])[:], xc,
                                     wq_bf[:, ch * D + f1 * 512: ch * D + (f1 + 1) * 512],
                                     start=(ch == 0), stop=(ch == 15), skip_group_check=True)
                ins.inc("spe")
                c.S.wait("spe", c.c["spe"])
                if pair == 0 and i >= 1:
                    c.S.wait("spe", c.m[("p2_qtr", i - 1)])
                c.S.activation(out=qbf[:, f0 * 512:(f0 + 1) * 512],
                               in_=bank_of(qb[f0])[:], func=AF.Identity).inc("sse")
                c.m[("b", qb[f0])] = ("sse", c.c["sse"])
                c.S.activation(out=qbf[:, f1 * 512:(f1 + 1) * 512],
                               in_=bank_of(qb[f1])[:], func=AF.Identity).inc("sse")
                c.m[("b", qb[f1])] = ("sse", c.c["sse"])
                c.S.wait("sve", c.c["sve"])
                c.S.activation(out=qsq[:, pair * 1024:(pair + 1) * 1024],
                               in_=qbf[:, pair * 1024:(pair + 1) * 1024],
                               func=AF.Square).inc("sse")
                c.V.wait("sse", c.c["sse"])
                c.V.tensor_reduce(
                    out=st2[:, pair * 8:(pair + 1) * 8],
                    in_=qsq[:, pair * 1024:(pair + 1) * 1024].rearrange(
                        "p (h d) -> p h d", h=8),
                    axis=AX.X, op=ALU.add).inc("sve")
            c.m[("p2_qc_done", i)] = c.c["spe"]
            c.m[("p2_qstat", i)] = c.c["sve"]

        def p2_KT(c, i):
            spar = "soa" if i % 2 == 0 else "sob"
            c.T.wait("sve", c.m[("p2_kbf", i)])
            bw(c, "x0")
            ins = None
            for h in range(NKV):
                ins = c.T.matmul(ps_xt[0][:, h * 128:(h + 1) * 128],
                                 kbf[:, h * 128:(h + 1) * 128], ident[:],
                                 start=True, stop=True, skip_group_check=True)
            ins.inc("spe")
            c.m[("p2_ktr", i)] = c.c["spe"]
            c.S.wait("spe", c.c["spe"])
            if i >= 2:
                c.S.wait(spar, c.m[("p2_st_kv", i - 2)])
            c.S.activation(out=kTb[i % 2][:], in_=ps_xt[0][:], func=AF.Identity).inc("sse")
            c.m[("b", "x0")] = ("sse", c.c["sse"])
            c.G.wait("sse", c.c["sse"])
            c.G.dma_start(
                out=kT_s.rearrange("p (h t) -> p h t", h=NKV)[:, :, i * 128:(i + 1) * 128],
                in_=kTb[i % 2][:].rearrange("p (h t) -> p h t", h=NKV)).inc(spar, 16)
            c.m[("p2_st_kv", i)] = c.c[spar]

        def p2_QP(c, i):
            par = "sxa" if i % 2 == 0 else "sxb"
            c.S.wait("sve", c.m[("p2_qstat", i)])
            c.S.activation(out=st3[:], in_=st2[:], func=AF.Ln,
                           scale=1.0 / HD, bias=epsc[:, 1:2])
            c.S.activation(out=st3[:], in_=st3[:], func=AF.Exp, scale=-0.5).inc("sse")
            c.V.wait("sse", c.c["sse"])
            c.V.tensor_tensor(out=sc16[:], in0=st3[:], in1=qg_sb[:], op=ALU.mult).inc("sve")
            c.V.wait("sve", c.c["sve"])
            for h in range(NH):
                c.V.tensor_scalar(out=qbf[:, h * 128:(h + 1) * 128],
                                  in0=qbf[:, h * 128:(h + 1) * 128],
                                  scalar1=sc16[:, h:h + 1], scalar2=None, op0=ALU.mult)
            qv = qbf[:].rearrange("p (h d) -> p h d", h=NH)
            co = ropeb[i % 2][:, 0:128].rearrange("p (h d) -> p h d", h=NH)
            si = ropeb[i % 2][:, 128:256].rearrange("p (h d) -> p h d", h=NH)
            x1, x2 = qv[:, :, 0:8], qv[:, :, 8:16]
            r_ = rtmp[:].rearrange("p (h d) -> p h d", h=64)
            r1, r2, r3, r4 = r_[:, 0:16], r_[:, 16:32], r_[:, 32:48], r_[:, 48:64]
            c.V.tensor_tensor(out=r1, in0=x1, in1=co, op=ALU.mult)
            c.V.tensor_tensor(out=r2, in0=x1, in1=si, op=ALU.mult)
            c.V.tensor_tensor(out=r3, in0=x2, in1=si, op=ALU.mult)
            c.V.tensor_tensor(out=x1, in0=r1, in1=r3, op=ALU.add)
            c.V.tensor_tensor(out=r4, in0=x2, in1=co, op=ALU.mult)
            c.V.tensor_tensor(out=x2, in0=r2, in1=r4, op=ALU.subtract).inc("sve")
            c.m[("p2_qprep", i)] = c.c["sve"]

        def p2_QT(c, i):
            spar = "soa" if i % 2 == 0 else "sob"
            c.T.wait("sve", c.m[("p2_qprep", i)])
            for g in range(4):
                bid = "x%d" % ((g + 1) % 2)
                bw(c, bid)
                ins = None
                for hh in range(4):
                    h = g * 4 + hh
                    ins = c.T.matmul(ps_xt[(g + 1) % 2][:, hh * 128:(hh + 1) * 128],
                                     qbf[:, h * 128:(h + 1) * 128], ident[:],
                                     start=True, stop=True, skip_group_check=True)
                ins.inc("spe")
                c.S.wait("spe", c.c["spe"])
                if g == 0 and i >= 2:
                    c.S.wait(spar, c.m[("p2_st_q", i - 2)])
                c.S.activation(out=qTb[i % 2][:, g * 512:(g + 1) * 512],
                               in_=ps_xt[(g + 1) % 2][:], func=AF.Identity).inc("sse")
                c.m[("b", bid)] = ("sse", c.c["sse"])
            c.m[("p2_qtr", i)] = c.c["spe"]
            c.G.wait("sse", c.c["sse"])
            c.G.dma_start(out=qT_s.rearrange("c p t -> p c t")[:, :, i * 128:(i + 1) * 128],
                          in_=qTb[i % 2][:].rearrange("p (c t) -> p c t", c=16)).inc(spar, 16)
            c.m[("p2_st_q", i)] = c.c[spar]

        def p2(c):
            p2_A(c, 0)
            p2_XTtr(c, 0)
            p2_KVR(c, 0)
            for i in range(nt):
                p2_KP(c, i)
                if i + 1 < nt:
                    p2_A(c, i + 1)
                p2_QC(c, i)
                p2_KT(c, i)
                if i + 1 < nt:
                    p2_XTtr(c, i + 1)
                    p2_KVR(c, i + 1)
                p2_QP(c, i)
                p2_QT(c, i)
            _barrier(c)

        _emit_all(blk, ctxs, p2)
        es2.close()

        es34 = contextlib.ExitStack()
        wb_bf = es34.enter_context(nc.sbuf_tensor("wb_bf", [128, 16 * D], BF16))
        ac_bf = es34.enter_context(nc.sbuf_tensor("ac_bf", [128, 16 * NCAT], BF16))
        bc_bf = es34.enter_context(nc.sbuf_tensor("bc_bf", [NCAT, D], BF16))
        bb_sb = es34.enter_context(nc.sbuf_tensor("bb_sb", [128, D], F32))

        es3 = contextlib.ExitStack()
        kcache = es3.enter_context(nc.sbuf_tensor("kcache", [128, NKV * T], BF16))
        vcache = es3.enter_context(nc.sbuf_tensor("vcache", [128, NT * 512], BF16))
        qTb3 = [es3.enter_context(nc.sbuf_tensor(f"qTb3_{j}", [128, D], BF16)) for j in range(2)]
        pT = es3.enter_context(nc.sbuf_tensor("pT", [128, 12 * 512], BF16))
        yTb = [es3.enter_context(nc.sbuf_tensor(f"yTb{j}", [128, D], BF16)) for j in range(2)]
        nb1 = es3.enter_context(nc.sbuf_tensor("nb1", [128, 1], F32))
        zrow = es3.enter_context(nc.sbuf_tensor("zrow", [1, 512], F32))
        zrec = es3.enter_context(nc.sbuf_tensor("zrec", [1, D], BF16))
        rzbc = es3.enter_context(nc.sbuf_tensor("rzbc", [128, 512], F32))
        ps_d = [es3.enter_context(nc.psum_tensor(f"ps_d{j}", [128, 1024], F32)) for j in range(2)]
        ps_yu = [es3.enter_context(nc.psum_tensor(f"ps_yu{j}", [128, 512], F32)) for j in range(2)]
        ps_z = [es3.enter_context(nc.psum_tensor(f"ps_z{j}", [128, 512], F32)) for j in range(2)]

        def _bkwait(c, eng, bid):
            prev = c.m.get(("p3bk", bid))
            if prev is not None:
                eng.wait(prev[0], prev[1])

        def p3_attn_kv(c, i, kv, pre_cb=None):
            nst = i + 1
            NSL = 12
            npair = (nst + 1) // 2

            def emit_pair(p):
                s0 = 2 * p
                s1 = min(2 * p + 1, nst - 1)
                w = s1 - s0 + 1
                bk = ps_d[p % 2]
                _bkwait(c, c.T, ("d", p % 2))
                if p == 0 and kv == 0:
                    par = "sxa" if i % 2 == 0 else "sxb"
                    c.T.wait(par, c.m[("p3_ld", i)])
                ins = None
                for s in range(s0, s1 + 1):
                    last = (s == nst - 1)
                    ins = c.T.matmul(bk[:, (s - s0) * 512:(s - s0 + 1) * 512],
                                     kcache[:, kv * T + s * 128: kv * T + (s + 1) * 128],
                                     qTb3[i % 2][:, kv * 512:(kv + 1) * 512],
                                     start=True, stop=not last, skip_group_check=True)
                    if last:
                        ins = c.T.matmul(bk[:, (s - s0) * 512:(s - s0 + 1) * 512],
                                         ident[:], trikm_sb[:],
                                         start=False, stop=True, skip_group_check=True)
                ins.inc("spe")
                c.S.wait("spe", c.c["spe"])
                prev = c.m.get(("p3_pt", s0 % NSL))
                if prev is not None:
                    c.S.wait("spe", prev)
                c.S.activation(out=pT[:, (s0 % NSL) * 512:((s0 % NSL) + w) * 512],
                               in_=bk[:, 0:w * 512],
                               func=AF.Exp, bias=nb1[:], scale=SCALE).inc("sse")
                c.m[("p3bk", ("d", p % 2))] = ("sse", c.c["sse"])
                c.m[("p3_expp", p)] = c.c["sse"]

            def emit_pv(s, need_wait):
                last = (s == nst - 1)
                if need_wait:
                    c.T.wait("sse", c.m[("p3_expp", s // 2)])
                if s == 0:
                    _bkwait(c, c.T, ("yu", kv % 2))
                c.T.matmul(ps_yu[kv % 2][:],
                           vcache[:, s * 512 + kv * 128: s * 512 + (kv + 1) * 128],
                           pT[:, (s % NSL) * 512:(s % NSL + 1) * 512],
                           start=(s == 0), stop=last, skip_group_check=True).inc("spe")

            def emit_zbatch(s0, s1):
                c.T.wait("sse", c.m[("p3_expp", (s1 - 1) // 2)])
                if s0 == 0:
                    _bkwait(c, c.T, ("z", kv % 2))
                ins = None
                for s in range(s0, s1):
                    ins = c.T.matmul(ps_z[kv % 2][0:1, :], ones_cr[:, 0:1],
                                     pT[:, (s % NSL) * 512:(s % NSL + 1) * 512],
                                     start=(s == 0), stop=(s == nst - 1),
                                     skip_group_check=True)
                ins.inc("spe")
                for s in range(s0, s1):
                    c.m[("p3_pt", s % NSL)] = c.c["spe"]

            emit_pair(0)
            if npair > 1:
                emit_pair(1)
            zb = 0
            for p in range(npair):
                if p + 2 < npair:
                    emit_pair(p + 2)
                if p == 0 and pre_cb is not None:
                    pre_cb[0]()
                if p == min(4, npair - 1) and pre_cb is not None:
                    pre_cb[1]()
                s0 = 2 * p
                s1 = min(2 * p + 1, nst - 1)
                emit_pv(s0, True)
                if s1 > s0:
                    emit_pv(s1, False)
                if s1 - zb >= 6 or s1 == nst - 1:
                    emit_zbatch(zb, s1 + 1)
                    zb = s1 + 1
            c.m[("p3_pvz", kv)] = c.c["spe"]

        def p3_epi_a(c, i, kv):
            c.V.wait("spe", c.m[("p3_pvz", kv)])
            c.V.tensor_copy(out=zrow[0:1, :], in_=ps_z[kv % 2][0:1, :]).inc("sve")
            c.m[("p3bk", ("z", kv % 2))] = ("sve", c.c["sve"])
            c.V.wait("sve", c.c["sve"])
            c.V.reciprocal(out=zrow[0:1, :], in_=zrow[0:1, :]).inc("sve")
            c.V.wait("sve", c.c["sve"])
            c.V.tensor_copy(out=zrec[0:1, kv * 512:(kv + 1) * 512], in_=zrow[0:1, :]).inc("sve")
            c.m[("p3_zr", kv)] = c.c["sve"]

        def p3_epi_b(c, i, kv):
            c.T.wait("sve", c.m[("p3_zr", kv)])
            c.T.matmul(ps_z[kv % 2][:], ones_cr[0:1, 1:129], zrec[0:1, kv * 512:(kv + 1) * 512],
                       start=True, stop=True, skip_group_check=True).inc("spe")
            c.V.wait("spe", c.c["spe"])
            c.V.tensor_copy(out=rzbc[:], in_=ps_z[kv % 2][:]).inc("sve")
            c.m[("p3bk", ("z", kv % 2))] = ("sve", c.c["sve"])
            c.V.wait("sve", c.c["sve"])
            if kv == 0 and i >= 2:
                spar_prev = "soa" if i % 2 == 0 else "sob"
                c.V.wait(spar_prev, c.m[("p3_st", i - 2)])
            c.V.scalar_tensor_tensor(out=yTb[i % 2][:, kv * 512:(kv + 1) * 512],
                                     in0=ps_yu[kv % 2][:],
                                     scalar=1.0, in1=rzbc[:],
                                     op0=ALU.mult, op1=ALU.mult).inc("sve")
            c.m[("p3bk", ("yu", kv % 2))] = ("sve", c.c["sve"])

        def p3(c):
            c.V.wait("scst", CONSTS)
            c.V.reduce_max(out=nb1[:], in_=qg_sb[:], axis=AX.X).inc("sve")
            c.V.wait("sve", c.c["sve"])
            c.V.tensor_scalar(out=nb1[:], in0=nb1[:], scalar1=-math.sqrt(float(HD)),
                              scalar2=40.0, op0=ALU.mult, op1=ALU.add).inc("sve")
            c.V.wait("sve", c.c["sve"])
            for i in range(nt):
                par = "sxa" if i % 2 == 0 else "sxb"
                spar = "soa" if i % 2 == 0 else "sob"
                if i >= 2:
                    c.Y.wait("spe", c.m[("p3_pe_done", i - 2)])
                c.Y.dma_start(out=qTb3[i % 2][:].rearrange("p (c t) -> p c t", c=16),
                              in_=qT_s.rearrange("c p t -> p c t")[:, :, i * 128:(i + 1) * 128]
                              ).inc(par, 16)
                c.Y.dma_start(
                    out=kcache[:].rearrange("p (h t) -> p h t", h=NKV)[:, :, i * 128:(i + 1) * 128],
                    in_=kT_s.rearrange("p (h t) -> p h t", h=NKV)[:, :, i * 128:(i + 1) * 128]
                ).inc(par, 16)
                c.Y.dma_start(out=vcache[:, i * 512:(i + 1) * 512],
                              in_=v_s[:, i * 512:(i + 1) * 512]).inc(par, 16)
                c.m[("p3_ld", i)] = c.c[par]

                if i >= 4 and (i - 4) % 3 == 0 and (i - 4) // 3 < 8:
                    j = (i - 4) // 3
                    fc, hf = j // 2, j % 2
                    c.G.dma_start(
                        out=wb_bf[:].rearrange("p (ch d) -> p ch d", ch=16)[
                            :, hf * 8:(hf + 1) * 8, fc * 512:(fc + 1) * 512],
                        in_=wb_i.rearrange("p (ch d) -> p ch d", ch=16)[
                            :, hf * 8:(hf + 1) * 8, fc * 512:(fc + 1) * 512],
                    ).inc("sw45", 16)
                elif i == 26:
                    c.G.dma_start(out=ac_bf[:], in_=ac_i).inc("sw45", 16)
                elif i == 27:
                    c.G.dma_start(out=bc_bf[:], in_=bc_i).inc("sw45", 16)
                elif i == 28:
                    c.G.dma_start(out=bb_sb[:], in_=bb_i).inc("sw45", 16)

                for kv in range(NKV):
                    if kv == 0:
                        p3_attn_kv(c, i, kv)
                    else:
                        p3_attn_kv(c, i, kv,
                                   pre_cb=(lambda kk=kv - 1: p3_epi_a(c, i, kk),
                                           lambda kk=kv - 1: p3_epi_b(c, i, kk)))
                p3_epi_a(c, i, NKV - 1)
                p3_epi_b(c, i, NKV - 1)
                c.m[("p3_pe_done", i)] = c.c["spe"]
                c.G.wait("sve", c.c["sve"])
                c.G.dma_start(out=yT_s.rearrange("c p t -> p c t")[:, :, i * 128:(i + 1) * 128],
                              in_=yTb[i % 2][:].rearrange("p (c t) -> p c t", c=16)).inc(spar, 16)
                c.m[("p3_st", i)] = c.c[spar]
            _barrier(c)

        _emit_all(blk, ctxs, p3)
        es3.close()

        es4 = contextlib.ExitStack()
        wp_bf = es4.enter_context(nc.sbuf_tensor("wp_bf", [128, 16 * D], BF16))
        xnTb4 = [es4.enter_context(nc.sbuf_tensor(f"xnTb4_{j}", [128, D], BF16)) for j in range(2)]
        yTb4 = [es4.enter_context(nc.sbuf_tensor(f"yTb4_{j}", [128, D], BF16)) for j in range(2)]
        xb4 = [es4.enter_context(nc.sbuf_tensor(f"xb4_{j}", [128, D], F32)) for j in range(2)]
        gbuf = es4.enter_context(nc.sbuf_tensor("gbuf", [128, D], F32))
        midf = es4.enter_context(nc.sbuf_tensor("midf", [128, NCAT], F32))
        hwbf = es4.enter_context(nc.sbuf_tensor("hwbf", [128, NCAT], BF16))
        hwT = es4.enter_context(nc.sbuf_tensor("hwT", [128, 128], BF16))
        ps_f = [es4.enter_context(nc.psum_tensor(f"ps_f{j}", [128, 512], F32)) for j in range(4)]
        ps_m = es4.enter_context(nc.psum_tensor("ps_m", [128, 512], F32))
        ps_h = es4.enter_context(nc.psum_tensor("ps_h", [128, 512], F32))
        ps_w = [es4.enter_context(nc.psum_tensor(f"ps_w{j}", [128, 512], F32)) for j in range(2)]

        W45 = 11 * 16

        def p45_setup(c):
            for fc in range(4):
                c.G.dma_start(
                    out=wp_bf[:].rearrange("p (ch d) -> p ch d", ch=16)[:, :, fc * 512:(fc + 1) * 512],
                    in_=wp_i.rearrange("p (ch d) -> p ch d", ch=16)[:, :, fc * 512:(fc + 1) * 512],
                ).inc("sw45", 16)

        def p45_tile(c, i):
            par = "sxa" if i % 2 == 0 else "sxb"
            spar = "soa" if i % 2 == 0 else "sob"
            if i >= 2:
                c.Y.wait("spe", c.m[("p45_pe_done", i - 2)])
                c.Y.wait("sve", c.m[("p45_v_done", i - 2)])
                c.Y.wait(spar, c.m[("p45_st", i - 2)])
            c.Y.dma_start(out=xnTb4[i % 2][:].rearrange("p (c t) -> p c t", c=16),
                          in_=xnT_s.rearrange("c p t -> p c t")[:, :, i * 128:(i + 1) * 128]
                          ).inc(par, 16)
            c.Y.dma_start(out=yTb4[i % 2][:].rearrange("p (c t) -> p c t", c=16),
                          in_=yT_s.rearrange("c p t -> p c t")[:, :, i * 128:(i + 1) * 128]
                          ).inc(par, 16)
            c.Y.dma_start(out=xb4[i % 2][:], in_=x_d[i * 128:(i + 1) * 128, :]).inc(par, 16)
            ld = c.c[par]

            c.T.wait(par, ld)
            if i == 0:
                c.T.wait("sw45", 9 * 16)
            if i >= 1:
                c.T.wait("sve", c.m[("p45_sgev", i - 1, 2)])
                for fc in range(4):
                    c.T.wait("sve", c.m[("p45_gev", i - 1, fc)])
            ins = None
            for ch in range(16):
                ins = c.T.matmul(ps_m[:, 0:NCAT], xnTb4[i % 2][:, ch * 128:(ch + 1) * 128],
                                 ac_bf[:, ch * NCAT:(ch + 1) * NCAT],
                                 start=(ch == 0), stop=(ch == 15), skip_group_check=True)
                for fc in range(4):
                    c.T.matmul(ps_f[fc][:], xnTb4[i % 2][:, ch * 128:(ch + 1) * 128],
                               wb_bf[:, ch * D + fc * 512: ch * D + (fc + 1) * 512],
                               start=(ch == 0), stop=False, skip_group_check=True)
            ins.inc("spe")
            mid_mark = c.c["spe"]

            c.V.wait("spe", mid_mark)
            c.V.tensor_copy(out=midf[:], in_=ps_m[:, 0:NCAT]).inc("sve")
            c.m[("p45_midev", i)] = c.c["sve"]
            c.V.wait("sve", c.c["sve"])
            for e in range(NE):
                c.V.tensor_scalar(out=midf[:, 16 + e * NR:16 + (e + 1) * NR],
                                  in0=midf[:, 16 + e * NR:16 + (e + 1) * NR],
                                  scalar1=w6[:, i * NE + e:i * NE + e + 1],
                                  scalar2=None, op0=ALU.mult).inc("sve")
            c.V.wait("sve", c.c["sve"])
            c.V.tensor_copy(out=hwbf[:], in_=midf[:]).inc("sve")
            hw_mark = c.c["sve"]
            c.T.wait("sve", hw_mark)
            if i >= 1:
                c.T.wait("sve", c.m[("p45_sgev", i - 1, 3)])
            c.T.matmul(ps_h[0:NCAT, 0:128], hwbf[:, 0:NCAT], ident[:],
                       start=True, stop=True, skip_group_check=True).inc("spe")
            c.V.wait("spe", c.c["spe"])
            c.V.tensor_copy(out=hwT[0:NCAT, :], in_=ps_h[0:NCAT, 0:128]).inc("sve")
            c.m[("p45_hwev", i)] = c.c["sve"]

            if i == 0:
                c.T.wait("sw45", W45 + 32)
            if i >= 1:
                c.T.wait("sve", c.m[("p45_sgev", i - 1, 0)])
                c.T.wait("sve", c.m[("p45_sgev", i - 1, 1)])
            ins = None
            for ch in range(16):
                for fc in (0, 1):
                    ins = c.T.matmul(ps_w[fc][:], yTb4[i % 2][:, ch * 128:(ch + 1) * 128],
                                     wp_bf[:, ch * D + fc * 512: ch * D + (fc + 1) * 512],
                                     start=(ch == 0), stop=(ch == 15), skip_group_check=True)
            ins.inc("spe")
            wp0_mark = c.c["spe"]

            c.T.wait("sve", c.m[("p45_hwev", i)])
            if i == 0:
                c.T.wait("sw45", 10 * 16)
            ins = None
            for fc in range(4):
                ins = c.T.matmul(ps_f[fc][:], hwT[0:NCAT, :],
                                 bc_bf[0:NCAT, fc * 512:(fc + 1) * 512],
                                 start=False, stop=True, skip_group_check=True)
            ins.inc("spe")
            ffn_mark = c.c["spe"]

            if i == 0:
                c.T.wait("sw45", W45 + 64)
            c.T.wait("sve", c.m[("p45_hwev", i)])
            ins = None
            for ch in range(16):
                for fc in (2, 3):
                    bank = ps_m if fc == 2 else ps_h
                    ins = c.T.matmul(bank[:], yTb4[i % 2][:, ch * 128:(ch + 1) * 128],
                                     wp_bf[:, ch * D + fc * 512: ch * D + (fc + 1) * 512],
                                     start=(ch == 0), stop=(ch == 15), skip_group_check=True)
            ins.inc("spe")
            wp1_mark = c.c["spe"]

            c.V.wait("spe", ffn_mark)
            if i == 0:
                c.V.wait("sw45", W45)
            silu_marks = {}

            def bias_silu(fc):
                sl = slice(fc * 512, (fc + 1) * 512)
                c.V.tensor_tensor(out=gbuf[:, sl], in0=ps_f[fc][:],
                                  in1=bb_sb[:, sl], op=ALU.add).inc("sve")
                c.m[("p45_gev", i, fc)] = c.c["sve"]
                c.S.wait("sve", c.c["sve"])
                c.S.activation(out=gbuf[:, sl], in_=gbuf[:, sl],
                               func=AF.Silu).inc("sse")
                silu_marks[fc] = c.c["sse"]

            for fc in range(4):
                bias_silu(fc)
            for fc in range(4):
                c.V.wait("spe", wp0_mark if fc < 2 else wp1_mark)
                c.V.wait("sse", silu_marks[fc])
                src = (ps_w[0], ps_w[1], ps_m, ps_h)[fc]
                sl = slice(fc * 512, (fc + 1) * 512)
                c.V.scalar_tensor_tensor(out=gbuf[:, sl], in0=src[:, 0:512],
                                         scalar=1.0, in1=gbuf[:, sl],
                                         op0=ALU.mult, op1=ALU.mult)
                c.V.tensor_tensor(out=xb4[i % 2][:, sl], in0=gbuf[:, sl],
                                  in1=xb4[i % 2][:, sl], op=ALU.add).inc("sve")
                c.m[("p45_sgev", i, fc)] = c.c["sve"]
            c.m[("p45_v_done", i)] = c.c["sve"]
            c.m[("p45_pe_done", i)] = c.c["spe"]

            c.G.wait("sve", c.c["sve"])
            c.G.dma_start(out=out_d[i * 128:(i + 1) * 128, :], in_=xb4[i % 2][:]).inc(spar, 16)
            c.m[("p45_st", i)] = c.c[spar]

        def p45(c):
            p45_setup(c)
            for i in range(nt):
                p45_tile(c, i)
            c.G.wait("soa", c.c["soa"])
            c.G.wait("sob", c.c["sob"])

        _emit_all(blk, ctxs, p45)
        es4.close()
        es34.close()

    return nc


_NC_CACHE = {}


def _get_nc():
    if "nc" not in _NC_CACHE:
        _NC_CACHE["nc"] = build_nc()
    return _NC_CACHE["nc"]


def _img(mat):
    n = mat.shape[1]
    return np.ascontiguousarray(
        mat.reshape(16, 128, n).transpose(1, 0, 2).reshape(128, 16 * n).astype(np.float32))


def prepare_shared(L0_attn_wq, L0_attn_wk, L0_attn_wv, L0_attn_wp, L0_attn_qg,
                   L0_ffn_W_base, L0_ffn_b_base, L0_ffn_A_s, L0_ffn_B_s, L0_ffn_A_r,
                   L0_ffn_B_r, L0_ffn_W_router, L0_ffn_b_router, **_ignore):
    wq_img = _img(np.asarray(L0_attn_wq, np.float32).T)
    wk_img = _img(np.asarray(L0_attn_wk, np.float32).T)
    wv_img = _img(np.asarray(L0_attn_wv, np.float32).T)
    wp_img = _img(np.asarray(L0_attn_wp, np.float32).T)
    wb_img = _img(np.asarray(L0_ffn_W_base, np.float32))
    wr_img = _img(np.asarray(L0_ffn_W_router, np.float32))
    A_r = np.asarray(L0_ffn_A_r, np.float32)
    acat = np.concatenate([np.asarray(L0_ffn_A_s, np.float32),
                           A_r.transpose(1, 0, 2).reshape(D, NE * NR)], axis=1)
    acat_img = _img(acat)
    bcat_img = np.ascontiguousarray(np.concatenate(
        [np.asarray(L0_ffn_B_s, np.float32),
         np.asarray(L0_ffn_B_r, np.float32).reshape(NE * NR, D)], axis=0))
    bb_rep = np.ascontiguousarray(np.tile(np.asarray(L0_ffn_b_base, np.float32)[None, :], (128, 1)))
    br_rep = np.ascontiguousarray(np.tile(np.asarray(L0_ffn_b_router, np.float32)[None, :], (128, 1)))
    qg_rep = np.ascontiguousarray(np.tile(np.asarray(L0_attn_qg, np.float32)[None, :], (128, 1)))

    ident_bf = np.eye(128, dtype=_bf)
    trikm1 = np.where(np.arange(128)[:, None] <= np.arange(128)[None, :],
                      np.float32(0.0), np.float32(NEGBIG)).astype(np.float32)
    trikm = np.ascontiguousarray(np.tile(trikm1, (1, 4)).astype(_bf))
    ones_cr = np.ones((128, 129), dtype=_bf)
    eps_c = np.ascontiguousarray(np.tile(np.array([[LN_EPS, RMS_EPS]], np.float32), (128, 1)))
    t_ = np.arange(T, dtype=np.float32)
    inv = (1.0 / (10000.0 ** (np.arange(0, PD, 2, dtype=np.float32) / PD))).astype(np.float32)
    f_ = t_[:, None] * inv[None, :]
    rope_tab = np.ascontiguousarray(np.concatenate(
        [np.tile(np.cos(f_).astype(np.float32), (1, 16)),
         np.tile(np.sin(f_).astype(np.float32), (1, 16))], axis=1))

    return dict(wq_img=wq_img, wk_img=wk_img, wv_img=wv_img, wp_img=wp_img,
                wb_img=wb_img, wr_img=wr_img, acat_img=acat_img, bcat_img=bcat_img,
                bb_rep=bb_rep, br_rep=br_rep, qg_rep=qg_rep, ident_bf=ident_bf,
                trikm=trikm, ones_cr=ones_cr, eps_c=eps_c, rope_tab=rope_tab)


def kernel(**inputs):
    x = np.asarray(inputs["L0_attn_x"], dtype=np.float32)
    shared = prepare_shared(**inputs)
    in_maps = [dict(shared, x=np.ascontiguousarray(x[b])) for b in range(B)]
    nc = _get_nc()
    res = run_bass_kernel_spmd(nc, in_maps, core_ids=list(range(B)))
    return np.stack([np.asarray(res.results[b]["out"], dtype=np.float32) for b in range(B)],
                    axis=0)


# version 22
# speedup vs baseline: 1.0916x; 1.0026x over previous
import contextlib
import math

import numpy as np
import ml_dtypes

import concourse.bass as bass
import concourse.mybir as mybir
from concourse.bass_utils import run_bass_kernel_spmd

F32 = mybir.dt.float32
BF16 = mybir.dt.bfloat16
ALU = mybir.AluOpType
AF = mybir.ActivationFunctionType
AX = mybir.AxisListType
_bf = ml_dtypes.bfloat16

B, T, D = 8, 4096, 2048
NT = T // 128
NH, NKV, HD, PD = 16, 4, 128, 16
GQ = NH // NKV
NE, NR = 6, 16
NCAT = 16 + NE * NR
SCALE = 1.0 / math.sqrt(HD)
LN_EPS = 1e-5
RMS_EPS = float(np.finfo(np.float32).eps)
NEGBIG = -1.0e30

ENGINES = ("sync", "gpsimd", "tensor", "vector", "scalar")
SEM_NAMES = ("spe", "sve", "sse",
             "sxa", "sxb",
             "soa", "sob",
             "scst", "sw2", "sw3", "sw45")


class _Ins:
    __slots__ = ("ctx", "real")

    def __init__(self, ctx, real):
        self.ctx, self.real = ctx, real

    def inc(self, sem_name, k=1):
        self.ctx.c[sem_name] += k
        if self.real is not None:
            self.real.then_inc(self.ctx.sems[sem_name], k)
        return self


class _Eng:
    def __init__(self, ctx, name):
        object.__setattr__(self, "ctx", ctx)
        object.__setattr__(self, "name", name)
        object.__setattr__(self, "real", None)

    def wait(self, sem_name, value):
        if self.real is not None and value > 0:
            self.real.wait_ge(self.ctx.sems[sem_name], value)

    def __getattr__(self, op):
        ctx = self.ctx
        real = self.real

        def f(*a, **kw):
            ri = getattr(real, op)(*a, **kw) if real is not None else None
            return _Ins(ctx, ri)
        return f


class Ctx:
    def __init__(self, sems, active):
        self.sems = sems
        self.active = active
        self.c = {k: 0 for k in SEM_NAMES}
        self.m = {}
        self.T = _Eng(self, "tensor")
        self.V = _Eng(self, "vector")
        self.S = _Eng(self, "scalar")
        self.Y = _Eng(self, "sync")
        self.G = _Eng(self, "gpsimd")
        self.by_name = {"tensor": self.T, "vector": self.V, "scalar": self.S,
                        "sync": self.Y, "gpsimd": self.G}

    def bind(self, eng):
        object.__setattr__(self.by_name[self.active], "real", eng)

    def unbind(self):
        object.__setattr__(self.by_name[self.active], "real", None)


def _emit_all(blk, ctxs, fn):
    for name in ENGINES:
        c = ctxs[name]

        def body(eng, c=c):
            c.bind(eng)
            fn(c)
            c.unbind()
        getattr(blk, name)(body)


def _barrier(c):
    snap = dict(c.c)
    for e in (c.T, c.V, c.S, c.Y, c.G):
        for sn in SEM_NAMES:
            if snap[sn] > 0:
                e.wait(sn, snap[sn])


def build_nc(nt=NT, dbg=False):
    nc = bass.Bass("TRN2", target_bir_lowering=False)

    def din(name, shape, dt=F32):
        return nc.dram_tensor(name, shape, dt, kind="ExternalInput").ap()

    x_d = din("x", [T, D])
    wq_i = din("wq_img", [128, 16 * D])
    wk_i = din("wk_img", [128, 16 * 512])
    wv_i = din("wv_img", [128, 16 * 512])
    wp_i = din("wp_img", [128, 16 * D])
    wb_i = din("wb_img", [128, 16 * D])
    wr_i = din("wr_img", [128, 16 * NE])
    ac_i = din("acat_img", [128, 16 * NCAT])
    bc_i = din("bcat_img", [NCAT, D])
    bb_i = din("bb_rep", [128, D])
    br_i = din("br_rep", [128, NE])
    qg_i = din("qg_rep", [128, NH])
    id_i = din("ident_bf", [128, 128], BF16)
    trikm_i = din("trikm", [128, 512], BF16)
    ones_i = din("ones_cr", [128, 129], BF16)
    eps_i = din("eps_c", [128, 2])
    rope_i = din("rope_tab", [T, 256])

    out_d = nc.dram_tensor("out", [T, D], F32, kind="ExternalOutput").ap()
    xnT_s = nc.dram_tensor("xnT_s", [16, 128, T], BF16, kind="Internal").ap()
    yT_s = nc.dram_tensor("yT_s", [16, 128, T], BF16, kind="Internal").ap()
    qT_s = nc.dram_tensor("qT_s", [16, 128, T], BF16, kind="Internal").ap()
    kT_s = nc.dram_tensor("kT_s", [128, NKV * T], BF16, kind="Internal").ap()
    v_s = nc.dram_tensor("v_s", [128, NT * 512], BF16, kind="Internal").ap()

    with contextlib.ExitStack() as es:
        sems = {n: es.enter_context(nc.semaphore(n)) for n in SEM_NAMES}
        blk = es.enter_context(nc.Block())
        ctxs = {n: Ctx(sems, n) for n in ENGINES}

        ident = es.enter_context(nc.sbuf_tensor("ident", [128, 128], BF16))
        trikm_sb = es.enter_context(nc.sbuf_tensor("trikm_sb", [128, 512], BF16))
        ones_cr = es.enter_context(nc.sbuf_tensor("ones_sb", [128, 129], BF16))
        epsc = es.enter_context(nc.sbuf_tensor("epsc", [128, 2], F32))
        qg_sb = es.enter_context(nc.sbuf_tensor("qg_sb", [128, NH], F32))
        br_sb = es.enter_context(nc.sbuf_tensor("br_sb", [128, NE], F32))
        w6 = es.enter_context(nc.sbuf_tensor("w6", [128, NT * NE], F32))
        st1 = es.enter_context(nc.sbuf_tensor("st1", [128, 16], F32))
        st2 = es.enter_context(nc.sbuf_tensor("st2", [128, 16], F32))
        st3 = es.enter_context(nc.sbuf_tensor("st3", [128, 16], F32))
        nmean = es.enter_context(nc.sbuf_tensor("nmean", [128, 1], F32))
        nmist = es.enter_context(nc.sbuf_tensor("nmist", [128, 1], F32))

        es2 = contextlib.ExitStack()
        wq_bf = es2.enter_context(nc.sbuf_tensor("wq_bf", [128, 16 * D], BF16))
        wk_bf = es2.enter_context(nc.sbuf_tensor("wk_bf", [128, 16 * 512], BF16))
        wv_bf = es2.enter_context(nc.sbuf_tensor("wv_bf", [128, 16 * 512], BF16))
        wr_bf = es2.enter_context(nc.sbuf_tensor("wr_bf", [128, 16 * NE], BF16))
        xb2 = [es2.enter_context(nc.sbuf_tensor(f"xb2_{j}", [128, D], F32)) for j in range(2)]
        xnbf = [es2.enter_context(nc.sbuf_tensor(f"xnbf{j}", [128, D], BF16)) for j in range(2)]
        xnTb2 = [es2.enter_context(nc.sbuf_tensor(f"xnTb2_{j}", [128, D], BF16)) for j in range(2)]
        ropeb = [es2.enter_context(nc.sbuf_tensor(f"ropeb{j}", [128, 256], F32)) for j in range(2)]
        rtmp = es2.enter_context(nc.sbuf_tensor("rtmp", [128, 512], F32))
        ktm = es2.enter_context(nc.sbuf_tensor("ktm", [128, 512], F32))
        kbf = es2.enter_context(nc.sbuf_tensor("kbf", [128, 512], BF16))
        kTb = [es2.enter_context(nc.sbuf_tensor(f"kTb{j}", [128, 512], BF16)) for j in range(2)]
        vb = [es2.enter_context(nc.sbuf_tensor(f"vb{j}", [128, 512], BF16)) for j in range(2)]
        qbf = es2.enter_context(nc.sbuf_tensor("qbf", [128, D], BF16))
        qsq = es2.enter_context(nc.sbuf_tensor("qsq", [128, D], BF16))
        qTb = [es2.enter_context(nc.sbuf_tensor(f"qTb{j}", [128, D], BF16)) for j in range(2)]
        sc16 = es2.enter_context(nc.sbuf_tensor("sc16", [128, NH], F32))
        rf6 = es2.enter_context(nc.sbuf_tensor("rf6", [128, NE], F32))
        g6a = es2.enter_context(nc.sbuf_tensor("g6a", [128, NE], F32))
        g6b = es2.enter_context(nc.sbuf_tensor("g6b", [128, NE], F32))
        z6 = es2.enter_context(nc.sbuf_tensor("z6", [128, 1], F32))
        m6 = es2.enter_context(nc.sbuf_tensor("m6", [128, 1], F32))
        ps_k = [es2.enter_context(nc.psum_tensor(f"ps_k{j}", [128, 512], F32)) for j in range(2)]
        ps_v = [es2.enter_context(nc.psum_tensor(f"ps_v{j}", [128, 512], F32)) for j in range(2)]
        ps_r = [es2.enter_context(nc.psum_tensor(f"ps_r{j}", [128, 512], F32)) for j in range(2)]
        ps_xt = [es2.enter_context(nc.psum_tensor(f"ps_xt{j}", [128, 512], F32)) for j in range(2)]

        def setup(c):
            c.G.dma_start(out=ident[:], in_=id_i).inc("scst", 16)
            c.G.dma_start(out=trikm_sb[:], in_=trikm_i).inc("scst", 16)
            c.G.dma_start(out=ones_cr[:], in_=ones_i).inc("scst", 16)
            c.G.dma_start(out=qg_sb[:], in_=qg_i).inc("scst", 16)
            c.G.dma_start(out=br_sb[:], in_=br_i).inc("scst", 16)
            c.G.dma_start(out=epsc[:], in_=eps_i).inc("scst", 16)
            c.G.dma_start(out=wk_bf[:], in_=wk_i).inc("sw2", 16)
            c.G.dma_start(out=wv_bf[:], in_=wv_i).inc("sw2", 16)
            c.G.dma_start(out=wr_bf[:], in_=wr_i).inc("sw2", 16)
            for fc in range(4):
                c.G.dma_start(
                    out=wq_bf[:].rearrange("p (ch d) -> p ch d", ch=16)[:, :, fc * 512:(fc + 1) * 512],
                    in_=wq_i.rearrange("p (ch d) -> p ch d", ch=16)[:, :, fc * 512:(fc + 1) * 512],
                ).inc("sw3", 16)

        _emit_all(blk, ctxs, setup)
        CONSTS = 6 * 16
        W2 = 3 * 16

        def bw(c, bank):
            prev = c.m.get(("b", bank))
            if prev is not None:
                c.T.wait(prev[0], prev[1])

        def bank_of(bid):
            return {"k0": ps_k[0], "k1": ps_k[1], "v0": ps_v[0], "v1": ps_v[1],
                    "r0": ps_r[0], "r1": ps_r[1], "x0": ps_xt[0], "x1": ps_xt[1]}[bid]

        def p2_A(c, i):
            par = "sxa" if i % 2 == 0 else "sxb"
            if i >= 2:
                c.Y.wait("sse", c.m[("p2_s_done", i - 2)])
                c.Y.wait("sve", c.m[("p2_qprep", i - 2)])
            c.Y.dma_start(out=xb2[i % 2][:], in_=x_d[i * 128:(i + 1) * 128, :]).inc(par, 16)
            c.Y.dma_start(out=ropeb[i % 2][:], in_=rope_i[i * 128:(i + 1) * 128, :]).inc(par, 16)
            ld = c.c[par]

            c.V.wait(par, ld)
            c.V.reduce_sum(out=nmean[:], in_=xb2[i % 2][:], axis=AX.X, negate=True).inc("sve")
            c.V.wait("sve", c.c["sve"])
            c.V.tensor_scalar(out=nmean[:], in0=nmean[:], scalar1=1.0 / D, scalar2=None,
                              op0=ALU.mult).inc("sve")
            c.S.wait("sve", c.c["sve"])
            c.S.wait(par, ld)
            c.S.activation(out=xnbf[i % 2][:], in_=xb2[i % 2][:], func=AF.Square,
                           bias=nmean[:], scale=1.0, accum_out=st1[:, 15:16])
            c.S.activation(out=st3[:, 14:15], in_=st1[:, 15:16], func=AF.Ln,
                           scale=1.0 / D, bias=epsc[:, 0:1])
            c.S.activation(out=st3[:, 15:16], in_=st3[:, 14:15], func=AF.Exp,
                           scale=-0.5).inc("sse")
            c.V.wait("sse", c.c["sse"])
            c.V.tensor_tensor(out=nmist[:], in0=nmean[:], in1=st3[:, 15:16], op=ALU.mult).inc("sve")
            c.S.wait("sve", c.c["sve"])
            c.S.activation(out=xnbf[i % 2][:], in_=xb2[i % 2][:], func=AF.Identity,
                           scale=st3[:, 15:16], bias=nmist[:]).inc("sse")
            c.m[("p2_s_done", i)] = c.c["sse"]
            c.m[("p2_ln", i)] = c.c["sse"]

        def p2_XTtr(c, i):
            par = "sxa" if i % 2 == 0 else "sxb"
            spar = "soa" if i % 2 == 0 else "sob"
            for g in range(4):
                c.T.wait("sse", c.m[("p2_ln", i)])
                if i == 0 and g == 0:
                    c.T.wait("scst", CONSTS)
                bid = "x%d" % (g % 2)
                bw(c, bid)
                ins = None
                for q in range(4):
                    ch = g * 4 + q
                    ins = c.T.matmul(ps_xt[g % 2][:, q * 128:(q + 1) * 128],
                                     xnbf[i % 2][:, ch * 128:(ch + 1) * 128], ident[:],
                                     start=True, stop=True, skip_group_check=True)
                ins.inc("spe")
                eng = c.S if g % 2 == 0 else c.V
                sem = "sse" if g % 2 == 0 else "sve"
                eng.wait("spe", c.c["spe"])
                if g < 2 and i >= 2:
                    eng.wait(spar, c.m[("p2_st", i - 2)])
                    eng.wait("spe", c.m[("p2_qc_done", i - 2)])
                if g % 2 == 0:
                    c.S.activation(out=xnTb2[i % 2][:, g * 512:(g + 1) * 512],
                                   in_=ps_xt[g % 2][:], func=AF.Identity).inc("sse")
                else:
                    c.V.tensor_copy(out=xnTb2[i % 2][:, g * 512:(g + 1) * 512],
                                    in_=ps_xt[g % 2][:]).inc("sve")
                c.m[("b", bid)] = (sem, c.c[sem])
            c.m[("p2_xnT_s", i)] = c.c["sse"]
            c.m[("p2_xnT_v", i)] = c.c["sve"]

            c.G.wait("sse", c.c["sse"])
            c.G.wait("sve", c.c["sve"])
            c.G.dma_start(out=xnT_s.rearrange("c p t -> p c t")[:, :, i * 128:(i + 1) * 128],
                          in_=xnTb2[i % 2][:].rearrange("p (c t) -> p c t", c=16)).inc(spar, 16)
            c.m[("p2_st", i)] = c.c[spar]

        def p2_KVR(c, i):
            c.T.wait("sw2", W2)
            c.T.wait("sse", c.m[("p2_xnT_s", i)])
            c.T.wait("sve", c.m[("p2_xnT_v", i)])
            for bid in ("k%d" % (i % 2), "v%d" % (i % 2), "r%d" % (i % 2)):
                bw(c, bid)
            ins = None
            for ch in range(16):
                xc = xnTb2[i % 2][:, ch * 128:(ch + 1) * 128]
                c.T.matmul(ps_k[i % 2][:], xc, wk_bf[:, ch * 512:(ch + 1) * 512],
                           start=(ch == 0), stop=(ch == 15), skip_group_check=True)
                c.T.matmul(ps_v[i % 2][:], xc, wv_bf[:, ch * 512:(ch + 1) * 512],
                           start=(ch == 0), stop=(ch == 15), skip_group_check=True)
                ins = c.T.matmul(ps_r[i % 2][:, 0:NE], xc, wr_bf[:, ch * NE:(ch + 1) * NE],
                                 start=(ch == 0), stop=(ch == 15), skip_group_check=True)
            ins.inc("spe")
            c.m[("p2_kvr", i)] = c.c["spe"]

        def p2_KP(c, i):
            par = "sxa" if i % 2 == 0 else "sxb"
            spar = "soa" if i % 2 == 0 else "sob"
            kvr = c.m[("p2_kvr", i)]
            c.V.wait("spe", kvr)
            c.V.tensor_copy(out=rf6[:], in_=ps_r[i % 2][:, 0:NE]).inc("sve")
            c.m[("b", "r%d" % (i % 2))] = ("sve", c.c["sve"])
            c.S.wait("spe", kvr)
            if i >= 1:
                c.S.wait("sve", c.m[("p2_kbf", i - 1)])
            c.S.activation(out=ktm[:], in_=ps_k[i % 2][:], func=AF.Identity).inc("sse")
            c.m[("b", "k%d" % (i % 2))] = ("sse", c.c["sse"])
            if i >= 2:
                c.S.wait(spar, c.m[("p2_st_kv", i - 2)])
            c.S.activation(out=vb[i % 2][:], in_=ps_v[i % 2][:], func=AF.Identity).inc("sse")
            c.m[("b", "v%d" % (i % 2))] = ("sse", c.c["sse"])
            c.G.wait("sse", c.c["sse"])
            c.G.dma_start(out=v_s[:, i * 512:(i + 1) * 512], in_=vb[i % 2][:]).inc(spar, 16)

            c.S.activation(out=kbf[:], in_=ktm[:], func=AF.Square).inc("sse")
            c.V.wait("sse", c.c["sse"])
            c.V.tensor_reduce(out=st1[:, 0:NKV],
                              in_=kbf[:].rearrange("p (h d) -> p h d", h=NKV),
                              axis=AX.X, op=ALU.add).inc("sve")
            c.S.wait("sve", c.c["sve"])
            c.S.activation(out=st3[:, 0:NKV], in_=st1[:, 0:NKV], func=AF.Ln,
                           scale=1.0 / HD, bias=epsc[:, 1:2])
            c.S.activation(out=st3[:, 0:NKV], in_=st3[:, 0:NKV], func=AF.Exp,
                           scale=-0.5).inc("sse")
            c.V.wait("sse", c.c["sse"])
            for h in range(NKV):
                c.V.tensor_scalar(out=ktm[:, h * 128:(h + 1) * 128],
                                  in0=ktm[:, h * 128:(h + 1) * 128],
                                  scalar1=st3[:, h:h + 1], scalar2=None, op0=ALU.mult)
            kv_ = ktm[:].rearrange("p (h d) -> p h d", h=NKV)
            co = ropeb[i % 2][:, 0:32].rearrange("p (h d) -> p h d", h=NKV)
            si = ropeb[i % 2][:, 128:160].rearrange("p (h d) -> p h d", h=NKV)
            x1, x2 = kv_[:, :, 0:8], kv_[:, :, 8:16]
            r_ = rtmp[:].rearrange("p (h d) -> p h d", h=64)
            r1, r2, r3, r4 = r_[:, 0:4], r_[:, 4:8], r_[:, 8:12], r_[:, 12:16]
            c.V.tensor_tensor(out=r1, in0=x1, in1=co, op=ALU.mult)
            c.V.tensor_tensor(out=r2, in0=x1, in1=si, op=ALU.mult)
            c.V.tensor_tensor(out=r3, in0=x2, in1=si, op=ALU.mult)
            c.V.tensor_tensor(out=x1, in0=r1, in1=r3, op=ALU.add)
            c.V.tensor_tensor(out=r4, in0=x2, in1=co, op=ALU.mult)
            c.V.tensor_tensor(out=x2, in0=r2, in1=r4, op=ALU.subtract).inc("sve")
            c.V.wait("sve", c.c["sve"])
            c.V.tensor_copy(out=kbf[:], in_=ktm[:]).inc("sve")
            c.m[("p2_kbf", i)] = c.c["sve"]

            def vs(ins):
                ins.inc("sve")
                c.V.wait("sve", c.c["sve"])
            vs(c.V.tensor_tensor(out=rf6[:], in0=rf6[:], in1=br_sb[:], op=ALU.add))
            c.V.reduce_max(out=m6[:], in_=rf6[:], axis=AX.X, negate=True).inc("sve")
            c.S.wait("sve", c.c["sve"])
            c.S.activation(out=g6a[:], in_=rf6[:], func=AF.Exp, bias=m6[:],
                           accum_out=z6[:]).inc("sse")
            c.V.wait("sse", c.c["sse"])
            vs(c.V.reciprocal(out=m6[:], in_=z6[:]))
            vs(c.V.tensor_scalar(out=g6a[:], in0=g6a[:], scalar1=m6[:], scalar2=None,
                                 op0=ALU.mult))
            vs(c.V.reduce_max(out=m6[:], in_=g6a[:], axis=AX.X))
            vs(c.V.tensor_scalar(out=g6b[:], in0=g6a[:], scalar1=m6[:], scalar2=None,
                                 op0=ALU.is_ge))
            vs(c.V.scalar_tensor_tensor(out=g6b[:], in0=g6b[:], scalar=-2.0, in1=g6a[:],
                                        op0=ALU.mult, op1=ALU.add))
            vs(c.V.reduce_max(out=m6[:], in_=g6b[:], axis=AX.X))
            vs(c.V.tensor_scalar(out=g6b[:], in0=g6a[:], scalar1=m6[:], scalar2=None,
                                 op0=ALU.is_ge))
            c.V.tensor_tensor(out=w6[:, i * NE:(i + 1) * NE], in0=g6a[:], in1=g6b[:],
                              op=ALU.mult).inc("sve")
            c.m[("p2_v_done", i)] = c.c["sve"]

        def p2_QC(c, i):
            par = i % 2
            qb = ["k%d" % (1 - par), "v%d" % (1 - par), "r%d" % (1 - par), "r%d" % par]
            if i == 0:
                c.T.wait("sw3", 64)
            for pair in range(2):
                f0, f1 = 2 * pair, 2 * pair + 1
                bw(c, qb[f0])
                bw(c, qb[f1])
                ins = None
                for ch in range(16):
                    xc = xnTb2[i % 2][:, ch * 128:(ch + 1) * 128]
                    c.T.matmul(bank_of(qb[f0])[:], xc,
                               wq_bf[:, ch * D + f0 * 512: ch * D + (f0 + 1) * 512],
                               start=(ch == 0), stop=(ch == 15), skip_group_check=True)
                    ins = c.T.matmul(bank_of(qb[f1])[:], xc,
                                     wq_bf[:, ch * D + f1 * 512: ch * D + (f1 + 1) * 512],
                                     start=(ch == 0), stop=(ch == 15), skip_group_check=True)
                ins.inc("spe")
                c.S.wait("spe", c.c["spe"])
                if pair == 0 and i >= 1:
                    c.S.wait("spe", c.m[("p2_qtr", i - 1)])
                c.S.activation(out=qbf[:, f0 * 512:(f0 + 1) * 512],
                               in_=bank_of(qb[f0])[:], func=AF.Identity).inc("sse")
                c.m[("b", qb[f0])] = ("sse", c.c["sse"])
                c.S.activation(out=qbf[:, f1 * 512:(f1 + 1) * 512],
                               in_=bank_of(qb[f1])[:], func=AF.Identity).inc("sse")
                c.m[("b", qb[f1])] = ("sse", c.c["sse"])
                c.S.wait("sve", c.c["sve"])
                c.S.activation(out=qsq[:, pair * 1024:(pair + 1) * 1024],
                               in_=qbf[:, pair * 1024:(pair + 1) * 1024],
                               func=AF.Square).inc("sse")
                c.V.wait("sse", c.c["sse"])
                c.V.tensor_reduce(
                    out=st2[:, pair * 8:(pair + 1) * 8],
                    in_=qsq[:, pair * 1024:(pair + 1) * 1024].rearrange(
                        "p (h d) -> p h d", h=8),
                    axis=AX.X, op=ALU.add).inc("sve")
            c.m[("p2_qc_done", i)] = c.c["spe"]
            c.m[("p2_qstat", i)] = c.c["sve"]

        def p2_KT(c, i):
            spar = "soa" if i % 2 == 0 else "sob"
            c.T.wait("sve", c.m[("p2_kbf", i)])
            bw(c, "x0")
            ins = None
            for h in range(NKV):
                ins = c.T.matmul(ps_xt[0][:, h * 128:(h + 1) * 128],
                                 kbf[:, h * 128:(h + 1) * 128], ident[:],
                                 start=True, stop=True, skip_group_check=True)
            ins.inc("spe")
            c.m[("p2_ktr", i)] = c.c["spe"]
            c.S.wait("spe", c.c["spe"])
            if i >= 2:
                c.S.wait(spar, c.m[("p2_st_kv", i - 2)])
            c.S.activation(out=kTb[i % 2][:], in_=ps_xt[0][:], func=AF.Identity).inc("sse")
            c.m[("b", "x0")] = ("sse", c.c["sse"])
            c.G.wait("sse", c.c["sse"])
            c.G.dma_start(
                out=kT_s.rearrange("p (h t) -> p h t", h=NKV)[:, :, i * 128:(i + 1) * 128],
                in_=kTb[i % 2][:].rearrange("p (h t) -> p h t", h=NKV)).inc(spar, 16)
            c.m[("p2_st_kv", i)] = c.c[spar]

        def p2_QP(c, i):
            par = "sxa" if i % 2 == 0 else "sxb"
            c.S.wait("sve", c.m[("p2_qstat", i)])
            c.S.activation(out=st3[:], in_=st2[:], func=AF.Ln,
                           scale=1.0 / HD, bias=epsc[:, 1:2])
            c.S.activation(out=st3[:], in_=st3[:], func=AF.Exp, scale=-0.5).inc("sse")
            c.V.wait("sse", c.c["sse"])
            c.V.tensor_tensor(out=sc16[:], in0=st3[:], in1=qg_sb[:], op=ALU.mult).inc("sve")
            c.V.wait("sve", c.c["sve"])
            for h in range(NH):
                c.V.tensor_scalar(out=qbf[:, h * 128:(h + 1) * 128],
                                  in0=qbf[:, h * 128:(h + 1) * 128],
                                  scalar1=sc16[:, h:h + 1], scalar2=None, op0=ALU.mult)
            qv = qbf[:].rearrange("p (h d) -> p h d", h=NH)
            co = ropeb[i % 2][:, 0:128].rearrange("p (h d) -> p h d", h=NH)
            si = ropeb[i % 2][:, 128:256].rearrange("p (h d) -> p h d", h=NH)
            x1, x2 = qv[:, :, 0:8], qv[:, :, 8:16]
            r_ = rtmp[:].rearrange("p (h d) -> p h d", h=64)
            r1, r2, r3, r4 = r_[:, 0:16], r_[:, 16:32], r_[:, 32:48], r_[:, 48:64]
            c.V.tensor_tensor(out=r1, in0=x1, in1=co, op=ALU.mult)
            c.V.tensor_tensor(out=r2, in0=x1, in1=si, op=ALU.mult)
            c.V.tensor_tensor(out=r3, in0=x2, in1=si, op=ALU.mult)
            c.V.tensor_tensor(out=x1, in0=r1, in1=r3, op=ALU.add)
            c.V.tensor_tensor(out=r4, in0=x2, in1=co, op=ALU.mult)
            c.V.tensor_tensor(out=x2, in0=r2, in1=r4, op=ALU.subtract).inc("sve")
            c.m[("p2_qprep", i)] = c.c["sve"]

        def p2_QT(c, i):
            spar = "soa" if i % 2 == 0 else "sob"
            c.T.wait("sve", c.m[("p2_qprep", i)])
            for g in range(4):
                bid = "x%d" % ((g + 1) % 2)
                bw(c, bid)
                ins = None
                for hh in range(4):
                    h = g * 4 + hh
                    ins = c.T.matmul(ps_xt[(g + 1) % 2][:, hh * 128:(hh + 1) * 128],
                                     qbf[:, h * 128:(h + 1) * 128], ident[:],
                                     start=True, stop=True, skip_group_check=True)
                ins.inc("spe")
                c.S.wait("spe", c.c["spe"])
                if g == 0 and i >= 2:
                    c.S.wait(spar, c.m[("p2_st_q", i - 2)])
                c.S.activation(out=qTb[i % 2][:, g * 512:(g + 1) * 512],
                               in_=ps_xt[(g + 1) % 2][:], func=AF.Identity).inc("sse")
                c.m[("b", bid)] = ("sse", c.c["sse"])
            c.m[("p2_qtr", i)] = c.c["spe"]
            c.G.wait("sse", c.c["sse"])
            c.G.dma_start(out=qT_s.rearrange("c p t -> p c t")[:, :, i * 128:(i + 1) * 128],
                          in_=qTb[i % 2][:].rearrange("p (c t) -> p c t", c=16)).inc(spar, 16)
            c.m[("p2_st_q", i)] = c.c[spar]

        def p2(c):
            p2_A(c, 0)
            p2_XTtr(c, 0)
            p2_KVR(c, 0)
            for i in range(nt):
                p2_KP(c, i)
                if i + 1 < nt:
                    p2_A(c, i + 1)
                p2_QC(c, i)
                p2_KT(c, i)
                if i + 1 < nt:
                    p2_XTtr(c, i + 1)
                    p2_KVR(c, i + 1)
                p2_QP(c, i)
                p2_QT(c, i)
            _barrier(c)

        _emit_all(blk, ctxs, p2)
        es2.close()

        es34 = contextlib.ExitStack()
        wb_bf = es34.enter_context(nc.sbuf_tensor("wb_bf", [128, 16 * D], BF16))
        ac_bf = es34.enter_context(nc.sbuf_tensor("ac_bf", [128, 16 * NCAT], BF16))
        bc_bf = es34.enter_context(nc.sbuf_tensor("bc_bf", [NCAT, D], BF16))
        bb_sb = es34.enter_context(nc.sbuf_tensor("bb_sb", [128, D], F32))

        es3 = contextlib.ExitStack()
        kcache = es3.enter_context(nc.sbuf_tensor("kcache", [128, NKV * T], BF16))
        vcache = es3.enter_context(nc.sbuf_tensor("vcache", [128, NT * 512], BF16))
        qTb3 = [es3.enter_context(nc.sbuf_tensor(f"qTb3_{j}", [128, D], BF16)) for j in range(2)]
        pT = es3.enter_context(nc.sbuf_tensor("pT", [128, 12 * 512], BF16))
        yTb = [es3.enter_context(nc.sbuf_tensor(f"yTb{j}", [128, D], BF16)) for j in range(2)]
        nb1 = es3.enter_context(nc.sbuf_tensor("nb1", [128, 1], F32))
        zrow = es3.enter_context(nc.sbuf_tensor("zrow", [1, 512], F32))
        zrec = es3.enter_context(nc.sbuf_tensor("zrec", [1, D], BF16))
        rzbc = es3.enter_context(nc.sbuf_tensor("rzbc", [128, 512], F32))
        ps_d = [es3.enter_context(nc.psum_tensor(f"ps_d{j}", [128, 1024], F32)) for j in range(2)]
        ps_yu = [es3.enter_context(nc.psum_tensor(f"ps_yu{j}", [128, 512], F32)) for j in range(2)]
        ps_z = [es3.enter_context(nc.psum_tensor(f"ps_z{j}", [128, 512], F32)) for j in range(2)]

        def _bkwait(c, eng, bid):
            prev = c.m.get(("p3bk", bid))
            if prev is not None:
                eng.wait(prev[0], prev[1])

        def p3_attn_kv(c, i, kv, pre_cb=None):
            nst = i + 1
            NSL = 12
            npair = (nst + 1) // 2

            def emit_pair(p):
                s0 = 2 * p
                s1 = min(2 * p + 1, nst - 1)
                w = s1 - s0 + 1
                bk = ps_d[p % 2]
                _bkwait(c, c.T, ("d", p % 2))
                if p == 0 and kv == 0:
                    par = "sxa" if i % 2 == 0 else "sxb"
                    c.T.wait(par, c.m[("p3_ld", i)])
                ins = None
                for s in range(s0, s1 + 1):
                    last = (s == nst - 1)
                    ins = c.T.matmul(bk[:, (s - s0) * 512:(s - s0 + 1) * 512],
                                     kcache[:, kv * T + s * 128: kv * T + (s + 1) * 128],
                                     qTb3[i % 2][:, kv * 512:(kv + 1) * 512],
                                     start=True, stop=not last, skip_group_check=True)
                    if last:
                        ins = c.T.matmul(bk[:, (s - s0) * 512:(s - s0 + 1) * 512],
                                         ident[:], trikm_sb[:],
                                         start=False, stop=True, skip_group_check=True)
                ins.inc("spe")
                c.S.wait("spe", c.c["spe"])
                prev = c.m.get(("p3_pt", s0 % NSL))
                if prev is not None:
                    c.S.wait("spe", prev)
                c.S.activation(out=pT[:, (s0 % NSL) * 512:((s0 % NSL) + w) * 512],
                               in_=bk[:, 0:w * 512],
                               func=AF.Exp, bias=nb1[:], scale=SCALE).inc("sse")
                c.m[("p3bk", ("d", p % 2))] = ("sse", c.c["sse"])
                c.m[("p3_expp", p)] = c.c["sse"]

            def emit_pv(s, need_wait):
                last = (s == nst - 1)
                if need_wait:
                    c.T.wait("sse", c.m[("p3_expp", s // 2)])
                if s == 0:
                    _bkwait(c, c.T, ("yu", kv % 2))
                c.T.matmul(ps_yu[kv % 2][:],
                           vcache[:, s * 512 + kv * 128: s * 512 + (kv + 1) * 128],
                           pT[:, (s % NSL) * 512:(s % NSL + 1) * 512],
                           start=(s == 0), stop=last, skip_group_check=True).inc("spe")

            def emit_zbatch(s0, s1):
                c.T.wait("sse", c.m[("p3_expp", (s1 - 1) // 2)])
                if s0 == 0:
                    _bkwait(c, c.T, ("z", kv % 2))
                ins = None
                for s in range(s0, s1):
                    ins = c.T.matmul(ps_z[kv % 2][0:1, :], ones_cr[:, 0:1],
                                     pT[:, (s % NSL) * 512:(s % NSL + 1) * 512],
                                     start=(s == 0), stop=(s == nst - 1),
                                     skip_group_check=True)
                ins.inc("spe")
                for s in range(s0, s1):
                    c.m[("p3_pt", s % NSL)] = c.c["spe"]

            emit_pair(0)
            if npair > 1:
                emit_pair(1)
            zb = 0
            for p in range(npair):
                if p + 2 < npair:
                    emit_pair(p + 2)
                if p == 0 and pre_cb is not None:
                    pre_cb[0]()
                if p == min(4, npair - 1) and pre_cb is not None:
                    pre_cb[1]()
                s0 = 2 * p
                s1 = min(2 * p + 1, nst - 1)
                emit_pv(s0, True)
                if s1 > s0:
                    emit_pv(s1, False)
                if s1 - zb >= 6 or s1 == nst - 1:
                    emit_zbatch(zb, s1 + 1)
                    zb = s1 + 1
            c.m[("p3_pvz", kv)] = c.c["spe"]

        def p3_epi_a(c, i, kv):
            c.V.wait("spe", c.m[("p3_pvz", kv)])
            c.V.tensor_copy(out=zrow[0:1, :], in_=ps_z[kv % 2][0:1, :]).inc("sve")
            c.m[("p3bk", ("z", kv % 2))] = ("sve", c.c["sve"])
            c.V.wait("sve", c.c["sve"])
            c.V.reciprocal(out=zrow[0:1, :], in_=zrow[0:1, :]).inc("sve")
            c.V.wait("sve", c.c["sve"])
            c.V.tensor_copy(out=zrec[0:1, kv * 512:(kv + 1) * 512], in_=zrow[0:1, :]).inc("sve")
            c.m[("p3_zr", kv)] = c.c["sve"]

        def p3_epi_b(c, i, kv):
            c.T.wait("sve", c.m[("p3_zr", kv)])
            c.T.matmul(ps_z[kv % 2][:], ones_cr[0:1, 1:129], zrec[0:1, kv * 512:(kv + 1) * 512],
                       start=True, stop=True, skip_group_check=True).inc("spe")
            c.V.wait("spe", c.c["spe"])
            c.V.tensor_copy(out=rzbc[:], in_=ps_z[kv % 2][:]).inc("sve")
            c.m[("p3bk", ("z", kv % 2))] = ("sve", c.c["sve"])
            c.V.wait("sve", c.c["sve"])
            if kv == 0 and i >= 2:
                spar_prev = "soa" if i % 2 == 0 else "sob"
                c.V.wait(spar_prev, c.m[("p3_st", i - 2)])
            c.V.scalar_tensor_tensor(out=yTb[i % 2][:, kv * 512:(kv + 1) * 512],
                                     in0=ps_yu[kv % 2][:],
                                     scalar=1.0, in1=rzbc[:],
                                     op0=ALU.mult, op1=ALU.mult).inc("sve")
            c.m[("p3bk", ("yu", kv % 2))] = ("sve", c.c["sve"])

        def p3(c):
            c.V.wait("scst", CONSTS)
            c.V.reduce_max(out=nb1[:], in_=qg_sb[:], axis=AX.X).inc("sve")
            c.V.wait("sve", c.c["sve"])
            c.V.tensor_scalar(out=nb1[:], in0=nb1[:], scalar1=-math.sqrt(float(HD)),
                              scalar2=40.0, op0=ALU.mult, op1=ALU.add).inc("sve")
            c.V.wait("sve", c.c["sve"])
            for i in range(nt):
                par = "sxa" if i % 2 == 0 else "sxb"
                spar = "soa" if i % 2 == 0 else "sob"
                if i >= 2:
                    c.Y.wait("spe", c.m[("p3_pe_done", i - 2)])
                c.Y.dma_start(out=qTb3[i % 2][:].rearrange("p (c t) -> p c t", c=16),
                              in_=qT_s.rearrange("c p t -> p c t")[:, :, i * 128:(i + 1) * 128]
                              ).inc(par, 16)
                c.Y.dma_start(
                    out=kcache[:].rearrange("p (h t) -> p h t", h=NKV)[:, :, i * 128:(i + 1) * 128],
                    in_=kT_s.rearrange("p (h t) -> p h t", h=NKV)[:, :, i * 128:(i + 1) * 128]
                ).inc(par, 16)
                c.Y.dma_start(out=vcache[:, i * 512:(i + 1) * 512],
                              in_=v_s[:, i * 512:(i + 1) * 512]).inc(par, 16)
                c.m[("p3_ld", i)] = c.c[par]

                if i >= 4 and (i - 4) % 3 == 0 and (i - 4) // 3 < 8:
                    j = (i - 4) // 3
                    fc, hf = j // 2, j % 2
                    c.G.dma_start(
                        out=wb_bf[:].rearrange("p (ch d) -> p ch d", ch=16)[
                            :, hf * 8:(hf + 1) * 8, fc * 512:(fc + 1) * 512],
                        in_=wb_i.rearrange("p (ch d) -> p ch d", ch=16)[
                            :, hf * 8:(hf + 1) * 8, fc * 512:(fc + 1) * 512],
                    ).inc("sw45", 16)
                elif i == 26:
                    c.G.dma_start(out=ac_bf[:], in_=ac_i).inc("sw45", 16)
                elif i == 27:
                    c.G.dma_start(out=bc_bf[:], in_=bc_i).inc("sw45", 16)
                elif i == 28:
                    c.G.dma_start(out=bb_sb[:], in_=bb_i).inc("sw45", 16)

                for kv in range(NKV):
                    if kv == 0:
                        p3_attn_kv(c, i, kv)
                    else:
                        p3_attn_kv(c, i, kv,
                                   pre_cb=(lambda kk=kv - 1: p3_epi_a(c, i, kk),
                                           lambda kk=kv - 1: p3_epi_b(c, i, kk)))
                p3_epi_a(c, i, NKV - 1)
                p3_epi_b(c, i, NKV - 1)
                c.m[("p3_pe_done", i)] = c.c["spe"]
                c.G.wait("sve", c.c["sve"])
                c.G.dma_start(out=yT_s.rearrange("c p t -> p c t")[:, :, i * 128:(i + 1) * 128],
                              in_=yTb[i % 2][:].rearrange("p (c t) -> p c t", c=16)).inc(spar, 16)
                c.m[("p3_st", i)] = c.c[spar]
            _barrier(c)

        _emit_all(blk, ctxs, p3)
        es3.close()

        es4 = contextlib.ExitStack()
        wp_bf = es4.enter_context(nc.sbuf_tensor("wp_bf", [128, 16 * D], BF16))
        xnTb4 = [es4.enter_context(nc.sbuf_tensor(f"xnTb4_{j}", [128, D], BF16)) for j in range(2)]
        yTb4 = [es4.enter_context(nc.sbuf_tensor(f"yTb4_{j}", [128, D], BF16)) for j in range(2)]
        xb4 = [es4.enter_context(nc.sbuf_tensor(f"xb4_{j}", [128, D], F32)) for j in range(2)]
        gbuf = es4.enter_context(nc.sbuf_tensor("gbuf", [128, D], F32))
        midf = es4.enter_context(nc.sbuf_tensor("midf", [128, NCAT], F32))
        hwbf = es4.enter_context(nc.sbuf_tensor("hwbf", [128, NCAT], BF16))
        hwT = es4.enter_context(nc.sbuf_tensor("hwT", [128, 128], BF16))
        ps_f = [es4.enter_context(nc.psum_tensor(f"ps_f{j}", [128, 512], F32)) for j in range(4)]
        ps_m = es4.enter_context(nc.psum_tensor("ps_m", [128, 512], F32))
        ps_h = es4.enter_context(nc.psum_tensor("ps_h", [128, 512], F32))
        ps_w = [es4.enter_context(nc.psum_tensor(f"ps_w{j}", [128, 512], F32)) for j in range(2)]

        W45 = 11 * 16

        def p45_setup(c):
            for fc in range(4):
                c.G.dma_start(
                    out=wp_bf[:].rearrange("p (ch d) -> p ch d", ch=16)[:, :, fc * 512:(fc + 1) * 512],
                    in_=wp_i.rearrange("p (ch d) -> p ch d", ch=16)[:, :, fc * 512:(fc + 1) * 512],
                ).inc("sw45", 16)

        def p45_tile(c, i):
            par = "sxa" if i % 2 == 0 else "sxb"
            spar = "soa" if i % 2 == 0 else "sob"
            if i >= 2:
                c.Y.wait("spe", c.m[("p45_pe_done", i - 2)])
                c.Y.wait("sve", c.m[("p45_v_done", i - 2)])
                c.Y.wait(spar, c.m[("p45_st", i - 2)])
            c.Y.dma_start(out=xnTb4[i % 2][:].rearrange("p (c t) -> p c t", c=16),
                          in_=xnT_s.rearrange("c p t -> p c t")[:, :, i * 128:(i + 1) * 128]
                          ).inc(par, 16)
            c.Y.dma_start(out=yTb4[i % 2][:].rearrange("p (c t) -> p c t", c=16),
                          in_=yT_s.rearrange("c p t -> p c t")[:, :, i * 128:(i + 1) * 128]
                          ).inc(par, 16)
            c.Y.dma_start(out=xb4[i % 2][:], in_=x_d[i * 128:(i + 1) * 128, :]).inc(par, 16)
            ld = c.c[par]

            c.T.wait(par, ld)
            if i == 0:
                c.T.wait("sw45", 9 * 16)
            if i >= 1:
                c.T.wait("sve", c.m[("p45_sgev", i - 1, 2)])
                for fc in range(4):
                    c.T.wait("sve", c.m[("p45_gev", i - 1, fc)])
            ins = None
            for ch in range(16):
                ins = c.T.matmul(ps_m[:, 0:NCAT], xnTb4[i % 2][:, ch * 128:(ch + 1) * 128],
                                 ac_bf[:, ch * NCAT:(ch + 1) * NCAT],
                                 start=(ch == 0), stop=(ch == 15), skip_group_check=True)
                for fc in range(4):
                    c.T.matmul(ps_f[fc][:], xnTb4[i % 2][:, ch * 128:(ch + 1) * 128],
                               wb_bf[:, ch * D + fc * 512: ch * D + (fc + 1) * 512],
                               start=(ch == 0), stop=False, skip_group_check=True)
            ins.inc("spe")
            mid_mark = c.c["spe"]

            c.V.wait("spe", mid_mark)
            c.V.tensor_copy(out=midf[:], in_=ps_m[:, 0:NCAT]).inc("sve")
            c.m[("p45_midev", i)] = c.c["sve"]
            c.V.wait("sve", c.c["sve"])
            for e in range(NE):
                c.V.tensor_scalar(out=midf[:, 16 + e * NR:16 + (e + 1) * NR],
                                  in0=midf[:, 16 + e * NR:16 + (e + 1) * NR],
                                  scalar1=w6[:, i * NE + e:i * NE + e + 1],
                                  scalar2=None, op0=ALU.mult).inc("sve")
            c.V.wait("sve", c.c["sve"])
            c.V.tensor_copy(out=hwbf[:], in_=midf[:]).inc("sve")
            hw_mark = c.c["sve"]

            if i == 0:
                c.T.wait("sw45", W45 + 32)
            if i >= 1:
                c.T.wait("sve", c.m[("p45_sgev", i - 1, 0)])
                c.T.wait("sve", c.m[("p45_sgev", i - 1, 1)])
            ins = None
            for ch in range(16):
                for fc in (0, 1):
                    ins = c.T.matmul(ps_w[fc][:], yTb4[i % 2][:, ch * 128:(ch + 1) * 128],
                                     wp_bf[:, ch * D + fc * 512: ch * D + (fc + 1) * 512],
                                     start=(ch == 0), stop=(ch == 15), skip_group_check=True)
            ins.inc("spe")
            wp0_mark = c.c["spe"]

            c.T.wait("sve", hw_mark)
            if i >= 1:
                c.T.wait("sve", c.m[("p45_sgev", i - 1, 3)])
            c.T.matmul(ps_h[0:NCAT, 0:128], hwbf[:, 0:NCAT], ident[:],
                       start=True, stop=True, skip_group_check=True).inc("spe")
            c.V.wait("spe", c.c["spe"])
            c.V.tensor_copy(out=hwT[0:NCAT, :], in_=ps_h[0:NCAT, 0:128]).inc("sve")
            c.m[("p45_hwev", i)] = c.c["sve"]

            c.T.wait("sve", c.m[("p45_hwev", i)])
            if i == 0:
                c.T.wait("sw45", 10 * 16)
            ins = None
            for fc in range(4):
                ins = c.T.matmul(ps_f[fc][:], hwT[0:NCAT, :],
                                 bc_bf[0:NCAT, fc * 512:(fc + 1) * 512],
                                 start=False, stop=True, skip_group_check=True)
            ins.inc("spe")
            ffn_mark = c.c["spe"]

            if i == 0:
                c.T.wait("sw45", W45 + 64)
            c.T.wait("sve", c.m[("p45_hwev", i)])
            ins = None
            for ch in range(16):
                for fc in (2, 3):
                    bank = ps_m if fc == 2 else ps_h
                    ins = c.T.matmul(bank[:], yTb4[i % 2][:, ch * 128:(ch + 1) * 128],
                                     wp_bf[:, ch * D + fc * 512: ch * D + (fc + 1) * 512],
                                     start=(ch == 0), stop=(ch == 15), skip_group_check=True)
            ins.inc("spe")
            wp1_mark = c.c["spe"]

            c.V.wait("spe", ffn_mark)
            if i == 0:
                c.V.wait("sw45", W45)
            silu_marks = {}

            def bias_silu(fc):
                sl = slice(fc * 512, (fc + 1) * 512)
                c.V.tensor_tensor(out=gbuf[:, sl], in0=ps_f[fc][:],
                                  in1=bb_sb[:, sl], op=ALU.add).inc("sve")
                c.m[("p45_gev", i, fc)] = c.c["sve"]
                c.S.wait("sve", c.c["sve"])
                c.S.activation(out=gbuf[:, sl], in_=gbuf[:, sl],
                               func=AF.Silu).inc("sse")
                silu_marks[fc] = c.c["sse"]

            for fc in range(4):
                bias_silu(fc)
            for fc in range(4):
                c.V.wait("spe", wp0_mark if fc < 2 else wp1_mark)
                c.V.wait("sse", silu_marks[fc])
                src = (ps_w[0], ps_w[1], ps_m, ps_h)[fc]
                sl = slice(fc * 512, (fc + 1) * 512)
                c.V.scalar_tensor_tensor(out=gbuf[:, sl], in0=src[:, 0:512],
                                         scalar=1.0, in1=gbuf[:, sl],
                                         op0=ALU.mult, op1=ALU.mult)
                c.V.tensor_tensor(out=xb4[i % 2][:, sl], in0=gbuf[:, sl],
                                  in1=xb4[i % 2][:, sl], op=ALU.add).inc("sve")
                c.m[("p45_sgev", i, fc)] = c.c["sve"]
            c.m[("p45_v_done", i)] = c.c["sve"]
            c.m[("p45_pe_done", i)] = c.c["spe"]

            c.G.wait("sve", c.c["sve"])
            c.G.dma_start(out=out_d[i * 128:(i + 1) * 128, :], in_=xb4[i % 2][:]).inc(spar, 16)
            c.m[("p45_st", i)] = c.c[spar]

        def p45(c):
            p45_setup(c)
            for i in range(nt):
                p45_tile(c, i)
            c.G.wait("soa", c.c["soa"])
            c.G.wait("sob", c.c["sob"])

        _emit_all(blk, ctxs, p45)
        es4.close()
        es34.close()

    return nc


_NC_CACHE = {}


def _get_nc():
    if "nc" not in _NC_CACHE:
        _NC_CACHE["nc"] = build_nc()
    return _NC_CACHE["nc"]


def _img(mat):
    n = mat.shape[1]
    return np.ascontiguousarray(
        mat.reshape(16, 128, n).transpose(1, 0, 2).reshape(128, 16 * n).astype(np.float32))


def prepare_shared(L0_attn_wq, L0_attn_wk, L0_attn_wv, L0_attn_wp, L0_attn_qg,
                   L0_ffn_W_base, L0_ffn_b_base, L0_ffn_A_s, L0_ffn_B_s, L0_ffn_A_r,
                   L0_ffn_B_r, L0_ffn_W_router, L0_ffn_b_router, **_ignore):
    wq_img = _img(np.asarray(L0_attn_wq, np.float32).T)
    wk_img = _img(np.asarray(L0_attn_wk, np.float32).T)
    wv_img = _img(np.asarray(L0_attn_wv, np.float32).T)
    wp_img = _img(np.asarray(L0_attn_wp, np.float32).T)
    wb_img = _img(np.asarray(L0_ffn_W_base, np.float32))
    wr_img = _img(np.asarray(L0_ffn_W_router, np.float32))
    A_r = np.asarray(L0_ffn_A_r, np.float32)
    acat = np.concatenate([np.asarray(L0_ffn_A_s, np.float32),
                           A_r.transpose(1, 0, 2).reshape(D, NE * NR)], axis=1)
    acat_img = _img(acat)
    bcat_img = np.ascontiguousarray(np.concatenate(
        [np.asarray(L0_ffn_B_s, np.float32),
         np.asarray(L0_ffn_B_r, np.float32).reshape(NE * NR, D)], axis=0))
    bb_rep = np.ascontiguousarray(np.tile(np.asarray(L0_ffn_b_base, np.float32)[None, :], (128, 1)))
    br_rep = np.ascontiguousarray(np.tile(np.asarray(L0_ffn_b_router, np.float32)[None, :], (128, 1)))
    qg_rep = np.ascontiguousarray(np.tile(np.asarray(L0_attn_qg, np.float32)[None, :], (128, 1)))

    ident_bf = np.eye(128, dtype=_bf)
    trikm1 = np.where(np.arange(128)[:, None] <= np.arange(128)[None, :],
                      np.float32(0.0), np.float32(NEGBIG)).astype(np.float32)
    trikm = np.ascontiguousarray(np.tile(trikm1, (1, 4)).astype(_bf))
    ones_cr = np.ones((128, 129), dtype=_bf)
    eps_c = np.ascontiguousarray(np.tile(np.array([[LN_EPS, RMS_EPS]], np.float32), (128, 1)))
    t_ = np.arange(T, dtype=np.float32)
    inv = (1.0 / (10000.0 ** (np.arange(0, PD, 2, dtype=np.float32) / PD))).astype(np.float32)
    f_ = t_[:, None] * inv[None, :]
    rope_tab = np.ascontiguousarray(np.concatenate(
        [np.tile(np.cos(f_).astype(np.float32), (1, 16)),
         np.tile(np.sin(f_).astype(np.float32), (1, 16))], axis=1))

    return dict(wq_img=wq_img, wk_img=wk_img, wv_img=wv_img, wp_img=wp_img,
                wb_img=wb_img, wr_img=wr_img, acat_img=acat_img, bcat_img=bcat_img,
                bb_rep=bb_rep, br_rep=br_rep, qg_rep=qg_rep, ident_bf=ident_bf,
                trikm=trikm, ones_cr=ones_cr, eps_c=eps_c, rope_tab=rope_tab)


def kernel(**inputs):
    x = np.asarray(inputs["L0_attn_x"], dtype=np.float32)
    shared = prepare_shared(**inputs)
    in_maps = [dict(shared, x=np.ascontiguousarray(x[b])) for b in range(B)]
    nc = _get_nc()
    res = run_bass_kernel_spmd(nc, in_maps, core_ids=list(range(B)))
    return np.stack([np.asarray(res.results[b]["out"], dtype=np.float32) for b in range(B)],
                    axis=0)


# version 23 (final)
# speedup vs baseline: 1.0972x; 1.0051x over previous
import contextlib
import math

import numpy as np
import ml_dtypes

import concourse.bass as bass
import concourse.mybir as mybir
from concourse.bass_utils import run_bass_kernel_spmd

F32 = mybir.dt.float32
BF16 = mybir.dt.bfloat16
ALU = mybir.AluOpType
AF = mybir.ActivationFunctionType
AX = mybir.AxisListType
_bf = ml_dtypes.bfloat16

B, T, D = 8, 4096, 2048
NT = T // 128
NH, NKV, HD, PD = 16, 4, 128, 16
GQ = NH // NKV
NE, NR = 6, 16
NCAT = 16 + NE * NR
SCALE = 1.0 / math.sqrt(HD)
LN_EPS = 1e-5
RMS_EPS = float(np.finfo(np.float32).eps)
NEGBIG = -1.0e30

ENGINES = ("sync", "gpsimd", "tensor", "vector", "scalar")
SEM_NAMES = ("spe", "sve", "sse",
             "sxa", "sxb",
             "soa", "sob",
             "scst", "sw2", "sw3", "sw45")


class _Ins:
    __slots__ = ("ctx", "real")

    def __init__(self, ctx, real):
        self.ctx, self.real = ctx, real

    def inc(self, sem_name, k=1):
        self.ctx.c[sem_name] += k
        if self.real is not None:
            self.real.then_inc(self.ctx.sems[sem_name], k)
        return self


class _Eng:
    def __init__(self, ctx, name):
        object.__setattr__(self, "ctx", ctx)
        object.__setattr__(self, "name", name)
        object.__setattr__(self, "real", None)

    def wait(self, sem_name, value):
        if self.real is not None and value > 0:
            self.real.wait_ge(self.ctx.sems[sem_name], value)

    def __getattr__(self, op):
        ctx = self.ctx
        real = self.real

        def f(*a, **kw):
            ri = getattr(real, op)(*a, **kw) if real is not None else None
            return _Ins(ctx, ri)
        return f


class Ctx:
    def __init__(self, sems, active):
        self.sems = sems
        self.active = active
        self.c = {k: 0 for k in SEM_NAMES}
        self.m = {}
        self.T = _Eng(self, "tensor")
        self.V = _Eng(self, "vector")
        self.S = _Eng(self, "scalar")
        self.Y = _Eng(self, "sync")
        self.G = _Eng(self, "gpsimd")
        self.by_name = {"tensor": self.T, "vector": self.V, "scalar": self.S,
                        "sync": self.Y, "gpsimd": self.G}

    def bind(self, eng):
        object.__setattr__(self.by_name[self.active], "real", eng)

    def unbind(self):
        object.__setattr__(self.by_name[self.active], "real", None)


def _emit_all(blk, ctxs, fn):
    for name in ENGINES:
        c = ctxs[name]

        def body(eng, c=c):
            c.bind(eng)
            fn(c)
            c.unbind()
        getattr(blk, name)(body)


def _barrier(c):
    snap = dict(c.c)
    for e in (c.T, c.V, c.S, c.Y, c.G):
        for sn in SEM_NAMES:
            if snap[sn] > 0:
                e.wait(sn, snap[sn])


def build_nc(nt=NT, dbg=False):
    nc = bass.Bass("TRN2", target_bir_lowering=False)

    def din(name, shape, dt=F32):
        return nc.dram_tensor(name, shape, dt, kind="ExternalInput").ap()

    x_d = din("x", [T, D])
    wq_i = din("wq_img", [128, 16 * D])
    wk_i = din("wk_img", [128, 16 * 512])
    wv_i = din("wv_img", [128, 16 * 512])
    wp_i = din("wp_img", [128, 16 * D])
    wb_i = din("wb_img", [128, 16 * D])
    wr_i = din("wr_img", [128, 16 * NE])
    ac_i = din("acat_img", [128, 16 * NCAT])
    bc_i = din("bcat_img", [NCAT, D])
    bb_i = din("bb_rep", [128, D])
    br_i = din("br_rep", [128, NE])
    qg_i = din("qg_rep", [128, NH])
    id_i = din("ident_bf", [128, 128], BF16)
    trikm_i = din("trikm", [128, 512], BF16)
    ones_i = din("ones_cr", [128, 129], BF16)
    eps_i = din("eps_c", [128, 2])
    rope_i = din("rope_tab", [T, 256])

    out_d = nc.dram_tensor("out", [T, D], F32, kind="ExternalOutput").ap()
    xnT_s = nc.dram_tensor("xnT_s", [16, 128, T], BF16, kind="Internal").ap()
    yT_s = nc.dram_tensor("yT_s", [16, 128, T], BF16, kind="Internal").ap()
    qT_s = nc.dram_tensor("qT_s", [16, 128, T], BF16, kind="Internal").ap()
    kT_s = nc.dram_tensor("kT_s", [128, NKV * T], BF16, kind="Internal").ap()
    v_s = nc.dram_tensor("v_s", [128, NT * 512], BF16, kind="Internal").ap()

    with contextlib.ExitStack() as es:
        sems = {n: es.enter_context(nc.semaphore(n)) for n in SEM_NAMES}
        blk = es.enter_context(nc.Block())
        ctxs = {n: Ctx(sems, n) for n in ENGINES}

        ident = es.enter_context(nc.sbuf_tensor("ident", [128, 128], BF16))
        trikm_sb = es.enter_context(nc.sbuf_tensor("trikm_sb", [128, 512], BF16))
        ones_cr = es.enter_context(nc.sbuf_tensor("ones_sb", [128, 129], BF16))
        epsc = es.enter_context(nc.sbuf_tensor("epsc", [128, 2], F32))
        qg_sb = es.enter_context(nc.sbuf_tensor("qg_sb", [128, NH], F32))
        br_sb = es.enter_context(nc.sbuf_tensor("br_sb", [128, NE], F32))
        w6 = es.enter_context(nc.sbuf_tensor("w6", [128, NT * NE], F32))
        st1 = es.enter_context(nc.sbuf_tensor("st1", [128, 16], F32))
        st2 = es.enter_context(nc.sbuf_tensor("st2", [128, 16], F32))
        st3 = es.enter_context(nc.sbuf_tensor("st3", [128, 16], F32))
        nmean = es.enter_context(nc.sbuf_tensor("nmean", [128, 1], F32))
        nmist = es.enter_context(nc.sbuf_tensor("nmist", [128, 1], F32))

        es2 = contextlib.ExitStack()
        wq_bf = es2.enter_context(nc.sbuf_tensor("wq_bf", [128, 16 * D], BF16))
        wk_bf = es2.enter_context(nc.sbuf_tensor("wk_bf", [128, 16 * 512], BF16))
        wv_bf = es2.enter_context(nc.sbuf_tensor("wv_bf", [128, 16 * 512], BF16))
        wr_bf = es2.enter_context(nc.sbuf_tensor("wr_bf", [128, 16 * NE], BF16))
        xb2 = [es2.enter_context(nc.sbuf_tensor(f"xb2_{j}", [128, D], F32)) for j in range(2)]
        xnbf = [es2.enter_context(nc.sbuf_tensor(f"xnbf{j}", [128, D], BF16)) for j in range(2)]
        xnTb2 = [es2.enter_context(nc.sbuf_tensor(f"xnTb2_{j}", [128, D], BF16)) for j in range(2)]
        ropeb = [es2.enter_context(nc.sbuf_tensor(f"ropeb{j}", [128, 256], F32)) for j in range(2)]
        rtmp = es2.enter_context(nc.sbuf_tensor("rtmp", [128, 512], F32))
        ktm = es2.enter_context(nc.sbuf_tensor("ktm", [128, 512], F32))
        kbf = es2.enter_context(nc.sbuf_tensor("kbf", [128, 512], BF16))
        kTb = [es2.enter_context(nc.sbuf_tensor(f"kTb{j}", [128, 512], BF16)) for j in range(2)]
        vb = [es2.enter_context(nc.sbuf_tensor(f"vb{j}", [128, 512], BF16)) for j in range(2)]
        qbf = es2.enter_context(nc.sbuf_tensor("qbf", [128, D], BF16))
        qsq = es2.enter_context(nc.sbuf_tensor("qsq", [128, D], BF16))
        qTb = [es2.enter_context(nc.sbuf_tensor(f"qTb{j}", [128, D], BF16)) for j in range(2)]
        sc16 = es2.enter_context(nc.sbuf_tensor("sc16", [128, NH], F32))
        rf6 = es2.enter_context(nc.sbuf_tensor("rf6", [128, NE], F32))
        g6a = es2.enter_context(nc.sbuf_tensor("g6a", [128, NE], F32))
        g6b = es2.enter_context(nc.sbuf_tensor("g6b", [128, NE], F32))
        z6 = es2.enter_context(nc.sbuf_tensor("z6", [128, 1], F32))
        m6 = es2.enter_context(nc.sbuf_tensor("m6", [128, 1], F32))
        ps_k = [es2.enter_context(nc.psum_tensor(f"ps_k{j}", [128, 512], F32)) for j in range(2)]
        ps_v = [es2.enter_context(nc.psum_tensor(f"ps_v{j}", [128, 512], F32)) for j in range(2)]
        ps_r = [es2.enter_context(nc.psum_tensor(f"ps_r{j}", [128, 512], F32)) for j in range(2)]
        ps_xt = [es2.enter_context(nc.psum_tensor(f"ps_xt{j}", [128, 512], F32)) for j in range(2)]

        def setup(c):
            c.G.dma_start(out=ident[:], in_=id_i).inc("scst", 16)
            c.G.dma_start(out=trikm_sb[:], in_=trikm_i).inc("scst", 16)
            c.G.dma_start(out=ones_cr[:], in_=ones_i).inc("scst", 16)
            c.G.dma_start(out=qg_sb[:], in_=qg_i).inc("scst", 16)
            c.G.dma_start(out=br_sb[:], in_=br_i).inc("scst", 16)
            c.G.dma_start(out=epsc[:], in_=eps_i).inc("scst", 16)
            c.G.dma_start(out=wk_bf[:], in_=wk_i).inc("sw2", 16)
            c.G.dma_start(out=wv_bf[:], in_=wv_i).inc("sw2", 16)
            c.G.dma_start(out=wr_bf[:], in_=wr_i).inc("sw2", 16)
            for fc in range(4):
                c.G.dma_start(
                    out=wq_bf[:].rearrange("p (ch d) -> p ch d", ch=16)[:, :, fc * 512:(fc + 1) * 512],
                    in_=wq_i.rearrange("p (ch d) -> p ch d", ch=16)[:, :, fc * 512:(fc + 1) * 512],
                ).inc("sw3", 16)

        _emit_all(blk, ctxs, setup)
        CONSTS = 6 * 16
        W2 = 3 * 16

        def bw(c, bank):
            prev = c.m.get(("b", bank))
            if prev is not None:
                c.T.wait(prev[0], prev[1])

        def bank_of(bid):
            return {"k0": ps_k[0], "k1": ps_k[1], "v0": ps_v[0], "v1": ps_v[1],
                    "r0": ps_r[0], "r1": ps_r[1], "x0": ps_xt[0], "x1": ps_xt[1]}[bid]

        def p2_A(c, i):
            par = "sxa" if i % 2 == 0 else "sxb"
            if i >= 2:
                c.Y.wait("sse", c.m[("p2_s_done", i - 2)])
                c.Y.wait("sve", c.m[("p2_qprep", i - 2)])
            c.Y.dma_start(out=xb2[i % 2][:], in_=x_d[i * 128:(i + 1) * 128, :]).inc(par, 16)
            c.Y.dma_start(out=ropeb[i % 2][:], in_=rope_i[i * 128:(i + 1) * 128, :]).inc(par, 16)
            ld = c.c[par]

            c.V.wait(par, ld)
            c.V.reduce_sum(out=nmean[:], in_=xb2[i % 2][:], axis=AX.X, negate=True).inc("sve")
            c.V.wait("sve", c.c["sve"])
            c.V.tensor_scalar(out=nmean[:], in0=nmean[:], scalar1=1.0 / D, scalar2=None,
                              op0=ALU.mult).inc("sve")
            c.S.wait("sve", c.c["sve"])
            c.S.wait(par, ld)
            c.S.activation(out=xnbf[i % 2][:], in_=xb2[i % 2][:], func=AF.Square,
                           bias=nmean[:], scale=1.0, accum_out=st1[:, 15:16])
            c.S.activation(out=st3[:, 14:15], in_=st1[:, 15:16], func=AF.Ln,
                           scale=1.0 / D, bias=epsc[:, 0:1])
            c.S.activation(out=st3[:, 15:16], in_=st3[:, 14:15], func=AF.Exp,
                           scale=-0.5).inc("sse")
            c.V.wait("sse", c.c["sse"])
            c.V.tensor_tensor(out=nmist[:], in0=nmean[:], in1=st3[:, 15:16], op=ALU.mult).inc("sve")
            c.S.wait("sve", c.c["sve"])
            c.S.activation(out=xnbf[i % 2][:], in_=xb2[i % 2][:], func=AF.Identity,
                           scale=st3[:, 15:16], bias=nmist[:]).inc("sse")
            c.m[("p2_s_done", i)] = c.c["sse"]
            c.m[("p2_ln", i)] = c.c["sse"]

        def p2_XTtr(c, i):
            par = "sxa" if i % 2 == 0 else "sxb"
            spar = "soa" if i % 2 == 0 else "sob"
            for g in range(4):
                c.T.wait("sse", c.m[("p2_ln", i)])
                if i == 0 and g == 0:
                    c.T.wait("scst", CONSTS)
                bid = "x%d" % (g % 2)
                bw(c, bid)
                ins = None
                for q in range(4):
                    ch = g * 4 + q
                    ins = c.T.matmul(ps_xt[g % 2][:, q * 128:(q + 1) * 128],
                                     xnbf[i % 2][:, ch * 128:(ch + 1) * 128], ident[:],
                                     start=True, stop=True, skip_group_check=True)
                ins.inc("spe")
                eng = c.S if g % 2 == 0 else c.V
                sem = "sse" if g % 2 == 0 else "sve"
                eng.wait("spe", c.c["spe"])
                if g < 2 and i >= 2:
                    eng.wait(spar, c.m[("p2_st", i - 2)])
                    eng.wait("spe", c.m[("p2_qc_done", i - 2)])
                if g % 2 == 0:
                    c.S.activation(out=xnTb2[i % 2][:, g * 512:(g + 1) * 512],
                                   in_=ps_xt[g % 2][:], func=AF.Identity).inc("sse")
                else:
                    c.V.tensor_copy(out=xnTb2[i % 2][:, g * 512:(g + 1) * 512],
                                    in_=ps_xt[g % 2][:]).inc("sve")
                c.m[("b", bid)] = (sem, c.c[sem])
            c.m[("p2_xnT_s", i)] = c.c["sse"]
            c.m[("p2_xnT_v", i)] = c.c["sve"]

            c.G.wait("sse", c.c["sse"])
            c.G.wait("sve", c.c["sve"])
            c.G.dma_start(out=xnT_s.rearrange("c p t -> p c t")[:, :, i * 128:(i + 1) * 128],
                          in_=xnTb2[i % 2][:].rearrange("p (c t) -> p c t", c=16)).inc(spar, 16)
            c.m[("p2_st", i)] = c.c[spar]

        def p2_KVR(c, i):
            c.T.wait("sw2", W2)
            c.T.wait("sse", c.m[("p2_xnT_s", i)])
            c.T.wait("sve", c.m[("p2_xnT_v", i)])
            for bid in ("k%d" % (i % 2), "v%d" % (i % 2), "r%d" % (i % 2)):
                bw(c, bid)
            ins = None
            for ch in range(16):
                xc = xnTb2[i % 2][:, ch * 128:(ch + 1) * 128]
                c.T.matmul(ps_k[i % 2][:], xc, wk_bf[:, ch * 512:(ch + 1) * 512],
                           start=(ch == 0), stop=(ch == 15), skip_group_check=True)
                c.T.matmul(ps_v[i % 2][:], xc, wv_bf[:, ch * 512:(ch + 1) * 512],
                           start=(ch == 0), stop=(ch == 15), skip_group_check=True)
                ins = c.T.matmul(ps_r[i % 2][:, 0:NE], xc, wr_bf[:, ch * NE:(ch + 1) * NE],
                                 start=(ch == 0), stop=(ch == 15), skip_group_check=True)
            ins.inc("spe")
            c.m[("p2_kvr", i)] = c.c["spe"]

        def p2_KP(c, i):
            par = "sxa" if i % 2 == 0 else "sxb"
            spar = "soa" if i % 2 == 0 else "sob"
            kvr = c.m[("p2_kvr", i)]
            c.V.wait("spe", kvr)
            c.V.tensor_copy(out=rf6[:], in_=ps_r[i % 2][:, 0:NE]).inc("sve")
            c.m[("b", "r%d" % (i % 2))] = ("sve", c.c["sve"])
            c.S.wait("spe", kvr)
            if i >= 1:
                c.S.wait("sve", c.m[("p2_kbf", i - 1)])
            c.S.activation(out=ktm[:], in_=ps_k[i % 2][:], func=AF.Identity).inc("sse")
            c.m[("b", "k%d" % (i % 2))] = ("sse", c.c["sse"])
            if i >= 2:
                c.S.wait(spar, c.m[("p2_st_kv", i - 2)])
            c.S.activation(out=vb[i % 2][:], in_=ps_v[i % 2][:], func=AF.Identity).inc("sse")
            c.m[("b", "v%d" % (i % 2))] = ("sse", c.c["sse"])
            c.G.wait("sse", c.c["sse"])
            c.G.dma_start(out=v_s[:, i * 512:(i + 1) * 512], in_=vb[i % 2][:]).inc(spar, 16)

            c.S.activation(out=kbf[:], in_=ktm[:], func=AF.Square).inc("sse")
            c.V.wait("sse", c.c["sse"])
            c.V.tensor_reduce(out=st1[:, 0:NKV],
                              in_=kbf[:].rearrange("p (h d) -> p h d", h=NKV),
                              axis=AX.X, op=ALU.add).inc("sve")
            c.S.wait("sve", c.c["sve"])
            c.S.activation(out=st3[:, 0:NKV], in_=st1[:, 0:NKV], func=AF.Ln,
                           scale=1.0 / HD, bias=epsc[:, 1:2])
            c.S.activation(out=st3[:, 0:NKV], in_=st3[:, 0:NKV], func=AF.Exp,
                           scale=-0.5).inc("sse")
            c.V.wait("sse", c.c["sse"])
            for h in range(NKV):
                c.V.tensor_scalar(out=ktm[:, h * 128:(h + 1) * 128],
                                  in0=ktm[:, h * 128:(h + 1) * 128],
                                  scalar1=st3[:, h:h + 1], scalar2=None, op0=ALU.mult)
            kv_ = ktm[:].rearrange("p (h d) -> p h d", h=NKV)
            co = ropeb[i % 2][:, 0:32].rearrange("p (h d) -> p h d", h=NKV)
            si = ropeb[i % 2][:, 128:160].rearrange("p (h d) -> p h d", h=NKV)
            x1, x2 = kv_[:, :, 0:8], kv_[:, :, 8:16]
            r_ = rtmp[:].rearrange("p (h d) -> p h d", h=64)
            r1, r2, r3, r4 = r_[:, 0:4], r_[:, 4:8], r_[:, 8:12], r_[:, 12:16]
            c.V.tensor_tensor(out=r1, in0=x1, in1=co, op=ALU.mult)
            c.V.tensor_tensor(out=r2, in0=x1, in1=si, op=ALU.mult)
            c.V.tensor_tensor(out=r3, in0=x2, in1=si, op=ALU.mult)
            c.V.tensor_tensor(out=x1, in0=r1, in1=r3, op=ALU.add)
            c.V.tensor_tensor(out=r4, in0=x2, in1=co, op=ALU.mult)
            c.V.tensor_tensor(out=x2, in0=r2, in1=r4, op=ALU.subtract).inc("sve")
            c.V.wait("sve", c.c["sve"])
            c.V.tensor_copy(out=kbf[:], in_=ktm[:]).inc("sve")
            c.m[("p2_kbf", i)] = c.c["sve"]

            def vs(ins):
                ins.inc("sve")
                c.V.wait("sve", c.c["sve"])
            vs(c.V.tensor_tensor(out=rf6[:], in0=rf6[:], in1=br_sb[:], op=ALU.add))
            c.V.reduce_max(out=m6[:], in_=rf6[:], axis=AX.X, negate=True).inc("sve")
            c.S.wait("sve", c.c["sve"])
            c.S.activation(out=g6a[:], in_=rf6[:], func=AF.Exp, bias=m6[:],
                           accum_out=z6[:]).inc("sse")
            c.V.wait("sse", c.c["sse"])
            vs(c.V.reciprocal(out=m6[:], in_=z6[:]))
            vs(c.V.tensor_scalar(out=g6a[:], in0=g6a[:], scalar1=m6[:], scalar2=None,
                                 op0=ALU.mult))
            vs(c.V.reduce_max(out=m6[:], in_=g6a[:], axis=AX.X))
            vs(c.V.tensor_scalar(out=g6b[:], in0=g6a[:], scalar1=m6[:], scalar2=None,
                                 op0=ALU.is_ge))
            vs(c.V.scalar_tensor_tensor(out=g6b[:], in0=g6b[:], scalar=-2.0, in1=g6a[:],
                                        op0=ALU.mult, op1=ALU.add))
            vs(c.V.reduce_max(out=m6[:], in_=g6b[:], axis=AX.X))
            vs(c.V.tensor_scalar(out=g6b[:], in0=g6a[:], scalar1=m6[:], scalar2=None,
                                 op0=ALU.is_ge))
            c.V.tensor_tensor(out=w6[:, i * NE:(i + 1) * NE], in0=g6a[:], in1=g6b[:],
                              op=ALU.mult).inc("sve")
            c.m[("p2_v_done", i)] = c.c["sve"]

        def p2_QC(c, i):
            par = i % 2
            qb = ["k%d" % (1 - par), "v%d" % (1 - par), "r%d" % (1 - par), "r%d" % par]
            if i == 0:
                c.T.wait("sw3", 64)
            for pair in range(2):
                f0, f1 = 2 * pair, 2 * pair + 1
                bw(c, qb[f0])
                bw(c, qb[f1])
                ins = None
                for ch in range(16):
                    xc = xnTb2[i % 2][:, ch * 128:(ch + 1) * 128]
                    c.T.matmul(bank_of(qb[f0])[:], xc,
                               wq_bf[:, ch * D + f0 * 512: ch * D + (f0 + 1) * 512],
                               start=(ch == 0), stop=(ch == 15), skip_group_check=True)
                    ins = c.T.matmul(bank_of(qb[f1])[:], xc,
                                     wq_bf[:, ch * D + f1 * 512: ch * D + (f1 + 1) * 512],
                                     start=(ch == 0), stop=(ch == 15), skip_group_check=True)
                ins.inc("spe")
                c.S.wait("spe", c.c["spe"])
                if pair == 0 and i >= 1:
                    c.S.wait("spe", c.m[("p2_qtr", i - 1)])
                c.S.activation(out=qbf[:, f0 * 512:(f0 + 1) * 512],
                               in_=bank_of(qb[f0])[:], func=AF.Identity).inc("sse")
                c.m[("b", qb[f0])] = ("sse", c.c["sse"])
                c.S.activation(out=qbf[:, f1 * 512:(f1 + 1) * 512],
                               in_=bank_of(qb[f1])[:], func=AF.Identity).inc("sse")
                c.m[("b", qb[f1])] = ("sse", c.c["sse"])
                c.V.wait("sse", c.c["sse"])
                c.V.tensor_tensor(out=qsq[:, pair * 1024:(pair + 1) * 1024],
                                  in0=qbf[:, pair * 1024:(pair + 1) * 1024],
                                  in1=qbf[:, pair * 1024:(pair + 1) * 1024],
                                  op=ALU.mult).inc("sve")
                c.V.wait("sve", c.c["sve"])
                c.V.tensor_reduce(
                    out=st2[:, pair * 8:(pair + 1) * 8],
                    in_=qsq[:, pair * 1024:(pair + 1) * 1024].rearrange(
                        "p (h d) -> p h d", h=8),
                    axis=AX.X, op=ALU.add).inc("sve")
            c.m[("p2_qc_done", i)] = c.c["spe"]
            c.m[("p2_qstat", i)] = c.c["sve"]

        def p2_KT(c, i):
            c.T.wait("sve", c.m[("p2_kbf", i)])
            bw(c, "k%d" % (i % 2))
            ins = None
            for h in range(NKV):
                ins = c.T.matmul(ps_k[i % 2][:, h * 128:(h + 1) * 128],
                                 kbf[:, h * 128:(h + 1) * 128], ident[:],
                                 start=True, stop=True, skip_group_check=True)
            ins.inc("spe")
            c.m[("p2_ktr", i)] = c.c["spe"]

        def p2_KTev(c, i):
            spar = "soa" if i % 2 == 0 else "sob"
            c.S.wait("spe", c.m[("p2_ktr", i)])
            if i >= 2:
                c.S.wait(spar, c.m[("p2_st_kv", i - 2)])
            c.S.activation(out=kTb[i % 2][:], in_=ps_k[i % 2][:], func=AF.Identity).inc("sse")
            c.m[("b", "k%d" % (i % 2))] = ("sse", c.c["sse"])
            c.G.wait("sse", c.c["sse"])
            c.G.dma_start(
                out=kT_s.rearrange("p (h t) -> p h t", h=NKV)[:, :, i * 128:(i + 1) * 128],
                in_=kTb[i % 2][:].rearrange("p (h t) -> p h t", h=NKV)).inc(spar, 16)
            c.m[("p2_st_kv", i)] = c.c[spar]

        def p2_QP(c, i):
            par = "sxa" if i % 2 == 0 else "sxb"
            c.S.wait("sve", c.m[("p2_qstat", i)])
            c.S.activation(out=st3[:], in_=st2[:], func=AF.Ln,
                           scale=1.0 / HD, bias=epsc[:, 1:2])
            c.S.activation(out=st3[:], in_=st3[:], func=AF.Exp, scale=-0.5).inc("sse")
            c.V.wait("sse", c.c["sse"])
            c.V.tensor_tensor(out=sc16[:], in0=st3[:], in1=qg_sb[:], op=ALU.mult).inc("sve")
            c.V.wait("sve", c.c["sve"])
            for h in range(NH):
                c.V.tensor_scalar(out=qbf[:, h * 128:(h + 1) * 128],
                                  in0=qbf[:, h * 128:(h + 1) * 128],
                                  scalar1=sc16[:, h:h + 1], scalar2=None, op0=ALU.mult)
            qv = qbf[:].rearrange("p (h d) -> p h d", h=NH)
            co = ropeb[i % 2][:, 0:128].rearrange("p (h d) -> p h d", h=NH)
            si = ropeb[i % 2][:, 128:256].rearrange("p (h d) -> p h d", h=NH)
            x1, x2 = qv[:, :, 0:8], qv[:, :, 8:16]
            r_ = rtmp[:].rearrange("p (h d) -> p h d", h=64)
            r1, r2, r3, r4 = r_[:, 0:16], r_[:, 16:32], r_[:, 32:48], r_[:, 48:64]
            c.V.tensor_tensor(out=r1, in0=x1, in1=co, op=ALU.mult)
            c.V.tensor_tensor(out=r2, in0=x1, in1=si, op=ALU.mult)
            c.V.tensor_tensor(out=r3, in0=x2, in1=si, op=ALU.mult)
            c.V.tensor_tensor(out=x1, in0=r1, in1=r3, op=ALU.add)
            c.V.tensor_tensor(out=r4, in0=x2, in1=co, op=ALU.mult)
            c.V.tensor_tensor(out=x2, in0=r2, in1=r4, op=ALU.subtract).inc("sve")
            c.m[("p2_qprep", i)] = c.c["sve"]

        def p2_QT(c, i):
            spar = "soa" if i % 2 == 0 else "sob"
            c.T.wait("sve", c.m[("p2_qprep", i)])
            for g in range(4):
                bid = "x%d" % ((g + 1) % 2)
                bw(c, bid)
                ins = None
                for hh in range(4):
                    h = g * 4 + hh
                    ins = c.T.matmul(ps_xt[(g + 1) % 2][:, hh * 128:(hh + 1) * 128],
                                     qbf[:, h * 128:(h + 1) * 128], ident[:],
                                     start=True, stop=True, skip_group_check=True)
                ins.inc("spe")
                c.S.wait("spe", c.c["spe"])
                if g == 0 and i >= 2:
                    c.S.wait(spar, c.m[("p2_st_q", i - 2)])
                c.S.activation(out=qTb[i % 2][:, g * 512:(g + 1) * 512],
                               in_=ps_xt[(g + 1) % 2][:], func=AF.Identity).inc("sse")
                c.m[("b", bid)] = ("sse", c.c["sse"])
            c.m[("p2_qtr", i)] = c.c["spe"]
            c.G.wait("sse", c.c["sse"])
            c.G.dma_start(out=qT_s.rearrange("c p t -> p c t")[:, :, i * 128:(i + 1) * 128],
                          in_=qTb[i % 2][:].rearrange("p (c t) -> p c t", c=16)).inc(spar, 16)
            c.m[("p2_st_q", i)] = c.c[spar]

        def p2(c):
            p2_A(c, 0)
            p2_XTtr(c, 0)
            p2_KVR(c, 0)
            for i in range(nt):
                p2_KP(c, i)
                if i + 1 < nt:
                    p2_A(c, i + 1)
                p2_QC(c, i)
                p2_KT(c, i)
                if i + 1 < nt:
                    p2_XTtr(c, i + 1)
                    p2_KVR(c, i + 1)
                p2_KTev(c, i)
                p2_QP(c, i)
                p2_QT(c, i)
            _barrier(c)

        _emit_all(blk, ctxs, p2)
        es2.close()

        es34 = contextlib.ExitStack()
        wb_bf = es34.enter_context(nc.sbuf_tensor("wb_bf", [128, 16 * D], BF16))
        ac_bf = es34.enter_context(nc.sbuf_tensor("ac_bf", [128, 16 * NCAT], BF16))
        bc_bf = es34.enter_context(nc.sbuf_tensor("bc_bf", [NCAT, D], BF16))
        bb_sb = es34.enter_context(nc.sbuf_tensor("bb_sb", [128, D], F32))

        es3 = contextlib.ExitStack()
        kcache = es3.enter_context(nc.sbuf_tensor("kcache", [128, NKV * T], BF16))
        vcache = es3.enter_context(nc.sbuf_tensor("vcache", [128, NT * 512], BF16))
        qTb3 = [es3.enter_context(nc.sbuf_tensor(f"qTb3_{j}", [128, D], BF16)) for j in range(2)]
        pT = es3.enter_context(nc.sbuf_tensor("pT", [128, 12 * 512], BF16))
        yTb = [es3.enter_context(nc.sbuf_tensor(f"yTb{j}", [128, D], BF16)) for j in range(2)]
        nb1 = es3.enter_context(nc.sbuf_tensor("nb1", [128, 1], F32))
        zrow = es3.enter_context(nc.sbuf_tensor("zrow", [1, 512], F32))
        zrec = es3.enter_context(nc.sbuf_tensor("zrec", [1, D], BF16))
        rzbc = es3.enter_context(nc.sbuf_tensor("rzbc", [128, 512], F32))
        ps_d = [es3.enter_context(nc.psum_tensor(f"ps_d{j}", [128, 1024], F32)) for j in range(2)]
        ps_yu = [es3.enter_context(nc.psum_tensor(f"ps_yu{j}", [128, 512], F32)) for j in range(2)]
        ps_z = [es3.enter_context(nc.psum_tensor(f"ps_z{j}", [128, 512], F32)) for j in range(2)]

        def _bkwait(c, eng, bid):
            prev = c.m.get(("p3bk", bid))
            if prev is not None:
                eng.wait(prev[0], prev[1])

        def p3_attn_kv(c, i, kv, pre_cb=None):
            nst = i + 1
            NSL = 12
            npair = (nst + 1) // 2

            def emit_pair(p):
                s0 = 2 * p
                s1 = min(2 * p + 1, nst - 1)
                w = s1 - s0 + 1
                bk = ps_d[p % 2]
                _bkwait(c, c.T, ("d", p % 2))
                if p == 0 and kv == 0:
                    par = "sxa" if i % 2 == 0 else "sxb"
                    c.T.wait(par, c.m[("p3_ld", i)])
                ins = None
                for s in range(s0, s1 + 1):
                    last = (s == nst - 1)
                    ins = c.T.matmul(bk[:, (s - s0) * 512:(s - s0 + 1) * 512],
                                     kcache[:, kv * T + s * 128: kv * T + (s + 1) * 128],
                                     qTb3[i % 2][:, kv * 512:(kv + 1) * 512],
                                     start=True, stop=not last, skip_group_check=True)
                    if last:
                        ins = c.T.matmul(bk[:, (s - s0) * 512:(s - s0 + 1) * 512],
                                         ident[:], trikm_sb[:],
                                         start=False, stop=True, skip_group_check=True)
                ins.inc("spe")
                c.S.wait("spe", c.c["spe"])
                prev = c.m.get(("p3_pt", s0 % NSL))
                if prev is not None:
                    c.S.wait("spe", prev)
                c.S.activation(out=pT[:, (s0 % NSL) * 512:((s0 % NSL) + w) * 512],
                               in_=bk[:, 0:w * 512],
                               func=AF.Exp, bias=nb1[:], scale=SCALE).inc("sse")
                c.m[("p3bk", ("d", p % 2))] = ("sse", c.c["sse"])
                c.m[("p3_expp", p)] = c.c["sse"]

            def emit_pv(s, need_wait):
                last = (s == nst - 1)
                if need_wait:
                    c.T.wait("sse", c.m[("p3_expp", s // 2)])
                if s == 0:
                    _bkwait(c, c.T, ("yu", kv % 2))
                c.T.matmul(ps_yu[kv % 2][:],
                           vcache[:, s * 512 + kv * 128: s * 512 + (kv + 1) * 128],
                           pT[:, (s % NSL) * 512:(s % NSL + 1) * 512],
                           start=(s == 0), stop=last, skip_group_check=True).inc("spe")

            def emit_zbatch(s0, s1):
                c.T.wait("sse", c.m[("p3_expp", (s1 - 1) // 2)])
                if s0 == 0:
                    _bkwait(c, c.T, ("z", kv % 2))
                ins = None
                for s in range(s0, s1):
                    ins = c.T.matmul(ps_z[kv % 2][0:1, :], ones_cr[:, 0:1],
                                     pT[:, (s % NSL) * 512:(s % NSL + 1) * 512],
                                     start=(s == 0), stop=(s == nst - 1),
                                     skip_group_check=True)
                ins.inc("spe")
                for s in range(s0, s1):
                    c.m[("p3_pt", s % NSL)] = c.c["spe"]

            emit_pair(0)
            if npair > 1:
                emit_pair(1)
            zb = 0
            for p in range(npair):
                if p + 2 < npair:
                    emit_pair(p + 2)
                if p == 0 and pre_cb is not None:
                    pre_cb[0]()
                if p == min(4, npair - 1) and pre_cb is not None:
                    pre_cb[1]()
                s0 = 2 * p
                s1 = min(2 * p + 1, nst - 1)
                emit_pv(s0, True)
                if s1 > s0:
                    emit_pv(s1, False)
                if s1 - zb >= 6 or s1 == nst - 1:
                    emit_zbatch(zb, s1 + 1)
                    zb = s1 + 1
            c.m[("p3_pvz", kv)] = c.c["spe"]

        def p3_epi_a(c, i, kv):
            c.V.wait("spe", c.m[("p3_pvz", kv)])
            c.V.tensor_copy(out=zrow[0:1, :], in_=ps_z[kv % 2][0:1, :]).inc("sve")
            c.m[("p3bk", ("z", kv % 2))] = ("sve", c.c["sve"])
            c.V.wait("sve", c.c["sve"])
            c.V.reciprocal(out=zrow[0:1, :], in_=zrow[0:1, :]).inc("sve")
            c.V.wait("sve", c.c["sve"])
            c.V.tensor_copy(out=zrec[0:1, kv * 512:(kv + 1) * 512], in_=zrow[0:1, :]).inc("sve")
            c.m[("p3_zr", kv)] = c.c["sve"]

        def p3_epi_b(c, i, kv):
            c.T.wait("sve", c.m[("p3_zr", kv)])
            c.T.matmul(ps_z[kv % 2][:], ones_cr[0:1, 1:129], zrec[0:1, kv * 512:(kv + 1) * 512],
                       start=True, stop=True, skip_group_check=True).inc("spe")
            c.V.wait("spe", c.c["spe"])
            c.V.tensor_copy(out=rzbc[:], in_=ps_z[kv % 2][:]).inc("sve")
            c.m[("p3bk", ("z", kv % 2))] = ("sve", c.c["sve"])
            c.V.wait("sve", c.c["sve"])
            if kv == 0 and i >= 2:
                spar_prev = "soa" if i % 2 == 0 else "sob"
                c.V.wait(spar_prev, c.m[("p3_st", i - 2)])
            c.V.scalar_tensor_tensor(out=yTb[i % 2][:, kv * 512:(kv + 1) * 512],
                                     in0=ps_yu[kv % 2][:],
                                     scalar=1.0, in1=rzbc[:],
                                     op0=ALU.mult, op1=ALU.mult).inc("sve")
            c.m[("p3bk", ("yu", kv % 2))] = ("sve", c.c["sve"])

        def p3(c):
            c.V.wait("scst", CONSTS)
            c.V.reduce_max(out=nb1[:], in_=qg_sb[:], axis=AX.X).inc("sve")
            c.V.wait("sve", c.c["sve"])
            c.V.tensor_scalar(out=nb1[:], in0=nb1[:], scalar1=-math.sqrt(float(HD)),
                              scalar2=40.0, op0=ALU.mult, op1=ALU.add).inc("sve")
            c.V.wait("sve", c.c["sve"])
            for i in range(nt):
                par = "sxa" if i % 2 == 0 else "sxb"
                spar = "soa" if i % 2 == 0 else "sob"
                if i >= 2:
                    c.Y.wait("spe", c.m[("p3_pe_done", i - 2)])
                c.Y.dma_start(out=qTb3[i % 2][:].rearrange("p (c t) -> p c t", c=16),
                              in_=qT_s.rearrange("c p t -> p c t")[:, :, i * 128:(i + 1) * 128]
                              ).inc(par, 16)
                c.Y.dma_start(
                    out=kcache[:].rearrange("p (h t) -> p h t", h=NKV)[:, :, i * 128:(i + 1) * 128],
                    in_=kT_s.rearrange("p (h t) -> p h t", h=NKV)[:, :, i * 128:(i + 1) * 128]
                ).inc(par, 16)
                c.Y.dma_start(out=vcache[:, i * 512:(i + 1) * 512],
                              in_=v_s[:, i * 512:(i + 1) * 512]).inc(par, 16)
                c.m[("p3_ld", i)] = c.c[par]

                if i >= 4 and (i - 4) % 3 == 0 and (i - 4) // 3 < 8:
                    j = (i - 4) // 3
                    fc, hf = j // 2, j % 2
                    c.G.dma_start(
                        out=wb_bf[:].rearrange("p (ch d) -> p ch d", ch=16)[
                            :, hf * 8:(hf + 1) * 8, fc * 512:(fc + 1) * 512],
                        in_=wb_i.rearrange("p (ch d) -> p ch d", ch=16)[
                            :, hf * 8:(hf + 1) * 8, fc * 512:(fc + 1) * 512],
                    ).inc("sw45", 16)
                elif i == 26:
                    c.G.dma_start(out=ac_bf[:], in_=ac_i).inc("sw45", 16)
                elif i == 27:
                    c.G.dma_start(out=bc_bf[:], in_=bc_i).inc("sw45", 16)
                elif i == 28:
                    c.G.dma_start(out=bb_sb[:], in_=bb_i).inc("sw45", 16)

                for kv in range(NKV):
                    if kv == 0:
                        p3_attn_kv(c, i, kv)
                    else:
                        p3_attn_kv(c, i, kv,
                                   pre_cb=(lambda kk=kv - 1: p3_epi_a(c, i, kk),
                                           lambda kk=kv - 1: p3_epi_b(c, i, kk)))
                p3_epi_a(c, i, NKV - 1)
                p3_epi_b(c, i, NKV - 1)
                c.m[("p3_pe_done", i)] = c.c["spe"]
                c.G.wait("sve", c.c["sve"])
                c.G.dma_start(out=yT_s.rearrange("c p t -> p c t")[:, :, i * 128:(i + 1) * 128],
                              in_=yTb[i % 2][:].rearrange("p (c t) -> p c t", c=16)).inc(spar, 16)
                c.m[("p3_st", i)] = c.c[spar]
            _barrier(c)

        _emit_all(blk, ctxs, p3)
        es3.close()

        es4 = contextlib.ExitStack()
        wp_bf = es4.enter_context(nc.sbuf_tensor("wp_bf", [128, 16 * D], BF16))
        xnTb4 = [es4.enter_context(nc.sbuf_tensor(f"xnTb4_{j}", [128, D], BF16)) for j in range(2)]
        yTb4 = [es4.enter_context(nc.sbuf_tensor(f"yTb4_{j}", [128, D], BF16)) for j in range(2)]
        xb4 = [es4.enter_context(nc.sbuf_tensor(f"xb4_{j}", [128, D], F32)) for j in range(2)]
        gbuf = es4.enter_context(nc.sbuf_tensor("gbuf", [128, D], F32))
        midf = es4.enter_context(nc.sbuf_tensor("midf", [128, NCAT], F32))
        hwbf = es4.enter_context(nc.sbuf_tensor("hwbf", [128, NCAT], BF16))
        hwT = es4.enter_context(nc.sbuf_tensor("hwT", [128, 128], BF16))
        ps_f = [es4.enter_context(nc.psum_tensor(f"ps_f{j}", [128, 512], F32)) for j in range(4)]
        ps_m = es4.enter_context(nc.psum_tensor("ps_m", [128, 512], F32))
        ps_h = es4.enter_context(nc.psum_tensor("ps_h", [128, 512], F32))
        ps_w = [es4.enter_context(nc.psum_tensor(f"ps_w{j}", [128, 512], F32)) for j in range(2)]

        W45 = 11 * 16

        def p45_setup(c):
            for fc in range(4):
                c.G.dma_start(
                    out=wp_bf[:].rearrange("p (ch d) -> p ch d", ch=16)[:, :, fc * 512:(fc + 1) * 512],
                    in_=wp_i.rearrange("p (ch d) -> p ch d", ch=16)[:, :, fc * 512:(fc + 1) * 512],
                ).inc("sw45", 16)

        def p45_tile(c, i):
            par = "sxa" if i % 2 == 0 else "sxb"
            spar = "soa" if i % 2 == 0 else "sob"
            if i >= 2:
                c.Y.wait("spe", c.m[("p45_pe_done", i - 2)])
                c.Y.wait("sve", c.m[("p45_v_done", i - 2)])
                c.Y.wait(spar, c.m[("p45_st", i - 2)])
            c.Y.dma_start(out=xnTb4[i % 2][:].rearrange("p (c t) -> p c t", c=16),
                          in_=xnT_s.rearrange("c p t -> p c t")[:, :, i * 128:(i + 1) * 128]
                          ).inc(par, 16)
            c.Y.dma_start(out=yTb4[i % 2][:].rearrange("p (c t) -> p c t", c=16),
                          in_=yT_s.rearrange("c p t -> p c t")[:, :, i * 128:(i + 1) * 128]
                          ).inc(par, 16)
            c.Y.dma_start(out=xb4[i % 2][:], in_=x_d[i * 128:(i + 1) * 128, :]).inc(par, 16)
            ld = c.c[par]

            c.T.wait(par, ld)
            if i == 0:
                c.T.wait("sw45", 9 * 16)
            if i >= 1:
                c.T.wait("sve", c.m[("p45_sgev", i - 1, 2)])
                for fc in range(4):
                    c.T.wait("sve", c.m[("p45_gev", i - 1, fc)])
            ins = None
            for ch in range(16):
                ins = c.T.matmul(ps_m[:, 0:NCAT], xnTb4[i % 2][:, ch * 128:(ch + 1) * 128],
                                 ac_bf[:, ch * NCAT:(ch + 1) * NCAT],
                                 start=(ch == 0), stop=(ch == 15), skip_group_check=True)
                for fc in range(4):
                    c.T.matmul(ps_f[fc][:], xnTb4[i % 2][:, ch * 128:(ch + 1) * 128],
                               wb_bf[:, ch * D + fc * 512: ch * D + (fc + 1) * 512],
                               start=(ch == 0), stop=False, skip_group_check=True)
            ins.inc("spe")
            mid_mark = c.c["spe"]

            c.V.wait("spe", mid_mark)
            c.V.tensor_copy(out=midf[:], in_=ps_m[:, 0:NCAT]).inc("sve")
            c.m[("p45_midev", i)] = c.c["sve"]
            c.V.wait("sve", c.c["sve"])
            for e in range(NE):
                c.V.tensor_scalar(out=midf[:, 16 + e * NR:16 + (e + 1) * NR],
                                  in0=midf[:, 16 + e * NR:16 + (e + 1) * NR],
                                  scalar1=w6[:, i * NE + e:i * NE + e + 1],
                                  scalar2=None, op0=ALU.mult).inc("sve")
            c.V.wait("sve", c.c["sve"])
            c.V.tensor_copy(out=hwbf[:], in_=midf[:]).inc("sve")
            hw_mark = c.c["sve"]

            if i == 0:
                c.T.wait("sw45", W45 + 32)
            if i >= 1:
                c.T.wait("sve", c.m[("p45_sgev", i - 1, 0)])
                c.T.wait("sve", c.m[("p45_sgev", i - 1, 1)])
            ins = None
            for ch in range(16):
                for fc in (0, 1):
                    ins = c.T.matmul(ps_w[fc][:], yTb4[i % 2][:, ch * 128:(ch + 1) * 128],
                                     wp_bf[:, ch * D + fc * 512: ch * D + (fc + 1) * 512],
                                     start=(ch == 0), stop=(ch == 15), skip_group_check=True)
            ins.inc("spe")
            wp0_mark = c.c["spe"]

            c.T.wait("sve", hw_mark)
            if i >= 1:
                c.T.wait("sve", c.m[("p45_sgev", i - 1, 3)])
            c.T.matmul(ps_h[0:NCAT, 0:128], hwbf[:, 0:NCAT], ident[:],
                       start=True, stop=True, skip_group_check=True).inc("spe")
            c.V.wait("spe", c.c["spe"])
            c.V.tensor_copy(out=hwT[0:NCAT, :], in_=ps_h[0:NCAT, 0:128]).inc("sve")
            c.m[("p45_hwev", i)] = c.c["sve"]

            c.T.wait("sve", c.m[("p45_hwev", i)])
            if i == 0:
                c.T.wait("sw45", 10 * 16)
            ins = None
            for fc in range(4):
                ins = c.T.matmul(ps_f[fc][:], hwT[0:NCAT, :],
                                 bc_bf[0:NCAT, fc * 512:(fc + 1) * 512],
                                 start=False, stop=True, skip_group_check=True)
            ins.inc("spe")
            ffn_mark = c.c["spe"]

            if i == 0:
                c.T.wait("sw45", W45 + 64)
            c.T.wait("sve", c.m[("p45_hwev", i)])
            ins = None
            for ch in range(16):
                for fc in (2, 3):
                    bank = ps_m if fc == 2 else ps_h
                    ins = c.T.matmul(bank[:], yTb4[i % 2][:, ch * 128:(ch + 1) * 128],
                                     wp_bf[:, ch * D + fc * 512: ch * D + (fc + 1) * 512],
                                     start=(ch == 0), stop=(ch == 15), skip_group_check=True)
            ins.inc("spe")
            wp1_mark = c.c["spe"]

            c.V.wait("spe", ffn_mark)
            if i == 0:
                c.V.wait("sw45", W45)
            silu_marks = {}

            def bias_silu(fc):
                sl = slice(fc * 512, (fc + 1) * 512)
                c.V.tensor_tensor(out=gbuf[:, sl], in0=ps_f[fc][:],
                                  in1=bb_sb[:, sl], op=ALU.add).inc("sve")
                c.m[("p45_gev", i, fc)] = c.c["sve"]
                c.S.wait("sve", c.c["sve"])
                c.S.activation(out=gbuf[:, sl], in_=gbuf[:, sl],
                               func=AF.Silu).inc("sse")
                silu_marks[fc] = c.c["sse"]

            for fc in range(4):
                bias_silu(fc)
            for fc in range(4):
                c.V.wait("spe", wp0_mark if fc < 2 else wp1_mark)
                c.V.wait("sse", silu_marks[fc])
                src = (ps_w[0], ps_w[1], ps_m, ps_h)[fc]
                sl = slice(fc * 512, (fc + 1) * 512)
                c.V.scalar_tensor_tensor(out=gbuf[:, sl], in0=src[:, 0:512],
                                         scalar=1.0, in1=gbuf[:, sl],
                                         op0=ALU.mult, op1=ALU.mult)
                c.V.tensor_tensor(out=xb4[i % 2][:, sl], in0=gbuf[:, sl],
                                  in1=xb4[i % 2][:, sl], op=ALU.add).inc("sve")
                c.m[("p45_sgev", i, fc)] = c.c["sve"]
            c.m[("p45_v_done", i)] = c.c["sve"]
            c.m[("p45_pe_done", i)] = c.c["spe"]

            c.G.wait("sve", c.c["sve"])
            c.G.dma_start(out=out_d[i * 128:(i + 1) * 128, :], in_=xb4[i % 2][:]).inc(spar, 16)
            c.m[("p45_st", i)] = c.c[spar]

        def p45(c):
            p45_setup(c)
            for i in range(nt):
                p45_tile(c, i)
            c.G.wait("soa", c.c["soa"])
            c.G.wait("sob", c.c["sob"])

        _emit_all(blk, ctxs, p45)
        es4.close()
        es34.close()

    return nc


_NC_CACHE = {}


def _get_nc():
    if "nc" not in _NC_CACHE:
        _NC_CACHE["nc"] = build_nc()
    return _NC_CACHE["nc"]


def _img(mat):
    n = mat.shape[1]
    return np.ascontiguousarray(
        mat.reshape(16, 128, n).transpose(1, 0, 2).reshape(128, 16 * n).astype(np.float32))


def prepare_shared(L0_attn_wq, L0_attn_wk, L0_attn_wv, L0_attn_wp, L0_attn_qg,
                   L0_ffn_W_base, L0_ffn_b_base, L0_ffn_A_s, L0_ffn_B_s, L0_ffn_A_r,
                   L0_ffn_B_r, L0_ffn_W_router, L0_ffn_b_router, **_ignore):
    wq_img = _img(np.asarray(L0_attn_wq, np.float32).T)
    wk_img = _img(np.asarray(L0_attn_wk, np.float32).T)
    wv_img = _img(np.asarray(L0_attn_wv, np.float32).T)
    wp_img = _img(np.asarray(L0_attn_wp, np.float32).T)
    wb_img = _img(np.asarray(L0_ffn_W_base, np.float32))
    wr_img = _img(np.asarray(L0_ffn_W_router, np.float32))
    A_r = np.asarray(L0_ffn_A_r, np.float32)
    acat = np.concatenate([np.asarray(L0_ffn_A_s, np.float32),
                           A_r.transpose(1, 0, 2).reshape(D, NE * NR)], axis=1)
    acat_img = _img(acat)
    bcat_img = np.ascontiguousarray(np.concatenate(
        [np.asarray(L0_ffn_B_s, np.float32),
         np.asarray(L0_ffn_B_r, np.float32).reshape(NE * NR, D)], axis=0))
    bb_rep = np.ascontiguousarray(np.tile(np.asarray(L0_ffn_b_base, np.float32)[None, :], (128, 1)))
    br_rep = np.ascontiguousarray(np.tile(np.asarray(L0_ffn_b_router, np.float32)[None, :], (128, 1)))
    qg_rep = np.ascontiguousarray(np.tile(np.asarray(L0_attn_qg, np.float32)[None, :], (128, 1)))

    ident_bf = np.eye(128, dtype=_bf)
    trikm1 = np.where(np.arange(128)[:, None] <= np.arange(128)[None, :],
                      np.float32(0.0), np.float32(NEGBIG)).astype(np.float32)
    trikm = np.ascontiguousarray(np.tile(trikm1, (1, 4)).astype(_bf))
    ones_cr = np.ones((128, 129), dtype=_bf)
    eps_c = np.ascontiguousarray(np.tile(np.array([[LN_EPS, RMS_EPS]], np.float32), (128, 1)))
    t_ = np.arange(T, dtype=np.float32)
    inv = (1.0 / (10000.0 ** (np.arange(0, PD, 2, dtype=np.float32) / PD))).astype(np.float32)
    f_ = t_[:, None] * inv[None, :]
    rope_tab = np.ascontiguousarray(np.concatenate(
        [np.tile(np.cos(f_).astype(np.float32), (1, 16)),
         np.tile(np.sin(f_).astype(np.float32), (1, 16))], axis=1))

    return dict(wq_img=wq_img, wk_img=wk_img, wv_img=wv_img, wp_img=wp_img,
                wb_img=wb_img, wr_img=wr_img, acat_img=acat_img, bcat_img=bcat_img,
                bb_rep=bb_rep, br_rep=br_rep, qg_rep=qg_rep, ident_bf=ident_bf,
                trikm=trikm, ones_cr=ones_cr, eps_c=eps_c, rope_tab=rope_tab)


def kernel(**inputs):
    x = np.asarray(inputs["L0_attn_x"], dtype=np.float32)
    shared = prepare_shared(**inputs)
    in_maps = [dict(shared, x=np.ascontiguousarray(x[b])) for b in range(B)]
    nc = _get_nc()
    res = run_bass_kernel_spmd(nc, in_maps, core_ids=list(range(B)))
    return np.stack([np.asarray(res.results[b]["out"], dtype=np.float32) for b in range(B)],
                    axis=0)
